# version 09
# baseline (speedup 1.0000x reference)
import numpy as np
import concourse.bass as bass
import concourse.mybir as mybir
from concourse.bass_utils import run_bass_kernel_spmd
from contextlib import ExitStack

F32 = mybir.dt.float32
BF16 = mybir.dt.bfloat16
I32 = mybir.dt.int32
AF = mybir.ActivationFunctionType
ALU = mybir.AluOpType

B, L, D, E, EH, NST, RDT, KC = 4, 4096, 1024, 2048, 1024, 16, 64, 4
NL = 4
TC = 512
NCH = L // TC
DT8 = D // 128
ET8 = EH // 128
TF = 1024
EPS = 1e-5
HFF = 4096
MF = HFF // 128
RG_PAIRS = [[0, 1], [2, 3], [4, 5], [6, 7]]

QUEUES = ("pe", "act", "dve", "sync", "gp")
NDS = {"sync": 8, "gp": 8, "act": 6}
NCCS = 4
SEMKEYS = (("pe.x", "act.x", "dve.x", "gp.x")
           + tuple(f"sync.d{i}" for i in range(NDS["sync"]))
           + tuple(f"gp.d{i}" for i in range(NDS["gp"]))
           + tuple(f"act.d{i}" for i in range(NDS["act"]))
           + tuple(f"gp.cc{i}" for i in range(NCCS)))
KLASS_INC = {"x": 1, "d": 16, "cc": 1}


def _sm_layout():
    off, col = {}, 0

    def take(name, n):
        nonlocal col
        off[name] = col
        col += n
    take("nw", NL * DT8)
    take("nf", DT8)
    take("alog", NL * 128)
    take("dp", NL * ET8)
    take("cb", NL * ET8)
    take("dtb", NL * ET8)
    take("cw", NL * ET8 * KC)
    take("s1", MF)
    take("s2", MF)
    take("sh1", MF)
    take("sh2", MF)
    take("s3", DT8)
    take("c_eps", 1)
    take("c_mhalf", 1)
    return off, col


SM_OFF, SM_COLS = _sm_layout()


class Buf:
    __slots__ = ("w", "r")

    def __init__(self, w=None):
        self.w = dict(w) if w else {}
        self.r = {}


class Sched:
    def __init__(self):
        self.ops = {q: [] for q in QUEUES}
        self.count = {s: 0 for s in SEMKEYS}
        self.wait_cache = {q: {} for q in QUEUES}
        self.rr = {}

    def op(self, queue, fn, reads=(), writes=(), deps=(), klass="x",
           silent=False):
        if klass == "x" or silent:
            semkey = f"{queue}.x"
        else:
            n_pool = NDS[queue] if klass == "d" else NCCS
            idx = self.rr.get((queue, klass), 0)
            self.rr[(queue, klass)] = idx + 1
            semkey = f"{queue}.{klass}{idx % n_pool}"
        assert silent or semkey in SEMKEYS, semkey
        d = {}
        if klass in ("d", "cc") and not silent and self.count[semkey] > 0:
            d[semkey] = self.count[semkey]

        def add(tok):
            for s2, c2 in tok.items():
                if (queue == "pe" and s2 == "pe.x"):
                    continue
                if c2 > d.get(s2, 0):
                    d[s2] = c2
        for b in reads:
            add(b.w)
        for b in writes:
            add(b.w)
            add(b.r)
        for tok in deps:
            add(tok)
        wc = self.wait_cache[queue]
        waits = {}
        for s2, c2 in d.items():
            if c2 > wc.get(s2, 0):
                waits[s2] = c2
                wc[s2] = c2
        if silent:
            self.ops[queue].append((waits, fn, None, 0))
            return {}
        inc = KLASS_INC[klass]
        self.count[semkey] += inc
        tokv = self.count[semkey]
        self.ops[queue].append((waits, fn, semkey, inc))
        tok = {semkey: tokv}
        for b in reads:
            b.r[semkey] = max(b.r.get(semkey, 0), tokv)
        for b in writes:
            b.w[semkey] = tokv
        return tok


class SBuf:

    def __init__(self, tensor, lo, n, dtype, init_tok=None):
        self.t = tensor
        self.lo = lo
        self.n = n
        self.dt = dtype
        self.buf = Buf(init_tok)

    def ap(self, a=0, b=None):
        b = self.n if b is None else b
        if self.dt == BF16:
            return self.t[:, self.lo + a:self.lo + b]
        return self.t[:, self.lo + 2 * a:self.lo + 2 * b].bitcast(F32)

    def rows(self, r0, r1, a=0, b=None):
        b = self.n if b is None else b
        if self.dt == BF16:
            return self.t[r0:r1, self.lo + a:self.lo + b]
        return self.t[r0:r1, self.lo + 2 * a:self.lo + 2 * b].bitcast(F32)

    def v3(self, inner):
        return self.ap().rearrange("p (o i) -> p o i", i=inner)


class Arena:
    def __init__(self, tensor, n_units):
        self.t = tensor
        self.n = n_units
        self.pos = 0
        self.marks = []

    def alloc(self, n, dtype=BF16, init_tok=None):
        units = n * (2 if dtype == F32 else 1)
        self.pos = (self.pos + 1) & ~1
        assert self.pos + units <= self.n, ("arena", self.pos, units, self.n)
        lo = self.pos
        self.pos += units
        return SBuf(self.t, lo, n, dtype, init_tok)

    def alias(self, base, n, dtype, off_units=0):
        s = SBuf(self.t, base.lo + off_units, n, dtype)
        s.buf = base.buf
        return s

    def mark(self):
        self.marks.append(self.pos)

    def reset_to_mark(self):
        self.pos = self.marks[-1]


def build_program(n_layers=NL, n_chunks=NCH, do_ffn=True, tap_list=(), defer=True, opev_sep=False, revert=()):
    nc = bass.Bass("TRN2", target_bir_lowering=False)
    S = Sched()

    def din(name, shape, dt=F32):
        return nc.dram_tensor(name, shape, dt, kind="ExternalInput")

    x0_t = din("x0", [D, L])
    wip_t = din("wip", [NL, 16, 128, DT8 * 128])
    wxp_t = din("wxp", [NL, EH, 96])
    wdt_t = din("wdt", [NL, RDT, EH])
    wop_t = din("wop", [NL, DT8, 128, ET8 * 128])
    w1_t = din("w1t", [MF, 128, DT8 * 128])
    w2_t = din("w2t", [MF, 128, DT8 * 128])
    h1_t = din("h1t", [MF, 128, MF * 128])
    h2_t = din("h2t", [MF, 128, MF * 128])
    w3_t = din("w3t", [DT8, 128, MF * 128])
    smalls_t = din("smalls", [128, SM_COLS])
    sel_in_t = din("sel32", [32, 32 * 128])
    tok_in_t = din("tok0", [1, 1], I32)
    out_t = nc.dram_tensor("out_c", [D, L // 2], F32, kind="ExternalOutput")

    x0_in = x0_t.ap()
    wip_in, wxp_in = wip_t.ap(), wxp_t.ap()
    wdt_in, wop_in = wdt_t.ap(), wop_t.ap()
    w1_in, w2_in = w1_t.ap(), w2_t.ap()
    h1_in, h2_in, w3_in = h1_t.ap(), h2_t.ap(), w3_t.ap()
    out_ext = out_t.ap()

    wipb_t = nc.dram_tensor("wipb_d", [NL, 16, 128, DT8 * 128], BF16,
                            kind="Internal")
    wopb_t = nc.dram_tensor("wopb_d", [NL, DT8, 128, ET8 * 128], BF16,
                            kind="Internal")
    w1b_t = nc.dram_tensor("w1b_d", [MF, 128, DT8 * 128], BF16, kind="Internal")
    w2b_t = nc.dram_tensor("w2b_d", [MF, 128, DT8 * 128], BF16, kind="Internal")
    h1b_t = nc.dram_tensor("h1b_d", [MF, 128, MF * 128], BF16, kind="Internal")
    h2b_t = nc.dram_tensor("h2b_d", [MF, 128, MF * 128], BF16, kind="Internal")
    w3b_t = nc.dram_tensor("w3b_d", [DT8, 128, MF * 128], BF16, kind="Internal")
    wipb_in, wopb_in = wipb_t.ap(), wopb_t.ap()
    w1b_in, w2b_in = w1b_t.ap(), w2b_t.ap()
    h1b_in, h2b_in, w3b_in = h1b_t.ap(), h2b_t.ap(), w3b_t.ap()
    wb_bufs = {"wip": [Buf() for _ in range(NL)],
               "wop": [Buf() for _ in range(NL)],
               "w1": Buf(), "w2": Buf(), "h1": Buf(), "h2": Buf(),
               "w3": Buf()}
    resid_t = nc.dram_tensor("resid_d", [D, L], F32, kind="Internal")
    x1_dt = nc.dram_tensor("x1_d", [D, L], F32, kind="Internal")
    xn2_dt = nc.dram_tensor("xn2_d", [D, L], BF16, kind="Internal")
    xsrc_t = nc.dram_tensor("xsrc_d", [NCH * 96, TC], F32, kind="Internal")
    xdst_t = nc.dram_tensor("xdst_d", [NCH * 96, TC], F32, kind="Internal")
    osrc_t = nc.dram_tensor("osrc_d", [NCH * D, TC], F32, kind="Internal")
    odst_t = nc.dram_tensor("odst_d", [NCH * D, TC], F32, kind="Internal")
    resid_d, x1_d, xn2_d = resid_t.ap(), x1_dt.ap(), xn2_dt.ap()
    xsrc_d, xdst_d = xsrc_t.ap(), xdst_t.ap()
    osrc_d, odst_d = osrc_t.ap(), odst_t.ap()

    db = {}
    for t in ("resid", "x1", "xn2", "xsrc", "xdst", "osrc", "odst"):
        db[t] = [Buf() for _ in range(NCH)]
    db["out"] = Buf()

    with ExitStack() as st:
        NA = 106400
        ar_t = st.enter_context(nc.sbuf_tensor("ar", [128, NA], BF16))
        tok_t = st.enter_context(nc.sbuf_tensor("tok_t", [1, 1], I32))
        psum = [st.enter_context(nc.psum_tensor(f"ps{j}", [128, 512], F32))
                for j in range(8)]
        sems = {s: st.enter_context(nc.semaphore("sem_" + s.replace(".", "_")))
                for s in SEMKEYS}
        r_tok = st.enter_context(nc.sync.register("r_tok"))
        r_off = st.enter_context(nc.sync.register("r_off"))
        r_off2 = st.enter_context(nc.sync.register("r_off2"))

        A = Arena(ar_t, NA)
        psb = [Buf() for _ in range(8)]
        tok_b = Buf()
        OP = S.op

        def dma(queue, out_ap, in_ap, reads=(), writes=(), deps=()):
            def fn(eng, o=out_ap, i=in_ap):
                return eng.dma_start(out=o, in_=i)
            return OP(queue, fn, reads=reads, writes=writes, deps=deps,
                      klass="d")

        ones_st = A.alloc(128)
        state = A.alloc(NL * 128)
        tails = A.alloc(NL * ET8 * 3)
        smalls = A.alloc(SM_COLS, F32)
        A_sb = A.alloc(NL * 128, F32)
        wstgA = A.alloc(512, F32)
        wstgB = A.alloc(512, F32)

        def smc(col):
            return smalls.ap(col, col + 1)

        nc.const_aps.register(F32, EPS, smc(SM_OFF["c_eps"]))
        nc.const_aps.register(F32, -0.5, smc(SM_OFF["c_mhalf"]))

        A.mark()

        wipT = [A.alloc(1024), A.alloc(1024)]
        wopT = [A.alloc(1024), A.alloc(1024)]
        w_xp = A.alloc(ET8 * 96)
        w_dt = A.alloc(1024)
        hn_bf = A.alloc(DT8 * TC)
        y_g = A.alloc(ET8 * TC)
        u_bf2 = [A.alloc(ET8 * TC), A.alloc(ET8 * TC)]
        zs_bf2 = [A.alloc(ET8 * TC), A.alloc(ET8 * TC)]
        a_buf = A.alloc(2 * TC)
        w_buf = A.alloc(TC)
        b_strip = A.alloc(NST * TC)
        xi_ext = A.alloc(ET8 * (TC + 3))
        xi_bufs = [Buf() for _ in range(ET8)]
        cv = A.alloc(4 * TC)
        stat_bf = A.alias(cv, 2 * TC, BF16)
        h_strip = A.alloc(NST * TC)
        op_ev = (None if opev_sep
                 else A.alias(h_strip, DT8 * TC, F32))
        Bst = A.alloc(NST * TC)
        h_in = A.alias(Bst, DT8 * TC, F32)
        Cst = A.alloc(NST * TC)
        dtin_bf = A.alloc(TC)
        sel = A.alloc(32 * 128)
        bc = (A.alloc(TC, F32) if "bc" in revert else A.alias(wstgA, TC, F32))
        bcb = A.alloc(TC)
        Rc = A.alloc(DT8 * TC, F32)
        if opev_sep:
            op_ev = A.alias(Rc, DT8 * TC, F32)
        dt_et2 = [A.alloc(TC, F32), A.alloc(TC, F32)]
        sp2 = [A.alloc(2 * TC, F32), A.alloc(2 * TC, F32)]
        y_f2 = A.alloc(TC, F32)
        y_f = A.alloc(TC, F32)
        xp_ev = A.alias(y_f, TC, F32)
        m_rep = A.alloc(TC, F32)
        v_tmp = A.alloc(TC, F32)
        rstd = A.alloc(TC, F32)
        r1 = A.alloc(TC, F32)


        def emit_stats_ms(slices, stat_rg, r1_rg, rstd_rg):
            pb, pbb = psum[4], psb[4]
            nt = len(slices)
            for kk, (ap, rb) in enumerate(slices):
                half = kk % 2
                sl = stat_rg.ap(half * TC, (half + 1) * TC)

                def fsq(e, d=sl, s=ap):
                    return e.activation(d, s, AF.Square)
                OP("act", fsq, reads=[rb], writes=[stat_rg.buf])

                def fmm(e, s=sl, first=(kk == 0), last=(kk == nt - 1)):
                    return e.matmul(pb[:, :], ones_st.ap(), s,
                                    start=first, stop=last)
                OP("pe", fmm, reads=[stat_rg.buf, ones_st.buf], writes=[pbb])

            def fln(e):
                return e.activation(r1_rg.ap(), pb[:, :], AF.Ln, bias=EPS)
            OP("act", fln, reads=[pbb, smalls.buf], writes=[r1_rg.buf])

            def fex(e):
                return e.activation(rstd_rg.ap(), r1_rg.ap(), AF.Exp,
                                    scale=-0.5)
            OP("act", fex, reads=[r1_rg.buf, smalls.buf], writes=[rstd_rg.buf])

        def emit_stats_mean_var(slices, stat_rg, m_rg, v_rg, r1_rg, rstd_rg):
            pbm, pbs = psum[4], psum[5]
            pbmb, pbsb = psb[4], psb[5]
            nt = len(slices)
            for kk, (ap, rb) in enumerate(slices):
                half = kk % 2
                sl = stat_rg.ap(half * TC, (half + 1) * TC)

                def fcp(e, d=sl, s=ap):
                    return e.copy(d, s)
                OP("act", fcp, reads=[rb], writes=[stat_rg.buf])

                def fmm(e, s=sl, first=(kk == 0), last=(kk == nt - 1)):
                    return e.matmul(pbm[:, :], ones_st.ap(), s,
                                    start=first, stop=last)
                OP("pe", fmm, reads=[stat_rg.buf, ones_st.buf], writes=[pbmb])
            for kk, (ap, rb) in enumerate(slices):
                half = kk % 2
                sl = stat_rg.ap(half * TC, (half + 1) * TC)

                def fsq(e, d=sl, s=ap):
                    return e.activation(d, s, AF.Square)
                OP("act", fsq, reads=[rb], writes=[stat_rg.buf])

                def fmm(e, s=sl, first=(kk == 0), last=(kk == nt - 1)):
                    return e.matmul(pbs[:, :], ones_st.ap(), s,
                                    start=first, stop=last)
                OP("pe", fmm, reads=[stat_rg.buf, ones_st.buf], writes=[pbsb])
            OP("act", lambda e: e.copy(m_rg.ap(), pbm[:, :]),
               reads=[pbmb], writes=[m_rg.buf])
            OP("dve", lambda e: e.tensor_mul(v_rg.ap(), m_rg.ap(), m_rg.ap()),
               reads=[m_rg.buf], writes=[v_rg.buf])
            OP("dve", lambda e: e.tensor_sub(v_rg.ap(), pbs[:, :], v_rg.ap()),
               reads=[pbsb, v_rg.buf], writes=[v_rg.buf])
            OP("act", lambda e: e.activation(r1_rg.ap(), v_rg.ap(), AF.Ln,
                                             bias=EPS),
               reads=[v_rg.buf, smalls.buf], writes=[r1_rg.buf])
            OP("act", lambda e: e.activation(rstd_rg.ap(), r1_rg.ap(), AF.Exp,
                                             scale=-0.5),
               reads=[r1_rg.buf, smalls.buf], writes=[rstd_rg.buf])

        def cast_load(queue, stage, dst, dst_lo, src_ap, n_cols, rows=128):
            for j0 in range(0, n_cols, 512):
                w = min(512, n_cols - j0)
                stg_s = wstgA if "stage" in revert else                     (wstgA, wstgB)[(j0 // 512) % 2]
                stg = stg_s.rows(0, rows, 0, w)
                dma(queue, stg, src_ap[:, j0:j0 + w], writes=[stg_s.buf])

                def fcv(e, d=dst.rows(0, rows, dst_lo + j0, dst_lo + j0 + w),
                        s=stg):
                    return e.copy(d, s)
                OP("act", fcv, reads=[stg_s.buf], writes=[dst.buf])

        wl_state = {"i": 0, "stages": None}

        def wtile_load(src_tile_bf, n_k, dst, key):
            i = wl_state["i"]
            wl_state["i"] = i + 1
            q = ("sync", "gp")[i % 2]
            dma(q, dst.ap(0, n_k * 128), src_tile_bf,
                deps=[wb_bufs[key].w], writes=[dst.buf])

        prepS = [A.alloc(512, F32), A.alloc(512, F32)]
        prepB = [A.alloc(512), A.alloc(512)]
        prep_i = [0]

        def emit_prep(src_f32, dst_bf, dst_buf, n_cols, q_in="act",
                      q_out="gp"):
            for j0 in range(0, n_cols, 512):
                w = min(512, n_cols - j0)
                k = prep_i[0]
                prep_i[0] += 1
                ps_, pb_ = prepS[k % 2], prepB[k % 2]
                dma(q_in, ps_.ap(0, w), src_f32[:, j0:j0 + w],
                    writes=[ps_.buf])

                def fc(e, d=pb_.ap(0, w), s=ps_.ap(0, w)):
                    return e.copy(d, s)
                OP("act", fc, reads=[ps_.buf], writes=[pb_.buf])
                dma(q_out, dst_bf[:, j0:j0 + w], pb_.ap(0, w),
                    reads=[pb_.buf], writes=[dst_buf if isinstance(dst_buf, Buf)
                                             else dst_buf])

        def layer_w_items(i):
            items = []
            for m in range(16):
                items.append((wip_in[i, m, :, :], wipb_in[i, m, :, :],
                              ("wip", i), DT8 * 128))
            for m in range(DT8):
                items.append((wop_in[i, m, :, :], wopb_in[i, m, :, :],
                              ("wop", i), DT8 * 128))
            return items

        def wbuf_of(key):
            if isinstance(key, tuple):
                return wb_bufs[key[0]][key[1]]
            return wb_bufs[key]

        def emit_layer_weights(i):
            for kk in range(ET8):
                cast_load("sync", None, w_xp, kk * 96,
                          wxp_in[i, kk * 128:(kk + 1) * 128, :], 96)
            cast_load("sync", None, w_dt, 0, wdt_in[i, :, :], 1024, rows=RDT)
            if i == 0:
                for it in layer_w_items(0):
                    emit_prep(it[0], it[1], wbuf_of(it[2]), it[3])
            if i + 1 < n_layers:
                prep_work[0:0] = layer_w_items(i + 1)

        taps = {}

        def tap_dump(name, region):
            ap_out = nc.dram_tensor("tap_" + name, [128, region.n], region.dt,
                                    kind="ExternalOutput").ap()
            tb = Buf()
            dma("sync", ap_out, region.ap(), reads=[region.buf], writes=[tb])
            taps[name] = tb

        def emit_front(i, c):
            u_bf = u_bf2[c % 2]
            zs_bf = zs_bf2[c % 2]
            t0 = c * TC
            r3 = Rc.v3(TC)
            rsl = resid_d[:, t0:t0 + TC].rearrange("(k p) t -> p k t", p=128)
            if i == 0:
                dma("sync", r3,
                    x0_in[:, t0:t0 + TC].rearrange("(k p) t -> p k t", p=128),
                    writes=[Rc.buf])
                emit_stats_mean_var(
                    [(Rc.ap(kk * TC, (kk + 1) * TC), Rc.buf)
                     for kk in range(DT8)], stat_bf, m_rep, v_tmp, r1, rstd)
                for kk in range(DT8):
                    sl = Rc.ap(kk * TC, (kk + 1) * TC)

                    def f1(e, s=sl):
                        return e.tensor_sub(s, s, m_rep.ap())
                    OP("dve", f1, reads=[Rc.buf, m_rep.buf], writes=[Rc.buf])

                    def f2(e, s=sl):
                        return e.tensor_mul(s, s, rstd.ap())
                    OP("dve", f2, reads=[Rc.buf, rstd.buf], writes=[Rc.buf])
                dma("sync", rsl, r3, reads=[Rc.buf], writes=[db["resid"][c]])
            else:
                dma("sync", r3, rsl, reads=[db["resid"][c]], writes=[Rc.buf])
                hsl = odst_d[c * D:(c + 1) * D, :].rearrange(
                    "(k p) t -> p k t", p=128)
                dma("sync", h_in.v3(TC), hsl, reads=[db["odst"][c]],
                    writes=[h_in.buf])
                OP("dve", lambda e: e.tensor_add(Rc.ap(), Rc.ap(), h_in.ap()),
                   reads=[Rc.buf, h_in.buf], writes=[Rc.buf])
                dma("sync", rsl, r3, reads=[Rc.buf], writes=[db["resid"][c]])
            emit_stats_ms(
                [(Rc.ap(kk * TC, (kk + 1) * TC), Rc.buf) for kk in range(DT8)],
                stat_bf, r1, rstd)
            for kk in range(DT8):
                def fhn(e, d=hn_bf.ap(kk * TC, (kk + 1) * TC),
                        s=Rc.ap(kk * TC, (kk + 1) * TC),
                        sc=smc(SM_OFF["nw"] + i * DT8 + kk)):
                    return e.scalar_tensor_tensor(d, s, sc, rstd.ap(),
                                                  ALU.mult, ALU.mult)
                OP("dve", fhn, reads=[Rc.buf, smalls.buf, rstd.buf],
                   writes=[hn_bf.buf])

            for et in range(ET8):
                base = et * (TC + 3)
                tl = (i * ET8 + et) * 3

                def ftl(e, d=xi_ext.ap(base, base + 3),
                        s=tails.ap(tl, tl + 3)):
                    return e.copy(d, s)
                OP("act", ftl, reads=[tails.buf], writes=[xi_bufs[et]])

            for m in range(16):
                pb, pbb = psum[m % 4], psb[m % 4]
                wt_ = wipT[m % 2]
                dma("sync", wt_.ap(), wipb_in[i, m, :, :],
                    deps=[wb_bufs["wip"][i].w], writes=[wt_.buf])

                def fmm(e, pb=pb, wt_=wt_):
                    last = None
                    for kk in range(DT8):
                        last = e.matmul(
                            pb[:, :],
                            wt_.ap(kk * 128, (kk + 1) * 128),
                            hn_bf.ap(kk * TC, (kk + 1) * TC),
                            start=(kk == 0), stop=(kk == DT8 - 1))
                    return last
                OP("pe", fmm, reads=[wt_.buf, hn_bf.buf], writes=[pbb])
                if m < ET8:
                    def fev(e, d=xi_ext.ap(m * (TC + 3) + 3,
                                           (m + 1) * (TC + 3)), pb=pb):
                        return e.copy(d, pb[:, :])
                    OP("act", fev, reads=[pbb], writes=[xi_bufs[m]])
                else:
                    def fev(e, d=zs_bf.ap((m - ET8) * TC, (m - ET8 + 1) * TC),
                            pb=pb):
                        return e.activation(d, pb[:, :], AF.Silu)
                    OP("act", fev, reads=[pbb], writes=[zs_bf.buf])

            for et in range(ET8):
                base = et * (TC + 3)
                tl = (i * ET8 + et) * 3
                for k in range(KC):
                    def ftap(e, d=cv.ap(k * TC, (k + 1) * TC),
                             s=xi_ext.ap(base + k, base + k + TC),
                             sc=smc(SM_OFF["cw"] + i * 32 + et * 4 + k)):
                        return e.tensor_scalar(d, s, sc, None, ALU.mult)
                    OP("dve", ftap, reads=[xi_bufs[et], smalls.buf],
                       writes=[cv.buf])

                def fa1(e, d=cv.ap(0, TC), a=cv.ap(0, TC),
                        b2=cv.ap(TC, 2 * TC)):
                    return e.tensor_add(d, a, b2)
                OP("dve", fa1, reads=[cv.buf], writes=[cv.buf])

                def fa2(e, d=cv.ap(2 * TC, 3 * TC), a=cv.ap(2 * TC, 3 * TC),
                        b2=cv.ap(3 * TC, 4 * TC)):
                    return e.tensor_add(d, a, b2)
                OP("dve", fa2, reads=[cv.buf], writes=[cv.buf])

                def fa3(e, d=cv.ap(0, TC), a=cv.ap(0, TC),
                        b2=cv.ap(2 * TC, 3 * TC)):
                    return e.tensor_add(d, a, b2)
                OP("dve", fa3, reads=[cv.buf], writes=[cv.buf])

                def fu(e, d=u_bf.ap(et * TC, (et + 1) * TC), s=cv.ap(0, TC),
                       bias=smc(SM_OFF["cb"] + i * ET8 + et)):
                    return e.activation(d, s, AF.Silu, bias=bias)
                OP("act", fu, reads=[cv.buf, smalls.buf], writes=[u_bf.buf])

                def fsv(e, d=tails.ap(tl, tl + 3),
                        s=xi_ext.ap(base + TC, base + TC + 3)):
                    return e.copy(d, s)
                OP("act", fsv, reads=[xi_bufs[et]], writes=[tails.buf])

            pb4, pbb4 = psum[4], psb[4]

            def fxp(e):
                last = None
                for kk in range(ET8):
                    last = e.matmul(pb4[0:96, :],
                                    w_xp.ap(kk * 96, (kk + 1) * 96),
                                    u_bf.ap(kk * TC, (kk + 1) * TC),
                                    start=(kk == 0), stop=(kk == ET8 - 1))
                return last
            OP("pe", fxp, reads=[w_xp.buf, u_bf.buf], writes=[pbb4])
            OP("act", lambda e: e.copy(xp_ev.rows(0, 96), pb4[0:96, :]),
               reads=[pbb4], writes=[xp_ev.buf])
            dma("gp", xsrc_d[c * 96:(c + 1) * 96, :], xp_ev.rows(0, 96),
                reads=[xp_ev.buf], writes=[db["xsrc"][c]])

            def fcc(e, c=c):
                return e.collective_compute(
                    "AllReduce", ALU.add, replica_groups=RG_PAIRS,
                    ins=[xsrc_d[c * 96:(c + 1) * 96, :].opt()],
                    outs=[xdst_d[c * 96:(c + 1) * 96, :].opt()])
            OP("gp", fcc, reads=[db["xsrc"][c]], writes=[db["xdst"][c]],
               klass="cc")

        def emit_dt_chain(i, et):
            dt_et = dt_et2[et % 2]
            sp = sp2[et % 2]
            pb5, pbb5 = psum[5], psb[5]

            def fdt(e, et=et):
                return e.matmul(pb5[:, :],
                                w_dt.rows(0, RDT, et * 128, (et + 1) * 128),
                                dtin_bf.rows(0, RDT),
                                start=True, stop=True)
            OP("pe", fdt, reads=[w_dt.buf, dtin_bf.buf], writes=[pbb5])
            dtb_sc = smc(SM_OFF["dtb"] + i * ET8 + et)

            def f_abs(e, sc=dtb_sc, sp=sp):
                return e.activation(sp.ap(0, TC), pb5[:, :], AF.Abs, bias=sc)
            OP("act", f_abs, reads=[pbb5, smalls.buf], writes=[sp.buf])

            def f_exp(e, sp=sp):
                return e.activation(sp.ap(TC, 2 * TC), sp.ap(0, TC),
                                    AF.Exp, scale=-1.0)
            OP("act", f_exp, reads=[sp.buf], writes=[sp.buf])

            def f_ln(e, sp=sp):
                return e.activation(sp.ap(0, TC), sp.ap(TC, 2 * TC),
                                    AF.Ln, bias=1.0)
            OP("act", f_ln, reads=[sp.buf], writes=[sp.buf])

            def f_rel(e, sc=dtb_sc, sp=sp):
                return e.activation(sp.ap(TC, 2 * TC), pb5[:, :],
                                    AF.Relu, bias=sc)
            OP("act", f_rel, reads=[pbb5, smalls.buf], writes=[sp.buf])

            def f_dt(e, sp=sp, dt_et=dt_et):
                return e.tensor_add(dt_et.ap(), sp.ap(0, TC),
                                    sp.ap(TC, 2 * TC))
            OP("dve", f_dt, reads=[sp.buf], writes=[dt_et.buf])

        def emit_scan(i, c):
            u_bf = u_bf2[c % 2]
            zs_bf = zs_bf2[c % 2]
            dma("gp", dtin_bf.rows(0, RDT), xdst_d[c * 96:c * 96 + RDT, :],
                reads=[db["xdst"][c]], writes=[dtin_bf.buf])
            dma("gp", bc.rows(0, 32), xdst_d[c * 96 + RDT:(c + 1) * 96, :],
                reads=[db["xdst"][c]], writes=[bc.buf])
            OP("act", lambda e: e.copy(bcb.rows(0, 32), bc.rows(0, 32)),
               reads=[bc.buf], writes=[bcb.buf])

            for n in range(NST):
                jB = 4 + (n % 2) * 2
                jC = 5 + (n % 2) * 2

                def fb(e, n=n, pb=psum[jB]):
                    return e.matmul(pb[:, :],
                                    sel.rows(0, 32, n * 128, (n + 1) * 128),
                                    bcb.rows(0, 32), start=True, stop=True)
                OP("pe", fb, reads=[sel.buf, bcb.buf], writes=[psb[jB]])

                def fbe(e, d=Bst.ap(n * TC, (n + 1) * TC), pb=psum[jB]):
                    return e.copy(d, pb[:, :])
                OP("act", fbe, reads=[psb[jB]], writes=[Bst.buf])

                def fc2(e, n=n, pb=psum[jC]):
                    return e.matmul(
                        pb[:, :],
                        sel.rows(0, 32, (NST + n) * 128, (NST + n + 1) * 128),
                        bcb.rows(0, 32), start=True, stop=True)
                OP("pe", fc2, reads=[sel.buf, bcb.buf], writes=[psb[jC]])

                def fce(e, d=Cst.ap(n * TC, (n + 1) * TC), pb=psum[jC]):
                    return e.copy(d, pb[:, :])
                OP("act", fce, reads=[psb[jC]], writes=[Cst.buf])

            for et in range(ET8):
                dt_et = dt_et2[et % 2]
                if et == 0:
                    emit_dt_chain(i, 0)
                if et + 1 < ET8:
                    emit_dt_chain(i, et + 1)

                def fw(e, et=et, dt_et=dt_et):
                    return e.tensor_mul(w_buf.ap(), dt_et.ap(),
                                        u_bf.ap(et * TC, (et + 1) * TC))
                OP("dve", fw, reads=[dt_et.buf, u_bf.buf], writes=[w_buf.buf])

                def fb2(e):
                    return e.tensor_tensor(
                        b_strip.v3(TC),
                        w_buf.ap().unsqueeze(1).broadcast_to([128, NST, TC]),
                        Bst.v3(TC), ALU.mult)
                OP("dve", fb2, reads=[w_buf.buf, Bst.buf],
                   writes=[b_strip.buf])
                for n in range(NST):
                    acol = A_sb.ap(i * 128 + et * NST + n,
                                   i * 128 + et * NST + n + 1)
                    ab = a_buf.ap((n % 2) * TC, (n % 2 + 1) * TC)

                    def fe(e, ab=ab, acol=acol, dt_et=dt_et):
                        return e.activation(ab, dt_et.ap(), AF.Exp, scale=acol)
                    OP("act", fe, reads=[dt_et.buf, A_sb.buf],
                       writes=[a_buf.buf])
                    stcol = state.ap(i * 128 + et * NST + n,
                                     i * 128 + et * NST + n + 1)

                    def fs(e, n=n, ab=ab, stcol=stcol):
                        return e.tensor_tensor_scan(
                            h_strip.ap(n * TC, (n + 1) * TC), ab,
                            b_strip.ap(n * TC, (n + 1) * TC), stcol,
                            ALU.mult, ALU.add)
                    OP("dve", fs, reads=[a_buf.buf, b_strip.buf, state.buf],
                       writes=[h_strip.buf])

                def fst(e, et=et):
                    v = h_strip.v3(TC)[:, :, TC - 1:TC]
                    return e.tensor_copy(
                        state.ap(i * 128 + et * NST, i * 128 + (et + 1) * NST),
                        v)
                OP("dve", fst, reads=[h_strip.buf], writes=[state.buf])

                def fq(e):
                    return e.tensor_mul(b_strip.ap(), h_strip.ap(), Cst.ap())
                OP("dve", fq, reads=[h_strip.buf, Cst.buf],
                   writes=[b_strip.buf])
                qv = b_strip.ap().rearrange("p (n two t) -> p n two t",
                                            two=2, t=TC)

                def ftr1(e, qv=qv):
                    return e.tensor_tensor(h_strip.v3(TC)[:, 0:8, :],
                                           qv[:, :, 0, :], qv[:, :, 1, :],
                                           ALU.add)
                OP("dve", ftr1, reads=[b_strip.buf], writes=[h_strip.buf])
                hv = h_strip.ap().rearrange("p (g two t) -> p g two t",
                                            two=2, t=TC)

                def ftr2(e, hv=hv):
                    return e.tensor_tensor(h_strip.v3(TC)[:, 8:12, :],
                                           hv[:, 0:4, 0, :], hv[:, 0:4, 1, :],
                                           ALU.add)
                OP("dve", ftr2, reads=[h_strip.buf], writes=[h_strip.buf])

                def ftr3(e, hv=hv):
                    return e.tensor_tensor(h_strip.v3(TC)[:, 12:14, :],
                                           hv[:, 4:6, 0, :], hv[:, 4:6, 1, :],
                                           ALU.add)
                OP("dve", ftr3, reads=[h_strip.buf], writes=[h_strip.buf])

                def ftr4(e):
                    return e.tensor_tensor(y_f.ap(),
                                           h_strip.ap(12 * TC, 13 * TC),
                                           h_strip.ap(13 * TC, 14 * TC),
                                           ALU.add)
                OP("dve", ftr4, reads=[h_strip.buf], writes=[y_f.buf])

                def fy2(e, et=et, sc=smc(SM_OFF["dp"] + i * ET8 + et)):
                    return e.scalar_tensor_tensor(
                        y_f2.ap(), u_bf.ap(et * TC, (et + 1) * TC), sc,
                        y_f.ap(), ALU.mult, ALU.add)
                OP("dve", fy2, reads=[u_bf.buf, smalls.buf, y_f.buf],
                   writes=[y_f2.buf])

                def fyg(e, et=et):
                    return e.tensor_mul(y_g.ap(et * TC, (et + 1) * TC),
                                        y_f2.ap(),
                                        zs_bf.ap(et * TC, (et + 1) * TC))
                OP("dve", fyg, reads=[y_f2.buf, zs_bf.buf], writes=[y_g.buf])

        def emit_outproj(i, c):
            for m in range(DT8):
                pb, pbb = psum[m % 4], psb[m % 4]
                wt_ = wopT[m % 2]
                dma("sync", wt_.ap(), wopb_in[i, m, :, :],
                    deps=[wb_bufs["wop"][i].w], writes=[wt_.buf])

                def fop(e, pb=pb, wt_=wt_):
                    last = None
                    for kk in range(ET8):
                        last = e.matmul(
                            pb[:, :],
                            wt_.ap(kk * 128, (kk + 1) * 128),
                            y_g.ap(kk * TC, (kk + 1) * TC),
                            start=(kk == 0), stop=(kk == ET8 - 1))
                    return last
                OP("pe", fop, reads=[wt_.buf, y_g.buf], writes=[pbb])

                def foev(e, d=op_ev.ap(m * TC, (m + 1) * TC), pb=pb):
                    return e.copy(d, pb[:, :])
                OP("act", foev, reads=[pbb], writes=[op_ev.buf])
            dma("gp", osrc_d[c * D:(c + 1) * D, :].rearrange(
                "(k p) t -> p k t", p=128), op_ev.v3(TC),
                reads=[op_ev.buf], writes=[db["osrc"][c]])

            def focc(e, c=c):
                return e.collective_compute(
                    "AllReduce", ALU.add, replica_groups=RG_PAIRS,
                    ins=[osrc_d[c * D:(c + 1) * D, :].opt()],
                    outs=[odst_d[c * D:(c + 1) * D, :].opt()])
            OP("gp", focc, reads=[db["osrc"][c]], writes=[db["odst"][c]],
               klass="cc")

        def emit_tail(c):
            t0 = c * TC
            rsl = resid_d[:, t0:t0 + TC].rearrange("(k p) t -> p k t", p=128)
            dma("sync", Rc.v3(TC), rsl, reads=[db["resid"][c]],
                writes=[Rc.buf])
            hsl = odst_d[c * D:(c + 1) * D, :].rearrange("(k p) t -> p k t",
                                                         p=128)
            dma("sync", h_in.v3(TC), hsl, reads=[db["odst"][c]],
                writes=[h_in.buf])
            OP("dve", lambda e: e.tensor_add(Rc.ap(), Rc.ap(), h_in.ap()),
               reads=[Rc.buf, h_in.buf], writes=[Rc.buf])
            emit_stats_ms(
                [(Rc.ap(kk * TC, (kk + 1) * TC), Rc.buf) for kk in range(DT8)],
                stat_bf, r1, rstd)
            for kk in range(DT8):
                def fat(e, d=h_in.ap(kk * TC, (kk + 1) * TC),
                        s=Rc.ap(kk * TC, (kk + 1) * TC),
                        sc=smc(SM_OFF["nf"] + kk)):
                    return e.scalar_tensor_tensor(d, s, sc, rstd.ap(),
                                                  ALU.mult, ALU.mult)
                OP("dve", fat, reads=[Rc.buf, smalls.buf, rstd.buf],
                   writes=[h_in.buf])
            emit_stats_mean_var(
                [(h_in.ap(kk * TC, (kk + 1) * TC), h_in.buf)
                 for kk in range(DT8)], stat_bf, m_rep, v_tmp, r1, rstd)
            dma("sync", Rc.v3(TC),
                x0_in[:, t0:t0 + TC].rearrange("(k p) t -> p k t", p=128),
                writes=[Rc.buf])
            for kk in range(DT8):
                sl = h_in.ap(kk * TC, (kk + 1) * TC)

                def f1(e, s=sl):
                    return e.tensor_sub(s, s, m_rep.ap())
                OP("dve", f1, reads=[h_in.buf, m_rep.buf], writes=[h_in.buf])

                def f2(e, s=sl):
                    return e.tensor_mul(s, s, rstd.ap())
                OP("dve", f2, reads=[h_in.buf, rstd.buf], writes=[h_in.buf])

                def f3(e, s=sl, x=Rc.ap(kk * TC, (kk + 1) * TC)):
                    return e.tensor_add(s, s, x)
                OP("dve", f3, reads=[h_in.buf, Rc.buf], writes=[h_in.buf])
            dma("sync", x1_d[:, t0:t0 + TC].rearrange("(k p) t -> p k t",
                                                      p=128),
                h_in.v3(TC), reads=[h_in.buf], writes=[db["x1"][c]])
            emit_stats_mean_var(
                [(h_in.ap(kk * TC, (kk + 1) * TC), h_in.buf)
                 for kk in range(DT8)], stat_bf, m_rep, v_tmp, r1, rstd)
            for kk in range(DT8):
                sl = h_in.ap(kk * TC, (kk + 1) * TC)

                def f4(e, s=sl):
                    return e.tensor_sub(s, s, m_rep.ap())
                OP("dve", f4, reads=[h_in.buf, m_rep.buf], writes=[h_in.buf])

                def f5(e, d=hn_bf.ap(kk * TC, (kk + 1) * TC), s=sl):
                    return e.tensor_mul(d, s, rstd.ap())
                OP("dve", f5, reads=[h_in.buf, rstd.buf], writes=[hn_bf.buf])
            dma("sync", xn2_d[:, t0:t0 + TC].rearrange("(k p) t -> p k t",
                                                       p=128),
                hn_bf.v3(TC), reads=[hn_bf.buf], writes=[db["xn2"][c]])

        def emit_ffn_phase():
            snap = {s: S.count[s] for s in SEMKEYS}
            A.reset_to_mark()

            def nb(n, dtype=BF16):
                return A.alloc(n, dtype, init_tok=snap)
            hg = nb(MF * TF)
            h1s = nb(MF * TF)
            xn2t = A.alias(h1s, DT8 * TF, BF16)
            wtA = nb(MF * 128)
            wtB = nb(MF * 128)
            g_tmp = nb(TF)
            b2_tmp = nb(TF)
            stat2 = nb(2 * TC)
            f_out = nb(DT8 * TF, F32)
            x1kk = nb(TF, F32)
            r12 = A.alias(x1kk, TC, F32)
            wsC = nb(1024, F32)
            wsD = nb(1024, F32)
            wl_state["stages"] = (wsC, wsD)
            m2 = nb(TC, F32)
            v2 = A.alias(wsC, TC, F32)
            rs2 = A.alias(stat2, TC, F32)
            xout = A.alias(hg, TF, F32)

            OP("sync", lambda e: e.reg_load(r_tok, tok_t[0:1, 0:1]),
               deps=[tok_b.w], silent=True)

            for tt in range(2048 // TF):
                def f_ro(e, k=tt * TF):
                    return e.reg_add(r_off, r_tok, k)
                OP("sync", f_ro, silent=True)

                def fxl(e):
                    src = bass.AP(xn2_dt, r_off,
                                  [[L, 128], [128 * L, DT8], [1, TF]])
                    return e.dma_start(out=xn2t.v3(TF), in_=src)
                OP("sync", fxl, writes=[xn2t.buf],
                   deps=[cb.w for cb in db["xn2"]], klass="d")

                for m in range(MF):
                    wtile_load(w1b_in[m, :, :], DT8, wtA, "w1")
                    for ntile in range(TF // 512):
                        pb, pbb = psum[ntile], psb[ntile]

                        def fm1(e, ntile=ntile, pb=pb):
                            last = None
                            for kk in range(DT8):
                                last = e.matmul(
                                    pb[:, :], wtA.ap(kk * 128, (kk + 1) * 128),
                                    xn2t.ap(kk * TF + ntile * 512,
                                            kk * TF + ntile * 512 + 512),
                                    start=(kk == 0), stop=(kk == DT8 - 1))
                            return last
                        OP("pe", fm1, reads=[wtA.buf, xn2t.buf], writes=[pbb])

                        def fe1(e, d=g_tmp.ap(ntile * 512, ntile * 512 + 512),
                                pb=pb, sc=smc(SM_OFF["s1"] + m)):
                            return e.activation(d, pb[:, :], AF.Silu, scale=sc)
                        OP("act", fe1, reads=[pbb, smalls.buf],
                           writes=[g_tmp.buf])
                    wtile_load(w2b_in[m, :, :], DT8, wtB, "w2")
                    for ntile in range(TF // 512):
                        pb, pbb = psum[2 + ntile], psb[2 + ntile]

                        def fm2(e, ntile=ntile, pb=pb):
                            last = None
                            for kk in range(DT8):
                                last = e.matmul(
                                    pb[:, :], wtB.ap(kk * 128, (kk + 1) * 128),
                                    xn2t.ap(kk * TF + ntile * 512,
                                            kk * TF + ntile * 512 + 512),
                                    start=(kk == 0), stop=(kk == DT8 - 1))
                            return last
                        OP("pe", fm2, reads=[wtB.buf, xn2t.buf], writes=[pbb])

                        def fe2(e, d=b2_tmp.ap(ntile * 512, ntile * 512 + 512),
                                pb=pb, sc=smc(SM_OFF["s2"] + m)):
                            return e.activation(d, pb[:, :], AF.Identity,
                                                scale=sc)
                        OP("act", fe2, reads=[pbb, smalls.buf],
                           writes=[b2_tmp.buf])

                    def fhg(e, d=hg.ap(m * TF, (m + 1) * TF)):
                        return e.tensor_mul(d, g_tmp.ap(), b2_tmp.ap())
                    OP("dve", fhg, reads=[g_tmp.buf, b2_tmp.buf],
                       writes=[hg.buf])

                for src_w, wkey, dst_s, scoff, act in (
                        (h1b_in, "h1", h1s, "sh1", AF.Gelu),
                        (h2b_in, "h2", hg, "sh2", AF.Gelu),
                        (w3b_in, "w3", f_out, "s3", AF.Identity)):
                    rhs_s = hg if dst_s is h1s else (h1s if dst_s is hg
                                                    else hg)
                    n_m = DT8 if dst_s is f_out else MF
                    for m in range(n_m):
                        wtm = (wtA, wtB)[m % 2]
                        wtile_load(src_w[m, :, :], MF, wtm, wkey)
                        for ntile in range(TF // 512):
                            pb, pbb = psum[ntile], psb[ntile]

                            def fm3(e, ntile=ntile, pb=pb, rhs_s=rhs_s,
                                    wtm=wtm):
                                last = None
                                for kk in range(MF):
                                    last = e.matmul(
                                        pb[:, :],
                                        wtm.ap(kk * 128, (kk + 1) * 128),
                                        rhs_s.ap(kk * TF + ntile * 512,
                                                 kk * TF + ntile * 512 + 512),
                                        start=(kk == 0), stop=(kk == MF - 1))
                                return last
                            OP("pe", fm3, reads=[wtm.buf, rhs_s.buf],
                               writes=[pbb])

                            def fe3(e, d=dst_s.ap(m * TF + ntile * 512,
                                                  m * TF + ntile * 512 + 512),
                                    pb=pb, sc=smc(SM_OFF[scoff] + m), act=act):
                                return e.activation(d, pb[:, :], act, scale=sc)
                            OP("act", fe3, reads=[pbb, smalls.buf],
                               writes=[dst_s.buf])

                for half in range(TF // 512):
                    emit_stats_mean_var(
                        [(f_out.ap(kk * TF + half * 512,
                                   kk * TF + half * 512 + 512), f_out.buf)
                         for kk in range(DT8)], stat2, m2, v2, r12, rs2)
                    for kk in range(DT8):
                        sl = f_out.ap(kk * TF + half * 512,
                                      kk * TF + half * 512 + 512)

                        def g1(e, s=sl):
                            return e.tensor_sub(s, s, m2.ap())
                        OP("dve", g1, reads=[f_out.buf, m2.buf],
                           writes=[f_out.buf])

                        def g2(e, s=sl):
                            return e.tensor_mul(s, s, rs2.ap())
                        OP("dve", g2, reads=[f_out.buf, rs2.buf],
                           writes=[f_out.buf])
                for kk in range(DT8):
                    def f_ro2(e, k=kk * 128 * L):
                        return e.reg_add(r_off2, r_off, k)
                    OP("sync", f_ro2, silent=True)

                    def fx1(e):
                        src = bass.AP(x1_dt, r_off2, [[L, 128], [1, TF]])
                        return e.dma_start(out=x1kk.ap(), in_=src)
                    OP("sync", fx1, writes=[x1kk.buf],
                       deps=[cb.w for cb in db["x1"]], klass="d")

                    def g3(e, d=xout.ap(), s=f_out.ap(kk * TF, (kk + 1) * TF)):
                        return e.tensor_add(d, s, x1kk.ap())
                    OP("dve", g3, reads=[f_out.buf, x1kk.buf],
                       writes=[xout.buf])

                    def fo(e, kk=kk, tt=tt):
                        return e.dma_start(
                            out=out_ext[kk * 128:(kk + 1) * 128,
                                        tt * TF:(tt + 1) * TF],
                            in_=xout.ap())
                    OP("sync", fo, reads=[xout.buf], writes=[db["out"]],
                       klass="d")

        dma("sync", smalls.ap(), smalls_t.ap(), writes=[smalls.buf])
        for j0 in range(0, 32 * 128, 512):
            stg_s = (wstgA, wstgB)[(j0 // 512) % 2]
            stg = stg_s.rows(0, 32, 0, 512)
            dma("sync", stg, sel_in_t.ap()[:, j0:j0 + 512],
                writes=[stg_s.buf])

            def fselc(e, d=sel.rows(0, 32, j0, j0 + 512), s=stg):
                return e.copy(d, s)
            OP("act", fselc, reads=[stg_s.buf], writes=[sel.buf])
        dma("sync", tok_t[:, :], tok_in_t.ap(), writes=[tok_b])
        OP("gp", lambda e: e.memset(ones_st.ap(), 1.0 / 1024.0),
           writes=[ones_st.buf])
        OP("gp", lambda e: e.memset(state.ap(), 0.0), writes=[state.buf])
        OP("gp", lambda e: e.memset(tails.ap(), 0.0), writes=[tails.buf])
        OP("act", lambda e: e.activation(
            A_sb.ap(), smalls.ap(SM_OFF["alog"], SM_OFF["alog"] + NL * 128),
            AF.Exp), reads=[smalls.buf], writes=[A_sb.buf])
        OP("dve", lambda e: e.tensor_scalar(A_sb.ap(), A_sb.ap(), -1.0, None,
                                            ALU.mult),
           reads=[A_sb.buf], writes=[A_sb.buf])

        prep_work = []
        for m in range(MF):
            prep_work.append((w1_in[m, :, :], w1b_in[m, :, :], "w1",
                              DT8 * 128))
            prep_work.append((w2_in[m, :, :], w2b_in[m, :, :], "w2",
                              DT8 * 128))
        for m in range(MF):
            prep_work.append((h1_in[m, :, :], h1b_in[m, :, :], "h1",
                              MF * 128))
            prep_work.append((h2_in[m, :, :], h2b_in[m, :, :], "h2",
                              MF * 128))
        for m in range(DT8):
            prep_work.append((w3_in[m, :, :], w3b_in[m, :, :], "w3",
                              MF * 128))

        def drain_prep(n_pieces):
            budget = n_pieces
            while prep_work and budget > 0:
                sf, dbf, key, ncols = prep_work.pop(0)
                emit_prep(sf, dbf, wbuf_of(key), ncols)
                budget -= (ncols + 511) // 512

        tapmap = {"hn": lambda: hn_bf, "u": lambda: u_bf,
                  "zs": lambda: zs_bf, "yg": lambda: y_g,
                  "Rc": lambda: Rc, "op": lambda: op_ev,
                  "dt": lambda: dt_et, "st": lambda: state}
        for i in range(n_layers):
            emit_layer_weights(i)
            for c in range(n_chunks):
                if c == 0:
                    emit_front(i, 0)
                if c + 1 < n_chunks:
                    emit_front(i, c + 1)
                if c > 0:
                    emit_outproj(i, c - 1)
                emit_scan(i, c)
                drain_prep(26)
                for nm, ti, tc_ in tap_list:
                    if ti == i and tc_ == c:
                        tap_dump(f"{nm}_{ti}_{tc_}", tapmap[nm]())
            emit_outproj(i, n_chunks - 1)
        if do_ffn:
            drain_prep(10**9)
            for c in range(n_chunks):
                emit_tail(c)
            emit_ffn_phase()

        final_counts = dict(S.count)
        with nc.Block() as block:
            def make_body(qname):
                ops = S.ops[qname]

                def body(eng):
                    for waits, fn, semkey, inc in ops:
                        for s2, v in sorted(waits.items()):
                            eng.wait_ge(sems[s2], v)
                        inst = fn(eng)
                        if semkey is not None:
                            inst.then_inc(sems[semkey], inc)
                    if qname == "sync":
                        for s2, v in sorted(final_counts.items()):
                            if v > 0:
                                eng.wait_ge(sems[s2], v)
                return body
            block.sync(make_body("sync"))
            block.vector(make_body("dve"))
            block.scalar(make_body("act"))
            block.tensor(make_body("pe"))
            block.gpsimd(make_body("gp"))

    return nc


def make_inmaps(inputs):
    f32 = np.float32
    x = np.asarray(inputs["L0_attn_hidden_states"], f32)
    ipw = np.asarray(inputs["L0_attn_in_proj_w"], f32)
    cw = np.asarray(inputs["L0_attn_conv_w"], f32)
    cb = np.asarray(inputs["L0_attn_conv_b"], f32)
    xpw = np.asarray(inputs["L0_attn_x_proj_w"], f32)
    dtw = np.asarray(inputs["L0_attn_dt_proj_w"], f32)
    dtb = np.asarray(inputs["L0_attn_dt_proj_b"], f32)
    alog = np.asarray(inputs["L0_attn_A_log"], f32)
    dpar = np.asarray(inputs["L0_attn_D_param"], f32)
    opw = np.asarray(inputs["L0_attn_out_proj_w"], f32)
    nw = np.asarray(inputs["L0_attn_norm_w"], f32)
    nf = np.asarray(inputs["L0_attn_norm_f_w"], f32)
    w1 = np.asarray(inputs["L0_ffn_w1_w"], f32)
    s1 = np.asarray(inputs["L0_ffn_w1_s"], f32)
    w2 = np.asarray(inputs["L0_ffn_w2_w"], f32)
    s2 = np.asarray(inputs["L0_ffn_w2_s"], f32)
    h1 = np.asarray(inputs["L0_ffn_h1_w"], f32)
    sh1 = np.asarray(inputs["L0_ffn_h1_s"], f32)
    h2 = np.asarray(inputs["L0_ffn_h2_w"], f32)
    sh2 = np.asarray(inputs["L0_ffn_h2_s"], f32)
    w3 = np.asarray(inputs["L0_ffn_w3_w"], f32)
    s3 = np.asarray(inputs["L0_ffn_w3_s"], f32)

    def tile_lhs(wT):
        K, M = wT.shape
        n_k = K // 128
        out = np.empty((M // 128, 128, K), f32)
        for m in range(M // 128):
            out[m] = (wT[:, m * 128:(m + 1) * 128]
                      .reshape(n_k, 128, 128).transpose(1, 0, 2)
                      .reshape(128, K))
        return out

    w1t = tile_lhs(np.ascontiguousarray(w1.T))
    w2t = tile_lhs(np.ascontiguousarray(w2.T))
    h1t = tile_lhs(np.ascontiguousarray(h1.T))
    h2t = tile_lhs(np.ascontiguousarray(h2.T))
    w3t = tile_lhs(np.ascontiguousarray(w3.T))
    sel = np.zeros((32, 32 * 128), f32)
    for n in range(32):
        sel[n, n * 128:(n + 1) * 128] = 1.0

    in_maps = []
    p = np.arange(128)
    for core in range(8):
        b, k = core // 2, core % 2
        sl = slice(k * EH, (k + 1) * EH)
        sl2 = slice(E + k * EH, E + (k + 1) * EH)
        wip_flat = np.concatenate([ipw[:, sl, :], ipw[:, sl2, :]],
                                  axis=1).transpose(0, 2, 1)
        wop_flat = opw[:, :, sl].transpose(0, 2, 1)
        wip = np.stack([tile_lhs(np.ascontiguousarray(wip_flat[i]))
                        for i in range(NL)])
        wop = np.stack([tile_lhs(np.ascontiguousarray(wop_flat[i]))
                        for i in range(NL)])
        wxp = np.ascontiguousarray(xpw[:, :, sl].transpose(0, 2, 1))
        wdt = np.ascontiguousarray(dtw[:, sl, :].transpose(0, 2, 1))

        sm = np.zeros((128, SM_COLS), f32)
        for i in range(NL):
            for kk in range(DT8):
                sm[:, SM_OFF["nw"] + i * DT8 + kk] = nw[i, kk * 128 + p]
        for kk in range(DT8):
            sm[:, SM_OFF["nf"] + kk] = nf[kk * 128 + p]
        for i in range(NL):
            for et in range(ET8):
                eidx = k * EH + et * 128 + p
                for n in range(NST):
                    sm[:, SM_OFF["alog"] + i * 128 + et * NST + n] = \
                        alog[i, eidx, n]
                sm[:, SM_OFF["dp"] + i * ET8 + et] = dpar[i, eidx]
                sm[:, SM_OFF["cb"] + i * ET8 + et] = cb[i, eidx]
                sm[:, SM_OFF["dtb"] + i * ET8 + et] = dtb[i, eidx]
                for kc in range(KC):
                    sm[:, SM_OFF["cw"] + i * 32 + et * 4 + kc] = \
                        cw[i, eidx, kc]
        for m in range(MF):
            sm[:, SM_OFF["s1"] + m] = s1[m * 128 + p]
            sm[:, SM_OFF["s2"] + m] = s2[m * 128 + p]
            sm[:, SM_OFF["sh1"] + m] = sh1[m * 128 + p]
            sm[:, SM_OFF["sh2"] + m] = sh2[m * 128 + p]
        for m in range(DT8):
            sm[:, SM_OFF["s3"] + m] = s3[m * 128 + p]
        sm[:, SM_OFF["c_eps"]] = EPS
        sm[:, SM_OFF["c_mhalf"]] = -0.5

        in_maps.append(dict(
            x0=np.ascontiguousarray(x[b].T),
            wip=wip, wxp=wxp, wdt=wdt, wop=wop,
            w1t=w1t, w2t=w2t, h1t=h1t, h2t=h2t, w3t=w3t,
            smalls=sm, sel32=sel,
            tok0=np.array([[k * 2048]], np.int32),
        ))
    return in_maps


def kernel(**inputs):
    in_maps = make_inmaps(inputs)
    nc = build_program()
    res = run_bass_kernel_spmd(nc, in_maps, core_ids=list(range(8)))
    out = np.empty((B, L, D), np.float32)
    for core in range(8):
        b, k = core // 2, core % 2
        r = np.asarray(res.results[core]["out_c"], np.float32)
        out[b, k * 2048:(k + 1) * 2048, :] = r.T
    return out


# version 10
# speedup vs baseline: 1.1330x; 1.1330x over previous
import numpy as np
import concourse.bass as bass
import concourse.mybir as mybir
from concourse.bass_utils import run_bass_kernel_spmd
from contextlib import ExitStack

F32 = mybir.dt.float32
BF16 = mybir.dt.bfloat16
I32 = mybir.dt.int32
AF = mybir.ActivationFunctionType
ALU = mybir.AluOpType

B, L, D, E, EH, NST, RDT, KC = 4, 4096, 1024, 2048, 1024, 16, 64, 4
NL = 4
TC = 512
NCH = L // TC
DT8 = D // 128
ET8 = EH // 128
TF = 1024
EPS = 1e-5
HFF = 4096
MF = HFF // 128
RG_PAIRS = [[0, 1], [2, 3], [4, 5], [6, 7]]

QUEUES = ("pe", "act", "dve", "sync", "gp")
NDS = {"sync": 8, "gp": 8, "act": 6}
NCCS = 4
SEMKEYS = (("pe.x", "act.x", "dve.x", "gp.x")
           + tuple(f"sync.d{i}" for i in range(NDS["sync"]))
           + tuple(f"gp.d{i}" for i in range(NDS["gp"]))
           + tuple(f"act.d{i}" for i in range(NDS["act"]))
           + tuple(f"gp.cc{i}" for i in range(NCCS)))
KLASS_INC = {"x": 1, "d": 16, "cc": 1}


def _sm_layout():
    off, col = {}, 0

    def take(name, n):
        nonlocal col
        off[name] = col
        col += n
    take("nw", NL * DT8)
    take("nf", DT8)
    take("alog", NL * 128)
    take("dp", NL * ET8)
    take("cb", NL * ET8)
    take("dtb", NL * ET8)
    take("cw", NL * ET8 * KC)
    take("s1", MF)
    take("s2", MF)
    take("sh1", MF)
    take("sh2", MF)
    take("s3", DT8)
    take("c_eps", 1)
    take("c_mhalf", 1)
    return off, col


SM_OFF, SM_COLS = _sm_layout()


class Buf:
    __slots__ = ("w", "r")

    def __init__(self, w=None):
        self.w = dict(w) if w else {}
        self.r = {}


class Sched:
    def __init__(self):
        self.ops = {q: [] for q in QUEUES}
        self.count = {s: 0 for s in SEMKEYS}
        self.wait_cache = {q: {} for q in QUEUES}
        self.rr = {}

    def op(self, queue, fn, reads=(), writes=(), deps=(), klass="x",
           silent=False):
        if klass == "x" or silent:
            semkey = f"{queue}.x"
        else:
            n_pool = NDS[queue] if klass == "d" else NCCS
            idx = self.rr.get((queue, klass), 0)
            self.rr[(queue, klass)] = idx + 1
            semkey = f"{queue}.{klass}{idx % n_pool}"
        assert silent or semkey in SEMKEYS, semkey
        d = {}
        if klass in ("d", "cc") and not silent and self.count[semkey] > 0:
            d[semkey] = self.count[semkey]

        def add(tok):
            for s2, c2 in tok.items():
                if (queue == "pe" and s2 == "pe.x"):
                    continue
                if c2 > d.get(s2, 0):
                    d[s2] = c2
        for b in reads:
            add(b.w)
        for b in writes:
            add(b.w)
            add(b.r)
        for tok in deps:
            add(tok)
        wc = self.wait_cache[queue]
        waits = {}
        for s2, c2 in d.items():
            if c2 > wc.get(s2, 0):
                waits[s2] = c2
                wc[s2] = c2
        if silent:
            self.ops[queue].append((waits, fn, None, 0))
            return {}
        inc = KLASS_INC[klass]
        self.count[semkey] += inc
        tokv = self.count[semkey]
        self.ops[queue].append((waits, fn, semkey, inc))
        tok = {semkey: tokv}
        for b in reads:
            b.r[semkey] = max(b.r.get(semkey, 0), tokv)
        for b in writes:
            b.w[semkey] = tokv
        return tok


class SBuf:

    def __init__(self, tensor, lo, n, dtype, init_tok=None):
        self.t = tensor
        self.lo = lo
        self.n = n
        self.dt = dtype
        self.buf = Buf(init_tok)

    def ap(self, a=0, b=None):
        b = self.n if b is None else b
        if self.dt == BF16:
            return self.t[:, self.lo + a:self.lo + b]
        return self.t[:, self.lo + 2 * a:self.lo + 2 * b].bitcast(F32)

    def rows(self, r0, r1, a=0, b=None):
        b = self.n if b is None else b
        if self.dt == BF16:
            return self.t[r0:r1, self.lo + a:self.lo + b]
        return self.t[r0:r1, self.lo + 2 * a:self.lo + 2 * b].bitcast(F32)

    def v3(self, inner):
        return self.ap().rearrange("p (o i) -> p o i", i=inner)


class Arena:
    def __init__(self, tensor, n_units):
        self.t = tensor
        self.n = n_units
        self.pos = 0
        self.marks = []

    def alloc(self, n, dtype=BF16, init_tok=None):
        units = n * (2 if dtype == F32 else 1)
        self.pos = (self.pos + 1) & ~1
        assert self.pos + units <= self.n, ("arena", self.pos, units, self.n)
        lo = self.pos
        self.pos += units
        return SBuf(self.t, lo, n, dtype, init_tok)

    def alias(self, base, n, dtype, off_units=0):
        s = SBuf(self.t, base.lo + off_units, n, dtype)
        s.buf = base.buf
        return s

    def mark(self):
        self.marks.append(self.pos)

    def reset_to_mark(self):
        self.pos = self.marks[-1]


def build_program(n_layers=NL, n_chunks=NCH, do_ffn=True, tap_list=(), defer=True, opev_sep=False, revert=()):
    nc = bass.Bass("TRN2", target_bir_lowering=False)
    S = Sched()

    def din(name, shape, dt=F32):
        return nc.dram_tensor(name, shape, dt, kind="ExternalInput")

    x0_t = din("x0", [D, L])
    wip_t = din("wip", [NL, 16, 128, DT8 * 128])
    wxp_t = din("wxp", [NL, EH, 96])
    wdt_t = din("wdt", [NL, RDT, EH])
    wop_t = din("wop", [NL, DT8, 128, ET8 * 128])
    w1_t = din("w1t", [MF, 128, DT8 * 128])
    w2_t = din("w2t", [MF, 128, DT8 * 128])
    h1_t = din("h1t", [MF, 128, MF * 128])
    h2_t = din("h2t", [MF, 128, MF * 128])
    w3_t = din("w3t", [DT8, 128, MF * 128])
    smalls_t = din("smalls", [128, SM_COLS])
    sel_in_t = din("sel32", [32, 32 * 128])
    tok_in_t = din("tok0", [1, 1], I32)
    out_t = nc.dram_tensor("out_c", [D, L // 2], F32, kind="ExternalOutput")

    x0_in = x0_t.ap()
    wip_in, wxp_in = wip_t.ap(), wxp_t.ap()
    wdt_in, wop_in = wdt_t.ap(), wop_t.ap()
    w1_in, w2_in = w1_t.ap(), w2_t.ap()
    h1_in, h2_in, w3_in = h1_t.ap(), h2_t.ap(), w3_t.ap()
    out_ext = out_t.ap()

    wipb_t = nc.dram_tensor("wipb_d", [NL, 16, 128, DT8 * 128], BF16,
                            kind="Internal")
    wopb_t = nc.dram_tensor("wopb_d", [NL, DT8, 128, ET8 * 128], BF16,
                            kind="Internal")
    w1b_t = nc.dram_tensor("w1b_d", [MF, 128, DT8 * 128], BF16, kind="Internal")
    w2b_t = nc.dram_tensor("w2b_d", [MF, 128, DT8 * 128], BF16, kind="Internal")
    h1b_t = nc.dram_tensor("h1b_d", [MF, 128, MF * 128], BF16, kind="Internal")
    h2b_t = nc.dram_tensor("h2b_d", [MF, 128, MF * 128], BF16, kind="Internal")
    w3b_t = nc.dram_tensor("w3b_d", [DT8, 128, MF * 128], BF16, kind="Internal")
    wipb_in, wopb_in = wipb_t.ap(), wopb_t.ap()
    w1b_in, w2b_in = w1b_t.ap(), w2b_t.ap()
    h1b_in, h2b_in, w3b_in = h1b_t.ap(), h2b_t.ap(), w3b_t.ap()
    wb_bufs = {"wip": [Buf() for _ in range(NL)],
               "wop": [Buf() for _ in range(NL)],
               "w1": Buf(), "w2": Buf(), "h1": Buf(), "h2": Buf(),
               "w3": Buf()}
    resid_t = nc.dram_tensor("resid_d", [D, L], F32, kind="Internal")
    x1_dt = nc.dram_tensor("x1_d", [D, L], F32, kind="Internal")
    xn2_dt = nc.dram_tensor("xn2_d", [D, L], BF16, kind="Internal")
    xsrc_t = nc.dram_tensor("xsrc_d", [NCH * 96, TC], F32, kind="Internal")
    xdst_t = nc.dram_tensor("xdst_d", [NCH * 96, TC], F32, kind="Internal")
    osrc_t = nc.dram_tensor("osrc_d", [NCH * D, TC], F32, kind="Internal")
    odst_t = nc.dram_tensor("odst_d", [NCH * D, TC], F32, kind="Internal")
    resid_d, x1_d, xn2_d = resid_t.ap(), x1_dt.ap(), xn2_dt.ap()
    xsrc_d, xdst_d = xsrc_t.ap(), xdst_t.ap()
    osrc_d, odst_d = osrc_t.ap(), odst_t.ap()

    db = {}
    for t in ("resid", "x1", "xn2", "xsrc", "xdst", "osrc", "odst"):
        db[t] = [Buf() for _ in range(NCH)]
    db["out"] = Buf()

    with ExitStack() as st:
        NA = 106400
        ar_t = st.enter_context(nc.sbuf_tensor("ar", [128, NA], BF16))
        tok_t = st.enter_context(nc.sbuf_tensor("tok_t", [1, 1], I32))
        psum = [st.enter_context(nc.psum_tensor(f"ps{j}", [128, 512], F32))
                for j in range(8)]
        sems = {s: st.enter_context(nc.semaphore("sem_" + s.replace(".", "_")))
                for s in SEMKEYS}
        r_tok = st.enter_context(nc.sync.register("r_tok"))
        r_off = st.enter_context(nc.sync.register("r_off"))
        r_off2 = st.enter_context(nc.sync.register("r_off2"))

        A = Arena(ar_t, NA)
        psb = [Buf() for _ in range(8)]
        tok_b = Buf()
        OP = S.op

        def dma(queue, out_ap, in_ap, reads=(), writes=(), deps=()):
            def fn(eng, o=out_ap, i=in_ap):
                return eng.dma_start(out=o, in_=i)
            return OP(queue, fn, reads=reads, writes=writes, deps=deps,
                      klass="d")

        ones_st = A.alloc(128)
        state = A.alloc(NL * 128)
        tails = A.alloc(NL * ET8 * 3)
        smalls = A.alloc(SM_COLS, F32)
        A_sb = A.alloc(NL * 128, F32)
        wstgA = A.alloc(512, F32)
        wstgB = A.alloc(512, F32)

        def smc(col):
            return smalls.ap(col, col + 1)

        nc.const_aps.register(F32, EPS, smc(SM_OFF["c_eps"]))
        nc.const_aps.register(F32, -0.5, smc(SM_OFF["c_mhalf"]))

        A.mark()

        wipT = [A.alloc(1024), A.alloc(1024)]
        wopT = [A.alloc(1024), A.alloc(1024)]
        w_xp = A.alloc(ET8 * 96)
        w_dt = A.alloc(1024)
        hn_bf = A.alloc(DT8 * TC)
        y_g = A.alloc(ET8 * TC)
        u_bf2 = [A.alloc(ET8 * TC), A.alloc(ET8 * TC)]
        zs_bf2 = [A.alloc(ET8 * TC), A.alloc(ET8 * TC)]
        a_bufs = [A.alloc(TC) for _ in range(4)]
        w_buf = A.alloc(TC)
        b_strip = A.alloc(NST * TC)
        xi_ext = A.alloc(ET8 * (TC + 3))
        xi_bufs = [Buf() for _ in range(ET8)]
        cv = A.alloc(4 * TC)
        stat_bf = A.alias(cv, 2 * TC, BF16)
        h_strip = A.alloc(NST * TC)
        op_ev = (None if opev_sep
                 else A.alias(h_strip, DT8 * TC, F32))
        Bst = A.alloc(NST * TC)
        h_in = A.alias(Bst, DT8 * TC, F32)
        Cst = A.alloc(NST * TC)
        dtin_bf = A.alloc(TC)
        sel = A.alloc(32 * 128)
        bc = (A.alloc(TC, F32) if "bc" in revert else A.alias(wstgA, TC, F32))
        bcb = A.alloc(TC)
        Rc = A.alloc(DT8 * TC, F32)
        if opev_sep:
            op_ev = A.alias(Rc, DT8 * TC, F32)
        dt_et2 = [A.alloc(TC, F32), A.alloc(TC, F32)]
        sp2 = [A.alloc(2 * TC, F32), A.alloc(2 * TC, F32)]
        y_f2 = A.alloc(TC, F32)
        y_f = A.alloc(TC, F32)
        xp_ev = A.alias(y_f, TC, F32)
        m_rep = A.alloc(TC, F32)
        v_tmp = A.alloc(TC, F32)
        rstd = A.alloc(TC, F32)
        r1 = A.alloc(TC, F32)


        def emit_stats_ms(slices, stat_rg, r1_rg, rstd_rg):
            pb, pbb = psum[4], psb[4]
            nt = len(slices)
            for kk, (ap, rb) in enumerate(slices):
                half = kk % 2
                sl = stat_rg.ap(half * TC, (half + 1) * TC)

                def fsq(e, d=sl, s=ap):
                    return e.activation(d, s, AF.Square)
                OP("act", fsq, reads=[rb], writes=[stat_rg.buf])

                def fmm(e, s=sl, first=(kk == 0), last=(kk == nt - 1)):
                    return e.matmul(pb[:, :], ones_st.ap(), s,
                                    start=first, stop=last)
                OP("pe", fmm, reads=[stat_rg.buf, ones_st.buf], writes=[pbb])

            def fln(e):
                return e.activation(r1_rg.ap(), pb[:, :], AF.Ln, bias=EPS)
            OP("act", fln, reads=[pbb, smalls.buf], writes=[r1_rg.buf])

            def fex(e):
                return e.activation(rstd_rg.ap(), r1_rg.ap(), AF.Exp,
                                    scale=-0.5)
            OP("act", fex, reads=[r1_rg.buf, smalls.buf], writes=[rstd_rg.buf])

        def emit_stats_mean_var(slices, stat_rg, m_rg, v_rg, r1_rg, rstd_rg):
            pbm, pbs = psum[4], psum[5]
            pbmb, pbsb = psb[4], psb[5]
            nt = len(slices)
            for kk, (ap, rb) in enumerate(slices):
                half = kk % 2
                sl = stat_rg.ap(half * TC, (half + 1) * TC)

                def fcp(e, d=sl, s=ap):
                    return e.copy(d, s)
                OP("act", fcp, reads=[rb], writes=[stat_rg.buf])

                def fmm(e, s=sl, first=(kk == 0), last=(kk == nt - 1)):
                    return e.matmul(pbm[:, :], ones_st.ap(), s,
                                    start=first, stop=last)
                OP("pe", fmm, reads=[stat_rg.buf, ones_st.buf], writes=[pbmb])
            for kk, (ap, rb) in enumerate(slices):
                half = kk % 2
                sl = stat_rg.ap(half * TC, (half + 1) * TC)

                def fsq(e, d=sl, s=ap):
                    return e.activation(d, s, AF.Square)
                OP("act", fsq, reads=[rb], writes=[stat_rg.buf])

                def fmm(e, s=sl, first=(kk == 0), last=(kk == nt - 1)):
                    return e.matmul(pbs[:, :], ones_st.ap(), s,
                                    start=first, stop=last)
                OP("pe", fmm, reads=[stat_rg.buf, ones_st.buf], writes=[pbsb])
            OP("act", lambda e: e.copy(m_rg.ap(), pbm[:, :]),
               reads=[pbmb], writes=[m_rg.buf])
            OP("dve", lambda e: e.tensor_mul(v_rg.ap(), m_rg.ap(), m_rg.ap()),
               reads=[m_rg.buf], writes=[v_rg.buf])
            OP("dve", lambda e: e.tensor_sub(v_rg.ap(), pbs[:, :], v_rg.ap()),
               reads=[pbsb, v_rg.buf], writes=[v_rg.buf])
            OP("act", lambda e: e.activation(r1_rg.ap(), v_rg.ap(), AF.Ln,
                                             bias=EPS),
               reads=[v_rg.buf, smalls.buf], writes=[r1_rg.buf])
            OP("act", lambda e: e.activation(rstd_rg.ap(), r1_rg.ap(), AF.Exp,
                                             scale=-0.5),
               reads=[r1_rg.buf, smalls.buf], writes=[rstd_rg.buf])

        def cast_load(queue, stage, dst, dst_lo, src_ap, n_cols, rows=128):
            for j0 in range(0, n_cols, 512):
                w = min(512, n_cols - j0)
                stg_s = wstgA if "stage" in revert else                     (wstgA, wstgB)[(j0 // 512) % 2]
                stg = stg_s.rows(0, rows, 0, w)
                dma(queue, stg, src_ap[:, j0:j0 + w], writes=[stg_s.buf])

                def fcv(e, d=dst.rows(0, rows, dst_lo + j0, dst_lo + j0 + w),
                        s=stg):
                    return e.copy(d, s)
                OP("act", fcv, reads=[stg_s.buf], writes=[dst.buf])

        wl_state = {"i": 0, "stages": None}

        def wtile_load(src_tile_bf, n_k, dst, key):
            i = wl_state["i"]
            wl_state["i"] = i + 1
            q = ("sync", "gp")[i % 2]
            dma(q, dst.ap(0, n_k * 128), src_tile_bf,
                deps=[wb_bufs[key].w], writes=[dst.buf])

        prepS = [A.alloc(512, F32), A.alloc(512, F32)]
        prepB = [A.alloc(512), A.alloc(512)]
        prep_i = [0]

        def emit_prep(src_f32, dst_bf, dst_buf, n_cols, q_in="act",
                      q_out="gp"):
            for j0 in range(0, n_cols, 512):
                w = min(512, n_cols - j0)
                k = prep_i[0]
                prep_i[0] += 1
                ps_, pb_ = prepS[k % 2], prepB[k % 2]
                dma(q_in, ps_.ap(0, w), src_f32[:, j0:j0 + w],
                    writes=[ps_.buf])

                def fc(e, d=pb_.ap(0, w), s=ps_.ap(0, w)):
                    return e.copy(d, s)
                OP("act", fc, reads=[ps_.buf], writes=[pb_.buf])
                dma(q_out, dst_bf[:, j0:j0 + w], pb_.ap(0, w),
                    reads=[pb_.buf], writes=[dst_buf if isinstance(dst_buf, Buf)
                                             else dst_buf])

        def layer_w_items(i):
            items = []
            for m in range(16):
                items.append((wip_in[i, m, :, :], wipb_in[i, m, :, :],
                              ("wip", i), DT8 * 128))
            for m in range(DT8):
                items.append((wop_in[i, m, :, :], wopb_in[i, m, :, :],
                              ("wop", i), DT8 * 128))
            return items

        def wbuf_of(key):
            if isinstance(key, tuple):
                return wb_bufs[key[0]][key[1]]
            return wb_bufs[key]

        def emit_layer_weights(i):
            for kk in range(ET8):
                cast_load("sync", None, w_xp, kk * 96,
                          wxp_in[i, kk * 128:(kk + 1) * 128, :], 96)
            cast_load("sync", None, w_dt, 0, wdt_in[i, :, :], 1024, rows=RDT)
            if i == 0:
                for it in layer_w_items(0):
                    emit_prep(it[0], it[1], wbuf_of(it[2]), it[3])
            if i + 1 < n_layers:
                prep_work[0:0] = layer_w_items(i + 1)

        taps = {}

        def tap_dump(name, region):
            ap_out = nc.dram_tensor("tap_" + name, [128, region.n], region.dt,
                                    kind="ExternalOutput").ap()
            tb = Buf()
            dma("sync", ap_out, region.ap(), reads=[region.buf], writes=[tb])
            taps[name] = tb

        def emit_front(i, c):
            u_bf = u_bf2[c % 2]
            zs_bf = zs_bf2[c % 2]
            t0 = c * TC
            r3 = Rc.v3(TC)
            rsl = resid_d[:, t0:t0 + TC].rearrange("(k p) t -> p k t", p=128)
            if i == 0:
                dma("sync", r3,
                    x0_in[:, t0:t0 + TC].rearrange("(k p) t -> p k t", p=128),
                    writes=[Rc.buf])
                emit_stats_mean_var(
                    [(Rc.ap(kk * TC, (kk + 1) * TC), Rc.buf)
                     for kk in range(DT8)], stat_bf, m_rep, v_tmp, r1, rstd)
                for kk in range(DT8):
                    sl = Rc.ap(kk * TC, (kk + 1) * TC)

                    def f1(e, s=sl):
                        return e.tensor_sub(s, s, m_rep.ap())
                    OP("dve", f1, reads=[Rc.buf, m_rep.buf], writes=[Rc.buf])

                    def f2(e, s=sl):
                        return e.tensor_mul(s, s, rstd.ap())
                    OP("dve", f2, reads=[Rc.buf, rstd.buf], writes=[Rc.buf])
                dma("sync", rsl, r3, reads=[Rc.buf], writes=[db["resid"][c]])
            else:
                dma("sync", r3, rsl, reads=[db["resid"][c]], writes=[Rc.buf])
                hsl = odst_d[c * D:(c + 1) * D, :].rearrange(
                    "(k p) t -> p k t", p=128)
                dma("sync", h_in.v3(TC), hsl, reads=[db["odst"][c]],
                    writes=[h_in.buf])
                OP("dve", lambda e: e.tensor_add(Rc.ap(), Rc.ap(), h_in.ap()),
                   reads=[Rc.buf, h_in.buf], writes=[Rc.buf])
                dma("sync", rsl, r3, reads=[Rc.buf], writes=[db["resid"][c]])
            emit_stats_ms(
                [(Rc.ap(kk * TC, (kk + 1) * TC), Rc.buf) for kk in range(DT8)],
                stat_bf, r1, rstd)
            for kk in range(DT8):
                def fhn(e, d=hn_bf.ap(kk * TC, (kk + 1) * TC),
                        s=Rc.ap(kk * TC, (kk + 1) * TC),
                        sc=smc(SM_OFF["nw"] + i * DT8 + kk)):
                    return e.scalar_tensor_tensor(d, s, sc, rstd.ap(),
                                                  ALU.mult, ALU.mult)
                OP("dve", fhn, reads=[Rc.buf, smalls.buf, rstd.buf],
                   writes=[hn_bf.buf])

            for et in range(ET8):
                base = et * (TC + 3)
                tl = (i * ET8 + et) * 3

                def ftl(e, d=xi_ext.ap(base, base + 3),
                        s=tails.ap(tl, tl + 3)):
                    return e.copy(d, s)
                OP("act", ftl, reads=[tails.buf], writes=[xi_bufs[et]])

            for m in range(16):
                pb, pbb = psum[m % 4], psb[m % 4]
                wt_ = wipT[m % 2]
                dma("sync", wt_.ap(), wipb_in[i, m, :, :],
                    deps=[wb_bufs["wip"][i].w], writes=[wt_.buf])

                def fmm(e, pb=pb, wt_=wt_):
                    last = None
                    for kk in range(DT8):
                        last = e.matmul(
                            pb[:, :],
                            wt_.ap(kk * 128, (kk + 1) * 128),
                            hn_bf.ap(kk * TC, (kk + 1) * TC),
                            start=(kk == 0), stop=(kk == DT8 - 1))
                    return last
                OP("pe", fmm, reads=[wt_.buf, hn_bf.buf], writes=[pbb])
                if m < ET8:
                    def fev(e, d=xi_ext.ap(m * (TC + 3) + 3,
                                           (m + 1) * (TC + 3)), pb=pb):
                        return e.copy(d, pb[:, :])
                    OP("act", fev, reads=[pbb], writes=[xi_bufs[m]])
                else:
                    def fev(e, d=zs_bf.ap((m - ET8) * TC, (m - ET8 + 1) * TC),
                            pb=pb):
                        return e.activation(d, pb[:, :], AF.Silu)
                    OP("act", fev, reads=[pbb], writes=[zs_bf.buf])

            for et in range(ET8):
                base = et * (TC + 3)
                tl = (i * ET8 + et) * 3
                for k in range(KC):
                    def ftap(e, d=cv.ap(k * TC, (k + 1) * TC),
                             s=xi_ext.ap(base + k, base + k + TC),
                             sc=smc(SM_OFF["cw"] + i * 32 + et * 4 + k)):
                        return e.activation(d, s, AF.Identity, scale=sc)
                    OP("act", ftap, reads=[xi_bufs[et], smalls.buf],
                       writes=[cv.buf])

                def fa1(e, d=cv.ap(0, TC), a=cv.ap(0, TC),
                        b2=cv.ap(TC, 2 * TC)):
                    return e.tensor_add(d, a, b2)
                OP("dve", fa1, reads=[cv.buf], writes=[cv.buf])

                def fa2(e, d=cv.ap(2 * TC, 3 * TC), a=cv.ap(2 * TC, 3 * TC),
                        b2=cv.ap(3 * TC, 4 * TC)):
                    return e.tensor_add(d, a, b2)
                OP("dve", fa2, reads=[cv.buf], writes=[cv.buf])

                def fa3(e, d=cv.ap(0, TC), a=cv.ap(0, TC),
                        b2=cv.ap(2 * TC, 3 * TC)):
                    return e.tensor_add(d, a, b2)
                OP("dve", fa3, reads=[cv.buf], writes=[cv.buf])

                def fu(e, d=u_bf.ap(et * TC, (et + 1) * TC), s=cv.ap(0, TC),
                       bias=smc(SM_OFF["cb"] + i * ET8 + et)):
                    return e.activation(d, s, AF.Silu, bias=bias)
                OP("act", fu, reads=[cv.buf, smalls.buf], writes=[u_bf.buf])

                def fsv(e, d=tails.ap(tl, tl + 3),
                        s=xi_ext.ap(base + TC, base + TC + 3)):
                    return e.copy(d, s)
                OP("act", fsv, reads=[xi_bufs[et]], writes=[tails.buf])

            pb4, pbb4 = psum[4], psb[4]

            def fxp(e):
                last = None
                for kk in range(ET8):
                    last = e.matmul(pb4[0:96, :],
                                    w_xp.ap(kk * 96, (kk + 1) * 96),
                                    u_bf.ap(kk * TC, (kk + 1) * TC),
                                    start=(kk == 0), stop=(kk == ET8 - 1))
                return last
            OP("pe", fxp, reads=[w_xp.buf, u_bf.buf], writes=[pbb4])
            OP("act", lambda e: e.copy(xp_ev.rows(0, 96), pb4[0:96, :]),
               reads=[pbb4], writes=[xp_ev.buf])
            dma("gp", xsrc_d[c * 96:(c + 1) * 96, :], xp_ev.rows(0, 96),
                reads=[xp_ev.buf], writes=[db["xsrc"][c]])

            def fcc(e, c=c):
                return e.collective_compute(
                    "AllReduce", ALU.add, replica_groups=RG_PAIRS,
                    ins=[xsrc_d[c * 96:(c + 1) * 96, :].opt()],
                    outs=[xdst_d[c * 96:(c + 1) * 96, :].opt()])
            OP("gp", fcc, reads=[db["xsrc"][c]], writes=[db["xdst"][c]],
               klass="cc")

        def emit_dt_chain(i, et):
            dt_et = dt_et2[et % 2]
            sp = sp2[et % 2]
            pb5, pbb5 = psum[5], psb[5]

            def fdt(e, et=et):
                return e.matmul(pb5[:, :],
                                w_dt.rows(0, RDT, et * 128, (et + 1) * 128),
                                dtin_bf.rows(0, RDT),
                                start=True, stop=True)
            OP("pe", fdt, reads=[w_dt.buf, dtin_bf.buf], writes=[pbb5])
            dtb_sc = smc(SM_OFF["dtb"] + i * ET8 + et)

            def f_abs(e, sc=dtb_sc, sp=sp):
                return e.activation(sp.ap(0, TC), pb5[:, :], AF.Abs, bias=sc)
            OP("act", f_abs, reads=[pbb5, smalls.buf], writes=[sp.buf])

            def f_exp(e, sp=sp):
                return e.activation(sp.ap(TC, 2 * TC), sp.ap(0, TC),
                                    AF.Exp, scale=-1.0)
            OP("act", f_exp, reads=[sp.buf], writes=[sp.buf])

            def f_ln(e, sp=sp):
                return e.activation(sp.ap(0, TC), sp.ap(TC, 2 * TC),
                                    AF.Ln, bias=1.0)
            OP("act", f_ln, reads=[sp.buf], writes=[sp.buf])

            def f_rel(e, sc=dtb_sc, sp=sp):
                return e.activation(sp.ap(TC, 2 * TC), pb5[:, :],
                                    AF.Relu, bias=sc)
            OP("act", f_rel, reads=[pbb5, smalls.buf], writes=[sp.buf])

            def f_dt(e, sp=sp, dt_et=dt_et):
                return e.tensor_add(dt_et.ap(), sp.ap(0, TC),
                                    sp.ap(TC, 2 * TC))
            OP("dve", f_dt, reads=[sp.buf], writes=[dt_et.buf])

        def emit_scan(i, c):
            u_bf = u_bf2[c % 2]
            zs_bf = zs_bf2[c % 2]
            dma("gp", dtin_bf.rows(0, RDT), xdst_d[c * 96:c * 96 + RDT, :],
                reads=[db["xdst"][c]], writes=[dtin_bf.buf])
            dma("gp", bc.rows(0, 32), xdst_d[c * 96 + RDT:(c + 1) * 96, :],
                reads=[db["xdst"][c]], writes=[bc.buf])
            OP("act", lambda e: e.copy(bcb.rows(0, 32), bc.rows(0, 32)),
               reads=[bc.buf], writes=[bcb.buf])

            for n in range(NST):
                jB = 4 + (n % 2) * 2
                jC = 5 + (n % 2) * 2

                def fb(e, n=n, pb=psum[jB]):
                    return e.matmul(pb[:, :],
                                    sel.rows(0, 32, n * 128, (n + 1) * 128),
                                    bcb.rows(0, 32), start=True, stop=True)
                OP("pe", fb, reads=[sel.buf, bcb.buf], writes=[psb[jB]])

                def fbe(e, d=Bst.ap(n * TC, (n + 1) * TC), pb=psum[jB]):
                    return e.copy(d, pb[:, :])
                OP("act", fbe, reads=[psb[jB]], writes=[Bst.buf])

                def fc2(e, n=n, pb=psum[jC]):
                    return e.matmul(
                        pb[:, :],
                        sel.rows(0, 32, (NST + n) * 128, (NST + n + 1) * 128),
                        bcb.rows(0, 32), start=True, stop=True)
                OP("pe", fc2, reads=[sel.buf, bcb.buf], writes=[psb[jC]])

                def fce(e, d=Cst.ap(n * TC, (n + 1) * TC), pb=psum[jC]):
                    return e.copy(d, pb[:, :])
                OP("act", fce, reads=[psb[jC]], writes=[Cst.buf])

            for et in range(ET8):
                dt_et = dt_et2[et % 2]
                if et == 0:
                    emit_dt_chain(i, 0)
                if et + 1 < ET8:
                    emit_dt_chain(i, et + 1)

                def fw(e, et=et, dt_et=dt_et):
                    return e.tensor_mul(w_buf.ap(), dt_et.ap(),
                                        u_bf.ap(et * TC, (et + 1) * TC))
                OP("dve", fw, reads=[dt_et.buf, u_bf.buf], writes=[w_buf.buf])

                def fb2(e):
                    return e.tensor_tensor(
                        b_strip.v3(TC),
                        w_buf.ap().unsqueeze(1).broadcast_to([128, NST, TC]),
                        Bst.v3(TC), ALU.mult)
                OP("dve", fb2, reads=[w_buf.buf, Bst.buf],
                   writes=[b_strip.buf])
                for n in range(NST):
                    acol = A_sb.ap(i * 128 + et * NST + n,
                                   i * 128 + et * NST + n + 1)
                    ab_s = a_bufs[n % 4]
                    ab = ab_s.ap()

                    def fe(e, ab=ab, acol=acol, dt_et=dt_et):
                        return e.activation(ab, dt_et.ap(), AF.Exp, scale=acol)
                    OP("act", fe, reads=[dt_et.buf, A_sb.buf],
                       writes=[ab_s.buf])
                    stcol = state.ap(i * 128 + et * NST + n,
                                     i * 128 + et * NST + n + 1)

                    def fs(e, n=n, ab=ab, stcol=stcol, ab_s=ab_s):
                        return e.tensor_tensor_scan(
                            h_strip.ap(n * TC, (n + 1) * TC), ab,
                            b_strip.ap(n * TC, (n + 1) * TC), stcol,
                            ALU.mult, ALU.add)
                    OP("dve", fs, reads=[ab_s.buf, b_strip.buf, state.buf],
                       writes=[h_strip.buf])

                def fst(e, et=et):
                    v = h_strip.v3(TC)[:, :, TC - 1:TC]
                    return e.tensor_copy(
                        state.ap(i * 128 + et * NST, i * 128 + (et + 1) * NST),
                        v)
                OP("dve", fst, reads=[h_strip.buf], writes=[state.buf])

                def fq(e):
                    return e.tensor_mul(b_strip.ap(), h_strip.ap(), Cst.ap())
                OP("dve", fq, reads=[h_strip.buf, Cst.buf],
                   writes=[b_strip.buf])
                qv = b_strip.ap().rearrange("p (n two t) -> p n two t",
                                            two=2, t=TC)

                def ftr1(e, qv=qv):
                    return e.tensor_tensor(h_strip.v3(TC)[:, 0:8, :],
                                           qv[:, :, 0, :], qv[:, :, 1, :],
                                           ALU.add)
                OP("dve", ftr1, reads=[b_strip.buf], writes=[h_strip.buf])
                hv = h_strip.ap().rearrange("p (g two t) -> p g two t",
                                            two=2, t=TC)

                def ftr2(e, hv=hv):
                    return e.tensor_tensor(h_strip.v3(TC)[:, 8:12, :],
                                           hv[:, 0:4, 0, :], hv[:, 0:4, 1, :],
                                           ALU.add)
                OP("dve", ftr2, reads=[h_strip.buf], writes=[h_strip.buf])

                def ftr3(e, hv=hv):
                    return e.tensor_tensor(h_strip.v3(TC)[:, 12:14, :],
                                           hv[:, 4:6, 0, :], hv[:, 4:6, 1, :],
                                           ALU.add)
                OP("dve", ftr3, reads=[h_strip.buf], writes=[h_strip.buf])

                def ftr4(e):
                    return e.tensor_tensor(y_f.ap(),
                                           h_strip.ap(12 * TC, 13 * TC),
                                           h_strip.ap(13 * TC, 14 * TC),
                                           ALU.add)
                OP("dve", ftr4, reads=[h_strip.buf], writes=[y_f.buf])

                def fy2(e, et=et, sc=smc(SM_OFF["dp"] + i * ET8 + et)):
                    return e.scalar_tensor_tensor(
                        y_f2.ap(), u_bf.ap(et * TC, (et + 1) * TC), sc,
                        y_f.ap(), ALU.mult, ALU.add)
                OP("dve", fy2, reads=[u_bf.buf, smalls.buf, y_f.buf],
                   writes=[y_f2.buf])

                def fyg(e, et=et):
                    return e.tensor_mul(y_g.ap(et * TC, (et + 1) * TC),
                                        y_f2.ap(),
                                        zs_bf.ap(et * TC, (et + 1) * TC))
                OP("dve", fyg, reads=[y_f2.buf, zs_bf.buf], writes=[y_g.buf])

        def emit_outproj(i, c):
            for m in range(DT8):
                pb, pbb = psum[m % 4], psb[m % 4]
                wt_ = wopT[m % 2]
                dma("sync", wt_.ap(), wopb_in[i, m, :, :],
                    deps=[wb_bufs["wop"][i].w], writes=[wt_.buf])

                def fop(e, pb=pb, wt_=wt_):
                    last = None
                    for kk in range(ET8):
                        last = e.matmul(
                            pb[:, :],
                            wt_.ap(kk * 128, (kk + 1) * 128),
                            y_g.ap(kk * TC, (kk + 1) * TC),
                            start=(kk == 0), stop=(kk == ET8 - 1))
                    return last
                OP("pe", fop, reads=[wt_.buf, y_g.buf], writes=[pbb])

                def foev(e, d=op_ev.ap(m * TC, (m + 1) * TC), pb=pb):
                    return e.copy(d, pb[:, :])
                OP("act", foev, reads=[pbb], writes=[op_ev.buf])
            dma("gp", osrc_d[c * D:(c + 1) * D, :].rearrange(
                "(k p) t -> p k t", p=128), op_ev.v3(TC),
                reads=[op_ev.buf], writes=[db["osrc"][c]])

            def focc(e, c=c):
                return e.collective_compute(
                    "AllReduce", ALU.add, replica_groups=RG_PAIRS,
                    ins=[osrc_d[c * D:(c + 1) * D, :].opt()],
                    outs=[odst_d[c * D:(c + 1) * D, :].opt()])
            OP("gp", focc, reads=[db["osrc"][c]], writes=[db["odst"][c]],
               klass="cc")

        def emit_tail(c):
            t0 = c * TC
            rsl = resid_d[:, t0:t0 + TC].rearrange("(k p) t -> p k t", p=128)
            dma("sync", Rc.v3(TC), rsl, reads=[db["resid"][c]],
                writes=[Rc.buf])
            hsl = odst_d[c * D:(c + 1) * D, :].rearrange("(k p) t -> p k t",
                                                         p=128)
            dma("sync", h_in.v3(TC), hsl, reads=[db["odst"][c]],
                writes=[h_in.buf])
            OP("dve", lambda e: e.tensor_add(Rc.ap(), Rc.ap(), h_in.ap()),
               reads=[Rc.buf, h_in.buf], writes=[Rc.buf])
            emit_stats_ms(
                [(Rc.ap(kk * TC, (kk + 1) * TC), Rc.buf) for kk in range(DT8)],
                stat_bf, r1, rstd)
            for kk in range(DT8):
                def fat(e, d=h_in.ap(kk * TC, (kk + 1) * TC),
                        s=Rc.ap(kk * TC, (kk + 1) * TC),
                        sc=smc(SM_OFF["nf"] + kk)):
                    return e.scalar_tensor_tensor(d, s, sc, rstd.ap(),
                                                  ALU.mult, ALU.mult)
                OP("dve", fat, reads=[Rc.buf, smalls.buf, rstd.buf],
                   writes=[h_in.buf])
            emit_stats_mean_var(
                [(h_in.ap(kk * TC, (kk + 1) * TC), h_in.buf)
                 for kk in range(DT8)], stat_bf, m_rep, v_tmp, r1, rstd)
            dma("sync", Rc.v3(TC),
                x0_in[:, t0:t0 + TC].rearrange("(k p) t -> p k t", p=128),
                writes=[Rc.buf])
            for kk in range(DT8):
                sl = h_in.ap(kk * TC, (kk + 1) * TC)

                def f1(e, s=sl):
                    return e.tensor_sub(s, s, m_rep.ap())
                OP("dve", f1, reads=[h_in.buf, m_rep.buf], writes=[h_in.buf])

                def f2(e, s=sl):
                    return e.tensor_mul(s, s, rstd.ap())
                OP("dve", f2, reads=[h_in.buf, rstd.buf], writes=[h_in.buf])

                def f3(e, s=sl, x=Rc.ap(kk * TC, (kk + 1) * TC)):
                    return e.tensor_add(s, s, x)
                OP("dve", f3, reads=[h_in.buf, Rc.buf], writes=[h_in.buf])
            dma("sync", x1_d[:, t0:t0 + TC].rearrange("(k p) t -> p k t",
                                                      p=128),
                h_in.v3(TC), reads=[h_in.buf], writes=[db["x1"][c]])
            emit_stats_mean_var(
                [(h_in.ap(kk * TC, (kk + 1) * TC), h_in.buf)
                 for kk in range(DT8)], stat_bf, m_rep, v_tmp, r1, rstd)
            for kk in range(DT8):
                sl = h_in.ap(kk * TC, (kk + 1) * TC)

                def f4(e, s=sl):
                    return e.tensor_sub(s, s, m_rep.ap())
                OP("dve", f4, reads=[h_in.buf, m_rep.buf], writes=[h_in.buf])

                def f5(e, d=hn_bf.ap(kk * TC, (kk + 1) * TC), s=sl):
                    return e.tensor_mul(d, s, rstd.ap())
                OP("dve", f5, reads=[h_in.buf, rstd.buf], writes=[hn_bf.buf])
            dma("sync", xn2_d[:, t0:t0 + TC].rearrange("(k p) t -> p k t",
                                                       p=128),
                hn_bf.v3(TC), reads=[hn_bf.buf], writes=[db["xn2"][c]])

        def emit_ffn_phase():
            snap = {s: S.count[s] for s in SEMKEYS}
            A.reset_to_mark()

            def nb(n, dtype=BF16):
                return A.alloc(n, dtype, init_tok=snap)
            hg = nb(MF * TF)
            h1s = nb(MF * TF)
            xn2t = A.alias(h1s, DT8 * TF, BF16)
            wtA = nb(MF * 128)
            wtB = nb(MF * 128)
            g_tmp = nb(TF)
            b2_tmp = nb(TF)
            stat2 = nb(2 * TC)
            f_out = nb(DT8 * TF, F32)
            x1kk = nb(TF, F32)
            r12 = A.alias(x1kk, TC, F32)
            wsC = nb(1024, F32)
            wsD = nb(1024, F32)
            wl_state["stages"] = (wsC, wsD)
            m2 = nb(TC, F32)
            v2 = A.alias(wsC, TC, F32)
            rs2 = A.alias(stat2, TC, F32)
            xout = A.alias(hg, TF, F32)

            OP("sync", lambda e: e.reg_load(r_tok, tok_t[0:1, 0:1]),
               deps=[tok_b.w], silent=True)

            for tt in range(2048 // TF):
                def f_ro(e, k=tt * TF):
                    return e.reg_add(r_off, r_tok, k)
                OP("sync", f_ro, silent=True)

                def fxl(e):
                    src = bass.AP(xn2_dt, r_off,
                                  [[L, 128], [128 * L, DT8], [1, TF]])
                    return e.dma_start(out=xn2t.v3(TF), in_=src)
                OP("sync", fxl, writes=[xn2t.buf],
                   deps=[cb.w for cb in db["xn2"]], klass="d")

                for m in range(MF):
                    wtile_load(w1b_in[m, :, :], DT8, wtA, "w1")
                    for ntile in range(TF // 512):
                        pb, pbb = psum[ntile], psb[ntile]

                        def fm1(e, ntile=ntile, pb=pb):
                            last = None
                            for kk in range(DT8):
                                last = e.matmul(
                                    pb[:, :], wtA.ap(kk * 128, (kk + 1) * 128),
                                    xn2t.ap(kk * TF + ntile * 512,
                                            kk * TF + ntile * 512 + 512),
                                    start=(kk == 0), stop=(kk == DT8 - 1))
                            return last
                        OP("pe", fm1, reads=[wtA.buf, xn2t.buf], writes=[pbb])

                        def fe1(e, d=g_tmp.ap(ntile * 512, ntile * 512 + 512),
                                pb=pb, sc=smc(SM_OFF["s1"] + m)):
                            return e.activation(d, pb[:, :], AF.Silu, scale=sc)
                        OP("act", fe1, reads=[pbb, smalls.buf],
                           writes=[g_tmp.buf])
                    wtile_load(w2b_in[m, :, :], DT8, wtB, "w2")
                    for ntile in range(TF // 512):
                        pb, pbb = psum[2 + ntile], psb[2 + ntile]

                        def fm2(e, ntile=ntile, pb=pb):
                            last = None
                            for kk in range(DT8):
                                last = e.matmul(
                                    pb[:, :], wtB.ap(kk * 128, (kk + 1) * 128),
                                    xn2t.ap(kk * TF + ntile * 512,
                                            kk * TF + ntile * 512 + 512),
                                    start=(kk == 0), stop=(kk == DT8 - 1))
                            return last
                        OP("pe", fm2, reads=[wtB.buf, xn2t.buf], writes=[pbb])

                        def fe2(e, d=b2_tmp.ap(ntile * 512, ntile * 512 + 512),
                                pb=pb, sc=smc(SM_OFF["s2"] + m)):
                            return e.activation(d, pb[:, :], AF.Identity,
                                                scale=sc)
                        OP("act", fe2, reads=[pbb, smalls.buf],
                           writes=[b2_tmp.buf])

                    def fhg(e, d=hg.ap(m * TF, (m + 1) * TF)):
                        return e.tensor_mul(d, g_tmp.ap(), b2_tmp.ap())
                    OP("dve", fhg, reads=[g_tmp.buf, b2_tmp.buf],
                       writes=[hg.buf])

                for src_w, wkey, dst_s, scoff, act in (
                        (h1b_in, "h1", h1s, "sh1", AF.Gelu),
                        (h2b_in, "h2", hg, "sh2", AF.Gelu),
                        (w3b_in, "w3", f_out, "s3", AF.Identity)):
                    rhs_s = hg if dst_s is h1s else (h1s if dst_s is hg
                                                    else hg)
                    n_m = DT8 if dst_s is f_out else MF
                    for m in range(n_m):
                        wtm = (wtA, wtB)[m % 2]
                        wtile_load(src_w[m, :, :], MF, wtm, wkey)
                        for ntile in range(TF // 512):
                            pb, pbb = psum[ntile], psb[ntile]

                            def fm3(e, ntile=ntile, pb=pb, rhs_s=rhs_s,
                                    wtm=wtm):
                                last = None
                                for kk in range(MF):
                                    last = e.matmul(
                                        pb[:, :],
                                        wtm.ap(kk * 128, (kk + 1) * 128),
                                        rhs_s.ap(kk * TF + ntile * 512,
                                                 kk * TF + ntile * 512 + 512),
                                        start=(kk == 0), stop=(kk == MF - 1))
                                return last
                            OP("pe", fm3, reads=[wtm.buf, rhs_s.buf],
                               writes=[pbb])

                            def fe3(e, d=dst_s.ap(m * TF + ntile * 512,
                                                  m * TF + ntile * 512 + 512),
                                    pb=pb, sc=smc(SM_OFF[scoff] + m), act=act):
                                return e.activation(d, pb[:, :], act, scale=sc)
                            OP("act", fe3, reads=[pbb, smalls.buf],
                               writes=[dst_s.buf])

                for half in range(TF // 512):
                    emit_stats_mean_var(
                        [(f_out.ap(kk * TF + half * 512,
                                   kk * TF + half * 512 + 512), f_out.buf)
                         for kk in range(DT8)], stat2, m2, v2, r12, rs2)
                    for kk in range(DT8):
                        sl = f_out.ap(kk * TF + half * 512,
                                      kk * TF + half * 512 + 512)

                        def g1(e, s=sl):
                            return e.tensor_sub(s, s, m2.ap())
                        OP("dve", g1, reads=[f_out.buf, m2.buf],
                           writes=[f_out.buf])

                        def g2(e, s=sl):
                            return e.tensor_mul(s, s, rs2.ap())
                        OP("dve", g2, reads=[f_out.buf, rs2.buf],
                           writes=[f_out.buf])
                for kk in range(DT8):
                    def f_ro2(e, k=kk * 128 * L):
                        return e.reg_add(r_off2, r_off, k)
                    OP("sync", f_ro2, silent=True)

                    def fx1(e):
                        src = bass.AP(x1_dt, r_off2, [[L, 128], [1, TF]])
                        return e.dma_start(out=x1kk.ap(), in_=src)
                    OP("sync", fx1, writes=[x1kk.buf],
                       deps=[cb.w for cb in db["x1"]], klass="d")

                    def g3(e, d=xout.ap(), s=f_out.ap(kk * TF, (kk + 1) * TF)):
                        return e.tensor_add(d, s, x1kk.ap())
                    OP("dve", g3, reads=[f_out.buf, x1kk.buf],
                       writes=[xout.buf])

                    def fo(e, kk=kk, tt=tt):
                        return e.dma_start(
                            out=out_ext[kk * 128:(kk + 1) * 128,
                                        tt * TF:(tt + 1) * TF],
                            in_=xout.ap())
                    OP("sync", fo, reads=[xout.buf], writes=[db["out"]],
                       klass="d")

        dma("sync", smalls.ap(), smalls_t.ap(), writes=[smalls.buf])
        for j0 in range(0, 32 * 128, 512):
            stg_s = (wstgA, wstgB)[(j0 // 512) % 2]
            stg = stg_s.rows(0, 32, 0, 512)
            dma("sync", stg, sel_in_t.ap()[:, j0:j0 + 512],
                writes=[stg_s.buf])

            def fselc(e, d=sel.rows(0, 32, j0, j0 + 512), s=stg):
                return e.copy(d, s)
            OP("act", fselc, reads=[stg_s.buf], writes=[sel.buf])
        dma("sync", tok_t[:, :], tok_in_t.ap(), writes=[tok_b])
        OP("gp", lambda e: e.memset(ones_st.ap(), 1.0 / 1024.0),
           writes=[ones_st.buf])
        OP("gp", lambda e: e.memset(state.ap(), 0.0), writes=[state.buf])
        OP("gp", lambda e: e.memset(tails.ap(), 0.0), writes=[tails.buf])
        OP("act", lambda e: e.activation(
            A_sb.ap(), smalls.ap(SM_OFF["alog"], SM_OFF["alog"] + NL * 128),
            AF.Exp), reads=[smalls.buf], writes=[A_sb.buf])
        OP("dve", lambda e: e.tensor_scalar(A_sb.ap(), A_sb.ap(), -1.0, None,
                                            ALU.mult),
           reads=[A_sb.buf], writes=[A_sb.buf])

        prep_work = []
        for m in range(MF):
            prep_work.append((w1_in[m, :, :], w1b_in[m, :, :], "w1",
                              DT8 * 128))
            prep_work.append((w2_in[m, :, :], w2b_in[m, :, :], "w2",
                              DT8 * 128))
        for m in range(MF):
            prep_work.append((h1_in[m, :, :], h1b_in[m, :, :], "h1",
                              MF * 128))
            prep_work.append((h2_in[m, :, :], h2b_in[m, :, :], "h2",
                              MF * 128))
        for m in range(DT8):
            prep_work.append((w3_in[m, :, :], w3b_in[m, :, :], "w3",
                              MF * 128))

        def drain_prep(n_pieces):
            budget = n_pieces
            while prep_work and budget > 0:
                sf, dbf, key, ncols = prep_work.pop(0)
                emit_prep(sf, dbf, wbuf_of(key), ncols)
                budget -= (ncols + 511) // 512

        tapmap = {"hn": lambda: hn_bf, "u": lambda: u_bf,
                  "zs": lambda: zs_bf, "yg": lambda: y_g,
                  "Rc": lambda: Rc, "op": lambda: op_ev,
                  "dt": lambda: dt_et, "st": lambda: state}
        for i in range(n_layers):
            emit_layer_weights(i)
            for c in range(n_chunks):
                if c == 0:
                    emit_front(i, 0)
                if c + 1 < n_chunks:
                    emit_front(i, c + 1)
                if c > 0:
                    emit_outproj(i, c - 1)
                emit_scan(i, c)
                drain_prep(26)
                for nm, ti, tc_ in tap_list:
                    if ti == i and tc_ == c:
                        tap_dump(f"{nm}_{ti}_{tc_}", tapmap[nm]())
            emit_outproj(i, n_chunks - 1)
        if do_ffn:
            drain_prep(10**9)
            for c in range(n_chunks):
                emit_tail(c)
            emit_ffn_phase()

        final_counts = dict(S.count)
        with nc.Block() as block:
            def make_body(qname):
                ops = S.ops[qname]

                def body(eng):
                    for waits, fn, semkey, inc in ops:
                        for s2, v in sorted(waits.items()):
                            eng.wait_ge(sems[s2], v)
                        inst = fn(eng)
                        if semkey is not None:
                            inst.then_inc(sems[semkey], inc)
                    if qname == "sync":
                        for s2, v in sorted(final_counts.items()):
                            if v > 0:
                                eng.wait_ge(sems[s2], v)
                return body
            block.sync(make_body("sync"))
            block.vector(make_body("dve"))
            block.scalar(make_body("act"))
            block.tensor(make_body("pe"))
            block.gpsimd(make_body("gp"))

    return nc


def make_inmaps(inputs):
    f32 = np.float32
    x = np.asarray(inputs["L0_attn_hidden_states"], f32)
    ipw = np.asarray(inputs["L0_attn_in_proj_w"], f32)
    cw = np.asarray(inputs["L0_attn_conv_w"], f32)
    cb = np.asarray(inputs["L0_attn_conv_b"], f32)
    xpw = np.asarray(inputs["L0_attn_x_proj_w"], f32)
    dtw = np.asarray(inputs["L0_attn_dt_proj_w"], f32)
    dtb = np.asarray(inputs["L0_attn_dt_proj_b"], f32)
    alog = np.asarray(inputs["L0_attn_A_log"], f32)
    dpar = np.asarray(inputs["L0_attn_D_param"], f32)
    opw = np.asarray(inputs["L0_attn_out_proj_w"], f32)
    nw = np.asarray(inputs["L0_attn_norm_w"], f32)
    nf = np.asarray(inputs["L0_attn_norm_f_w"], f32)
    w1 = np.asarray(inputs["L0_ffn_w1_w"], f32)
    s1 = np.asarray(inputs["L0_ffn_w1_s"], f32)
    w2 = np.asarray(inputs["L0_ffn_w2_w"], f32)
    s2 = np.asarray(inputs["L0_ffn_w2_s"], f32)
    h1 = np.asarray(inputs["L0_ffn_h1_w"], f32)
    sh1 = np.asarray(inputs["L0_ffn_h1_s"], f32)
    h2 = np.asarray(inputs["L0_ffn_h2_w"], f32)
    sh2 = np.asarray(inputs["L0_ffn_h2_s"], f32)
    w3 = np.asarray(inputs["L0_ffn_w3_w"], f32)
    s3 = np.asarray(inputs["L0_ffn_w3_s"], f32)

    def tile_lhs(wT):
        K, M = wT.shape
        n_k = K // 128
        out = np.empty((M // 128, 128, K), f32)
        for m in range(M // 128):
            out[m] = (wT[:, m * 128:(m + 1) * 128]
                      .reshape(n_k, 128, 128).transpose(1, 0, 2)
                      .reshape(128, K))
        return out

    w1t = tile_lhs(np.ascontiguousarray(w1.T))
    w2t = tile_lhs(np.ascontiguousarray(w2.T))
    h1t = tile_lhs(np.ascontiguousarray(h1.T))
    h2t = tile_lhs(np.ascontiguousarray(h2.T))
    w3t = tile_lhs(np.ascontiguousarray(w3.T))
    sel = np.zeros((32, 32 * 128), f32)
    for n in range(32):
        sel[n, n * 128:(n + 1) * 128] = 1.0

    in_maps = []
    p = np.arange(128)
    for core in range(8):
        b, k = core // 2, core % 2
        sl = slice(k * EH, (k + 1) * EH)
        sl2 = slice(E + k * EH, E + (k + 1) * EH)
        wip_flat = np.concatenate([ipw[:, sl, :], ipw[:, sl2, :]],
                                  axis=1).transpose(0, 2, 1)
        wop_flat = opw[:, :, sl].transpose(0, 2, 1)
        wip = np.stack([tile_lhs(np.ascontiguousarray(wip_flat[i]))
                        for i in range(NL)])
        wop = np.stack([tile_lhs(np.ascontiguousarray(wop_flat[i]))
                        for i in range(NL)])
        wxp = np.ascontiguousarray(xpw[:, :, sl].transpose(0, 2, 1))
        wdt = np.ascontiguousarray(dtw[:, sl, :].transpose(0, 2, 1))

        sm = np.zeros((128, SM_COLS), f32)
        for i in range(NL):
            for kk in range(DT8):
                sm[:, SM_OFF["nw"] + i * DT8 + kk] = nw[i, kk * 128 + p]
        for kk in range(DT8):
            sm[:, SM_OFF["nf"] + kk] = nf[kk * 128 + p]
        for i in range(NL):
            for et in range(ET8):
                eidx = k * EH + et * 128 + p
                for n in range(NST):
                    sm[:, SM_OFF["alog"] + i * 128 + et * NST + n] = \
                        alog[i, eidx, n]
                sm[:, SM_OFF["dp"] + i * ET8 + et] = dpar[i, eidx]
                sm[:, SM_OFF["cb"] + i * ET8 + et] = cb[i, eidx]
                sm[:, SM_OFF["dtb"] + i * ET8 + et] = dtb[i, eidx]
                for kc in range(KC):
                    sm[:, SM_OFF["cw"] + i * 32 + et * 4 + kc] = \
                        cw[i, eidx, kc]
        for m in range(MF):
            sm[:, SM_OFF["s1"] + m] = s1[m * 128 + p]
            sm[:, SM_OFF["s2"] + m] = s2[m * 128 + p]
            sm[:, SM_OFF["sh1"] + m] = sh1[m * 128 + p]
            sm[:, SM_OFF["sh2"] + m] = sh2[m * 128 + p]
        for m in range(DT8):
            sm[:, SM_OFF["s3"] + m] = s3[m * 128 + p]
        sm[:, SM_OFF["c_eps"]] = EPS
        sm[:, SM_OFF["c_mhalf"]] = -0.5

        in_maps.append(dict(
            x0=np.ascontiguousarray(x[b].T),
            wip=wip, wxp=wxp, wdt=wdt, wop=wop,
            w1t=w1t, w2t=w2t, h1t=h1t, h2t=h2t, w3t=w3t,
            smalls=sm, sel32=sel,
            tok0=np.array([[k * 2048]], np.int32),
        ))
    return in_maps


def kernel(**inputs):
    in_maps = make_inmaps(inputs)
    nc = build_program()
    res = run_bass_kernel_spmd(nc, in_maps, core_ids=list(range(8)))
    out = np.empty((B, L, D), np.float32)
    for core in range(8):
        b, k = core // 2, core % 2
        r = np.asarray(res.results[core]["out_c"], np.float32)
        out[b, k * 2048:(k + 1) * 2048, :] = r.T
    return out


# version 11
# speedup vs baseline: 1.1972x; 1.0567x over previous
import numpy as np
import concourse.bass as bass
import concourse.mybir as mybir
from concourse.bass_utils import run_bass_kernel_spmd
from contextlib import ExitStack

F32 = mybir.dt.float32
BF16 = mybir.dt.bfloat16
I32 = mybir.dt.int32
AF = mybir.ActivationFunctionType
ALU = mybir.AluOpType

B, L, D, E, EH, NST, RDT, KC = 4, 4096, 1024, 2048, 1024, 16, 64, 4
NL = 4
TC = 512
NCH = L // TC
DT8 = D // 128
ET8 = EH // 128
TF = 1024
EPS = 1e-5
HFF = 4096
MF = HFF // 128
RG_PAIRS = [[0, 1], [2, 3], [4, 5], [6, 7]]

QUEUES = ("pe", "act", "dve", "sync", "gp")
NDS = {"sync": 8, "gp": 8, "act": 6}
NCCS = 4
SEMKEYS = (("pe.x", "act.x", "dve.x", "gp.x")
           + tuple(f"sync.d{i}" for i in range(NDS["sync"]))
           + tuple(f"gp.d{i}" for i in range(NDS["gp"]))
           + tuple(f"act.d{i}" for i in range(NDS["act"]))
           + tuple(f"gp.cc{i}" for i in range(NCCS)))
KLASS_INC = {"x": 1, "d": 16, "cc": 1}


def _sm_layout():
    off, col = {}, 0

    def take(name, n):
        nonlocal col
        off[name] = col
        col += n
    take("nw", NL * DT8)
    take("nf", DT8)
    take("alog", NL * 128)
    take("dp", NL * ET8)
    take("cb", NL * ET8)
    take("dtb", NL * ET8)
    take("cw", NL * ET8 * KC)
    take("s1", MF)
    take("s2", MF)
    take("sh1", MF)
    take("sh2", MF)
    take("s3", DT8)
    take("c_eps", 1)
    take("c_mhalf", 1)
    return off, col


SM_OFF, SM_COLS = _sm_layout()


class Buf:
    __slots__ = ("w", "r")

    def __init__(self, w=None):
        self.w = dict(w) if w else {}
        self.r = {}


class Sched:
    def __init__(self):
        self.ops = {q: [] for q in QUEUES}
        self.count = {s: 0 for s in SEMKEYS}
        self.wait_cache = {q: {} for q in QUEUES}
        self.rr = {}

    def op(self, queue, fn, reads=(), writes=(), deps=(), klass="x",
           silent=False):
        if klass == "x" or silent:
            semkey = f"{queue}.x"
        else:
            n_pool = NDS[queue] if klass == "d" else NCCS
            idx = self.rr.get((queue, klass), 0)
            self.rr[(queue, klass)] = idx + 1
            semkey = f"{queue}.{klass}{idx % n_pool}"
        assert silent or semkey in SEMKEYS, semkey
        d = {}
        if klass in ("d", "cc") and not silent and self.count[semkey] > 0:
            d[semkey] = self.count[semkey]

        def add(tok):
            for s2, c2 in tok.items():
                if (queue == "pe" and s2 == "pe.x"):
                    continue
                if c2 > d.get(s2, 0):
                    d[s2] = c2
        for b in reads:
            add(b.w)
        for b in writes:
            add(b.w)
            add(b.r)
        for tok in deps:
            add(tok)
        wc = self.wait_cache[queue]
        waits = {}
        for s2, c2 in d.items():
            if c2 > wc.get(s2, 0):
                waits[s2] = c2
                wc[s2] = c2
        if silent:
            self.ops[queue].append((waits, fn, None, 0))
            return {}
        inc = KLASS_INC[klass]
        self.count[semkey] += inc
        tokv = self.count[semkey]
        self.ops[queue].append((waits, fn, semkey, inc))
        tok = {semkey: tokv}
        for b in reads:
            b.r[semkey] = max(b.r.get(semkey, 0), tokv)
        for b in writes:
            b.w[semkey] = tokv
        return tok


class SBuf:

    def __init__(self, tensor, lo, n, dtype, init_tok=None):
        self.t = tensor
        self.lo = lo
        self.n = n
        self.dt = dtype
        self.buf = Buf(init_tok)

    def ap(self, a=0, b=None):
        b = self.n if b is None else b
        if self.dt == BF16:
            return self.t[:, self.lo + a:self.lo + b]
        return self.t[:, self.lo + 2 * a:self.lo + 2 * b].bitcast(F32)

    def rows(self, r0, r1, a=0, b=None):
        b = self.n if b is None else b
        if self.dt == BF16:
            return self.t[r0:r1, self.lo + a:self.lo + b]
        return self.t[r0:r1, self.lo + 2 * a:self.lo + 2 * b].bitcast(F32)

    def v3(self, inner):
        return self.ap().rearrange("p (o i) -> p o i", i=inner)


class Arena:
    def __init__(self, tensor, n_units):
        self.t = tensor
        self.n = n_units
        self.pos = 0
        self.marks = []

    def alloc(self, n, dtype=BF16, init_tok=None):
        units = n * (2 if dtype == F32 else 1)
        self.pos = (self.pos + 1) & ~1
        assert self.pos + units <= self.n, ("arena", self.pos, units, self.n)
        lo = self.pos
        self.pos += units
        return SBuf(self.t, lo, n, dtype, init_tok)

    def alias(self, base, n, dtype, off_units=0):
        s = SBuf(self.t, base.lo + off_units, n, dtype)
        s.buf = base.buf
        return s

    def mark(self):
        self.marks.append(self.pos)

    def reset_to_mark(self):
        self.pos = self.marks[-1]


def build_program(n_layers=NL, n_chunks=NCH, do_ffn=True, tap_list=(), defer=True, opev_sep=False, revert=()):
    nc = bass.Bass("TRN2", target_bir_lowering=False)
    S = Sched()

    def din(name, shape, dt=F32):
        return nc.dram_tensor(name, shape, dt, kind="ExternalInput")

    x0_t = din("x0", [D, L])
    wip_t = din("wip", [NL, 16, 128, DT8 * 128])
    wxp_t = din("wxp", [NL, EH, 96])
    wdt_t = din("wdt", [NL, RDT, EH])
    wop_t = din("wop", [NL, DT8, 128, ET8 * 128])
    w1_t = din("w1t", [MF, 128, DT8 * 128])
    w2_t = din("w2t", [MF, 128, DT8 * 128])
    h1_t = din("h1t", [MF, 128, MF * 128])
    h2_t = din("h2t", [MF, 128, MF * 128])
    w3_t = din("w3t", [DT8, 128, MF * 128])
    smalls_t = din("smalls", [128, SM_COLS])
    sel_in_t = din("sel32", [32, 32 * 128])
    tok_in_t = din("tok0", [1, 1], I32)
    out_t = nc.dram_tensor("out_c", [D, L // 2], F32, kind="ExternalOutput")

    x0_in = x0_t.ap()
    wip_in, wxp_in = wip_t.ap(), wxp_t.ap()
    wdt_in, wop_in = wdt_t.ap(), wop_t.ap()
    w1_in, w2_in = w1_t.ap(), w2_t.ap()
    h1_in, h2_in, w3_in = h1_t.ap(), h2_t.ap(), w3_t.ap()
    out_ext = out_t.ap()

    wipb_t = nc.dram_tensor("wipb_d", [NL, 16, 128, DT8 * 128], BF16,
                            kind="Internal")
    wopb_t = nc.dram_tensor("wopb_d", [NL, DT8, 128, ET8 * 128], BF16,
                            kind="Internal")
    w1b_t = nc.dram_tensor("w1b_d", [MF, 128, DT8 * 128], BF16, kind="Internal")
    w2b_t = nc.dram_tensor("w2b_d", [MF, 128, DT8 * 128], BF16, kind="Internal")
    h1b_t = nc.dram_tensor("h1b_d", [MF, 128, MF * 128], BF16, kind="Internal")
    h2b_t = nc.dram_tensor("h2b_d", [MF, 128, MF * 128], BF16, kind="Internal")
    w3b_t = nc.dram_tensor("w3b_d", [DT8, 128, MF * 128], BF16, kind="Internal")
    wipb_in, wopb_in = wipb_t.ap(), wopb_t.ap()
    w1b_in, w2b_in = w1b_t.ap(), w2b_t.ap()
    h1b_in, h2b_in, w3b_in = h1b_t.ap(), h2b_t.ap(), w3b_t.ap()
    wb_bufs = {"wip": [Buf() for _ in range(NL)],
               "wop": [Buf() for _ in range(NL)],
               "w1": Buf(), "w2": Buf(), "h1": Buf(), "h2": Buf(),
               "w3": Buf()}
    resid_t = nc.dram_tensor("resid_d", [D, L], F32, kind="Internal")
    x1_dt = nc.dram_tensor("x1_d", [D, L], F32, kind="Internal")
    xn2_dt = nc.dram_tensor("xn2_d", [D, L], BF16, kind="Internal")
    xsrc_t = nc.dram_tensor("xsrc_d", [NCH * 96, TC], F32, kind="Internal")
    xdst_t = nc.dram_tensor("xdst_d", [NCH * 96, TC], F32, kind="Internal")
    osrc_t = nc.dram_tensor("osrc_d", [NCH * D, TC], F32, kind="Internal")
    odst_t = nc.dram_tensor("odst_d", [NCH * D, TC], F32, kind="Internal")
    resid_d, x1_d, xn2_d = resid_t.ap(), x1_dt.ap(), xn2_dt.ap()
    xsrc_d, xdst_d = xsrc_t.ap(), xdst_t.ap()
    osrc_d, odst_d = osrc_t.ap(), odst_t.ap()

    db = {}
    for t in ("resid", "x1", "xn2", "xsrc", "xdst", "osrc", "odst"):
        db[t] = [Buf() for _ in range(NCH)]
    db["out"] = Buf()

    with ExitStack() as st:
        NA = 106400
        ar_t = st.enter_context(nc.sbuf_tensor("ar", [128, NA], BF16))
        tok_t = st.enter_context(nc.sbuf_tensor("tok_t", [1, 1], I32))
        psum = [st.enter_context(nc.psum_tensor(f"ps{j}", [128, 512], F32))
                for j in range(8)]
        sems = {s: st.enter_context(nc.semaphore("sem_" + s.replace(".", "_")))
                for s in SEMKEYS}
        r_tok = st.enter_context(nc.sync.register("r_tok"))
        r_off = st.enter_context(nc.sync.register("r_off"))
        r_off2 = st.enter_context(nc.sync.register("r_off2"))

        A = Arena(ar_t, NA)
        psb = [Buf() for _ in range(8)]
        tok_b = Buf()
        OP = S.op

        def dma(queue, out_ap, in_ap, reads=(), writes=(), deps=()):
            def fn(eng, o=out_ap, i=in_ap):
                return eng.dma_start(out=o, in_=i)
            return OP(queue, fn, reads=reads, writes=writes, deps=deps,
                      klass="d")

        ones_st = A.alloc(128)
        state = A.alloc(NL * 128)
        tails = A.alloc(NL * ET8 * 3)
        smalls = A.alloc(SM_COLS, F32)
        A_sb = A.alloc(NL * 128, F32)
        wstgA = A.alloc(512, F32)
        wstgB = A.alloc(512, F32)

        def smc(col):
            return smalls.ap(col, col + 1)

        nc.const_aps.register(F32, EPS, smc(SM_OFF["c_eps"]))
        nc.const_aps.register(F32, -0.5, smc(SM_OFF["c_mhalf"]))

        A.mark()

        wipT = [A.alloc(1024), A.alloc(1024)]
        wopT = [A.alloc(1024), A.alloc(1024)]
        w_xp = A.alloc(ET8 * 96)
        w_dt = A.alloc(1024)
        hn_bf = A.alloc(DT8 * TC)
        y_g = A.alloc(ET8 * TC)
        u_bf2 = [A.alloc(ET8 * TC), A.alloc(ET8 * TC)]
        zs_bf2 = [A.alloc(ET8 * TC), A.alloc(ET8 * TC)]
        a_bufs = [A.alloc(TC) for _ in range(4)]
        w_buf = A.alloc(TC)
        b_strip = A.alloc(NST * TC)
        xi_ext = A.alloc(ET8 * (TC + 3))
        xi_bufs = [Buf() for _ in range(ET8)]
        cv = A.alloc(4 * TC)
        stat_bf = A.alias(cv, 2 * TC, BF16)
        h_strip = A.alloc(NST * TC)
        op_ev = (None if opev_sep
                 else A.alias(h_strip, DT8 * TC, F32))
        Bst = A.alloc(NST * TC)
        h_in = A.alias(Bst, DT8 * TC, F32)
        Cst = A.alloc(NST * TC)
        dtin_bf = A.alloc(TC)
        sel = A.alloc(32 * 128)
        bc = (A.alloc(TC, F32) if "bc" in revert else A.alias(wstgA, TC, F32))
        bcb = A.alloc(TC)
        Rc = A.alloc(DT8 * TC, F32)
        if opev_sep:
            op_ev = A.alias(Rc, DT8 * TC, F32)
        dt_et2 = [A.alloc(TC, F32), A.alloc(TC, F32)]
        sp2 = [A.alloc(2 * TC, F32), A.alloc(2 * TC, F32)]
        y_f2 = A.alloc(TC, F32)
        y_f = A.alloc(TC, F32)
        xp_ev = A.alias(y_f, TC, F32)
        m_rep = A.alloc(TC, F32)
        v_tmp = A.alloc(TC, F32)
        rstd = A.alloc(TC, F32)
        r1 = A.alloc(TC, F32)


        def emit_stats_ms(slices, stat_rg, r1_rg, rstd_rg):
            pb, pbb = psum[4], psb[4]
            nt = len(slices)
            for kk, (ap, rb) in enumerate(slices):
                half = kk % 2
                sl = stat_rg.ap(half * TC, (half + 1) * TC)

                def fsq(e, d=sl, s=ap):
                    return e.activation(d, s, AF.Square)
                OP("act", fsq, reads=[rb], writes=[stat_rg.buf])

                def fmm(e, s=sl, first=(kk == 0), last=(kk == nt - 1)):
                    return e.matmul(pb[:, :], ones_st.ap(), s,
                                    start=first, stop=last)
                OP("pe", fmm, reads=[stat_rg.buf, ones_st.buf], writes=[pbb])

            def fln(e):
                return e.activation(r1_rg.ap(), pb[:, :], AF.Ln, bias=EPS)
            OP("act", fln, reads=[pbb, smalls.buf], writes=[r1_rg.buf])

            def fex(e):
                return e.activation(rstd_rg.ap(), r1_rg.ap(), AF.Exp,
                                    scale=-0.5)
            OP("act", fex, reads=[r1_rg.buf, smalls.buf], writes=[rstd_rg.buf])

        def emit_stats_mean_var(slices, stat_rg, m_rg, v_rg, r1_rg, rstd_rg):
            pbm, pbs = psum[4], psum[5]
            pbmb, pbsb = psb[4], psb[5]
            nt = len(slices)
            for kk, (ap, rb) in enumerate(slices):
                half = kk % 2
                sl = stat_rg.ap(half * TC, (half + 1) * TC)

                def fcp(e, d=sl, s=ap):
                    return e.copy(d, s)
                OP("act", fcp, reads=[rb], writes=[stat_rg.buf])

                def fmm(e, s=sl, first=(kk == 0), last=(kk == nt - 1)):
                    return e.matmul(pbm[:, :], ones_st.ap(), s,
                                    start=first, stop=last)
                OP("pe", fmm, reads=[stat_rg.buf, ones_st.buf], writes=[pbmb])
            for kk, (ap, rb) in enumerate(slices):
                half = kk % 2
                sl = stat_rg.ap(half * TC, (half + 1) * TC)

                def fsq(e, d=sl, s=ap):
                    return e.activation(d, s, AF.Square)
                OP("act", fsq, reads=[rb], writes=[stat_rg.buf])

                def fmm(e, s=sl, first=(kk == 0), last=(kk == nt - 1)):
                    return e.matmul(pbs[:, :], ones_st.ap(), s,
                                    start=first, stop=last)
                OP("pe", fmm, reads=[stat_rg.buf, ones_st.buf], writes=[pbsb])
            OP("act", lambda e: e.copy(m_rg.ap(), pbm[:, :]),
               reads=[pbmb], writes=[m_rg.buf])
            OP("dve", lambda e: e.tensor_mul(v_rg.ap(), m_rg.ap(), m_rg.ap()),
               reads=[m_rg.buf], writes=[v_rg.buf])
            OP("dve", lambda e: e.tensor_sub(v_rg.ap(), pbs[:, :], v_rg.ap()),
               reads=[pbsb, v_rg.buf], writes=[v_rg.buf])
            OP("act", lambda e: e.activation(r1_rg.ap(), v_rg.ap(), AF.Ln,
                                             bias=EPS),
               reads=[v_rg.buf, smalls.buf], writes=[r1_rg.buf])
            OP("act", lambda e: e.activation(rstd_rg.ap(), r1_rg.ap(), AF.Exp,
                                             scale=-0.5),
               reads=[r1_rg.buf, smalls.buf], writes=[rstd_rg.buf])

        def cast_load(queue, stage, dst, dst_lo, src_ap, n_cols, rows=128):
            for j0 in range(0, n_cols, 512):
                w = min(512, n_cols - j0)
                stg_s = wstgA if "stage" in revert else                     (wstgA, wstgB)[(j0 // 512) % 2]
                stg = stg_s.rows(0, rows, 0, w)
                dma(queue, stg, src_ap[:, j0:j0 + w], writes=[stg_s.buf])

                def fcv(e, d=dst.rows(0, rows, dst_lo + j0, dst_lo + j0 + w),
                        s=stg):
                    return e.copy(d, s)
                OP("act", fcv, reads=[stg_s.buf], writes=[dst.buf])

        wl_state = {"i": 0, "stages": None}

        def wtile_load(src_tile_bf, n_k, dst, key):
            i = wl_state["i"]
            wl_state["i"] = i + 1
            q = ("sync", "gp")[i % 2]
            dma(q, dst.ap(0, n_k * 128), src_tile_bf,
                deps=[wb_bufs[key].w], writes=[dst.buf])

        prepS = [A.alloc(512, F32), A.alloc(512, F32)]
        prepB = [A.alloc(512), A.alloc(512)]
        prep_i = [0]

        def emit_prep(src_f32, dst_bf, dst_buf, n_cols, q_in="act",
                      q_out="gp"):
            for j0 in range(0, n_cols, 512):
                w = min(512, n_cols - j0)
                k = prep_i[0]
                prep_i[0] += 1
                ps_, pb_ = prepS[k % 2], prepB[k % 2]
                dma(q_in, ps_.ap(0, w), src_f32[:, j0:j0 + w],
                    writes=[ps_.buf])

                def fc(e, d=pb_.ap(0, w), s=ps_.ap(0, w)):
                    return e.copy(d, s)
                OP("act", fc, reads=[ps_.buf], writes=[pb_.buf])
                dma(q_out, dst_bf[:, j0:j0 + w], pb_.ap(0, w),
                    reads=[pb_.buf], writes=[dst_buf if isinstance(dst_buf, Buf)
                                             else dst_buf])

        def layer_w_items(i):
            items = []
            for m in range(16):
                items.append((wip_in[i, m, :, :], wipb_in[i, m, :, :],
                              ("wip", i), DT8 * 128))
            for m in range(DT8):
                items.append((wop_in[i, m, :, :], wopb_in[i, m, :, :],
                              ("wop", i), DT8 * 128))
            return items

        def wbuf_of(key):
            if isinstance(key, tuple):
                return wb_bufs[key[0]][key[1]]
            return wb_bufs[key]

        def emit_layer_weights(i):
            for kk in range(ET8):
                cast_load("sync", None, w_xp, kk * 96,
                          wxp_in[i, kk * 128:(kk + 1) * 128, :], 96)
            cast_load("sync", None, w_dt, 0, wdt_in[i, :, :], 1024, rows=RDT)
            if i == 0:
                for it in layer_w_items(0):
                    emit_prep(it[0], it[1], wbuf_of(it[2]), it[3])
            if i + 1 < n_layers:
                prep_work[0:0] = layer_w_items(i + 1)

        taps = {}

        def tap_dump(name, region):
            ap_out = nc.dram_tensor("tap_" + name, [128, region.n], region.dt,
                                    kind="ExternalOutput").ap()
            tb = Buf()
            dma("sync", ap_out, region.ap(), reads=[region.buf], writes=[tb])
            taps[name] = tb

        def emit_front_p1(i, c):
            t0 = c * TC
            r3 = Rc.v3(TC)
            rsl = resid_d[:, t0:t0 + TC].rearrange("(k p) t -> p k t", p=128)
            if i == 0:
                dma("sync", r3,
                    x0_in[:, t0:t0 + TC].rearrange("(k p) t -> p k t", p=128),
                    writes=[Rc.buf])
                emit_stats_mean_var(
                    [(Rc.ap(kk * TC, (kk + 1) * TC), Rc.buf)
                     for kk in range(DT8)], stat_bf, m_rep, v_tmp, r1, rstd)
                for kk in range(DT8):
                    sl = Rc.ap(kk * TC, (kk + 1) * TC)

                    def f1(e, s=sl):
                        return e.tensor_sub(s, s, m_rep.ap())
                    OP("dve", f1, reads=[Rc.buf, m_rep.buf], writes=[Rc.buf])

                    def f2(e, s=sl):
                        return e.tensor_mul(s, s, rstd.ap())
                    OP("dve", f2, reads=[Rc.buf, rstd.buf], writes=[Rc.buf])
                dma("sync", rsl, r3, reads=[Rc.buf], writes=[db["resid"][c]])
            else:
                dma("sync", r3, rsl, reads=[db["resid"][c]], writes=[Rc.buf])
                hsl = odst_d[c * D:(c + 1) * D, :].rearrange(
                    "(k p) t -> p k t", p=128)
                dma("sync", h_in.v3(TC), hsl, reads=[db["odst"][c]],
                    writes=[h_in.buf])
                OP("dve", lambda e: e.tensor_add(Rc.ap(), Rc.ap(), h_in.ap()),
                   reads=[Rc.buf, h_in.buf], writes=[Rc.buf])
                dma("sync", rsl, r3, reads=[Rc.buf], writes=[db["resid"][c]])
            emit_stats_ms(
                [(Rc.ap(kk * TC, (kk + 1) * TC), Rc.buf) for kk in range(DT8)],
                stat_bf, r1, rstd)
            for kk in range(DT8):
                def fhn(e, d=hn_bf.ap(kk * TC, (kk + 1) * TC),
                        s=Rc.ap(kk * TC, (kk + 1) * TC),
                        sc=smc(SM_OFF["nw"] + i * DT8 + kk)):
                    return e.scalar_tensor_tensor(d, s, sc, rstd.ap(),
                                                  ALU.mult, ALU.mult)
                OP("dve", fhn, reads=[Rc.buf, smalls.buf, rstd.buf],
                   writes=[hn_bf.buf])

        def emit_front_p2(i, c):
            u_bf = u_bf2[c % 2]
            zs_bf = zs_bf2[c % 2]
            for et in range(ET8):
                base = et * (TC + 3)
                tl = (i * ET8 + et) * 3

                def ftl(e, d=xi_ext.ap(base, base + 3),
                        s=tails.ap(tl, tl + 3)):
                    return e.copy(d, s)
                OP("act", ftl, reads=[tails.buf], writes=[xi_bufs[et]])

            for m in range(16):
                pb, pbb = psum[m % 4], psb[m % 4]
                wt_ = wipT[m % 2]
                dma("sync", wt_.ap(), wipb_in[i, m, :, :],
                    deps=[wb_bufs["wip"][i].w], writes=[wt_.buf])

                def fmm(e, pb=pb, wt_=wt_):
                    last = None
                    for kk in range(DT8):
                        last = e.matmul(
                            pb[:, :],
                            wt_.ap(kk * 128, (kk + 1) * 128),
                            hn_bf.ap(kk * TC, (kk + 1) * TC),
                            start=(kk == 0), stop=(kk == DT8 - 1))
                    return last
                OP("pe", fmm, reads=[wt_.buf, hn_bf.buf], writes=[pbb])
                if m < ET8:
                    def fev(e, d=xi_ext.ap(m * (TC + 3) + 3,
                                           (m + 1) * (TC + 3)), pb=pb):
                        return e.copy(d, pb[:, :])
                    OP("act", fev, reads=[pbb], writes=[xi_bufs[m]])
                else:
                    def fev(e, d=zs_bf.ap((m - ET8) * TC, (m - ET8 + 1) * TC),
                            pb=pb):
                        return e.activation(d, pb[:, :], AF.Silu)
                    OP("act", fev, reads=[pbb], writes=[zs_bf.buf])

            for et in range(ET8):
                base = et * (TC + 3)
                tl = (i * ET8 + et) * 3
                for k in range(KC):
                    sc = smc(SM_OFF["cw"] + i * 32 + et * 4 + k)
                    src = xi_ext.ap(base + k, base + k + TC)
                    if k == 0:
                        def ftap(e, d=cv.ap(0, TC), s=src, sc=sc):
                            return e.tensor_scalar(d, s, sc, None, ALU.mult)
                        OP("dve", ftap, reads=[xi_bufs[et], smalls.buf],
                           writes=[cv.buf])
                    else:
                        def ftap(e, d=cv.ap((k % 2) * TC, (k % 2 + 1) * TC),
                                 s=src, sc=sc,
                                 prev=cv.ap(((k - 1) % 2) * TC,
                                            ((k - 1) % 2 + 1) * TC)):
                            return e.scalar_tensor_tensor(d, s, sc, prev,
                                                          ALU.mult, ALU.add)
                        OP("dve", ftap, reads=[xi_bufs[et], smalls.buf,
                                               cv.buf], writes=[cv.buf])

                def fu(e, d=u_bf.ap(et * TC, (et + 1) * TC),
                       s=cv.ap(TC, 2 * TC),
                       bias=smc(SM_OFF["cb"] + i * ET8 + et)):
                    return e.activation(d, s, AF.Silu, bias=bias)
                OP("act", fu, reads=[cv.buf, smalls.buf], writes=[u_bf.buf])

                def fsv(e, d=tails.ap(tl, tl + 3),
                        s=xi_ext.ap(base + TC, base + TC + 3)):
                    return e.copy(d, s)
                OP("act", fsv, reads=[xi_bufs[et]], writes=[tails.buf])

            pb4, pbb4 = psum[4], psb[4]

            def fxp(e):
                last = None
                for kk in range(ET8):
                    last = e.matmul(pb4[0:96, :],
                                    w_xp.ap(kk * 96, (kk + 1) * 96),
                                    u_bf.ap(kk * TC, (kk + 1) * TC),
                                    start=(kk == 0), stop=(kk == ET8 - 1))
                return last
            OP("pe", fxp, reads=[w_xp.buf, u_bf.buf], writes=[pbb4])
            OP("act", lambda e: e.copy(xp_ev.rows(0, 96), pb4[0:96, :]),
               reads=[pbb4], writes=[xp_ev.buf])
            dma("gp", xsrc_d[c * 96:(c + 1) * 96, :], xp_ev.rows(0, 96),
                reads=[xp_ev.buf], writes=[db["xsrc"][c]])

            def fcc(e, c=c):
                return e.collective_compute(
                    "AllReduce", ALU.add, replica_groups=RG_PAIRS,
                    ins=[xsrc_d[c * 96:(c + 1) * 96, :].opt()],
                    outs=[xdst_d[c * 96:(c + 1) * 96, :].opt()])
            OP("gp", fcc, reads=[db["xsrc"][c]], writes=[db["xdst"][c]],
               klass="cc")

        def emit_dt_chain(i, et):
            dt_et = dt_et2[et % 2]
            sp = sp2[et % 2]
            pb5, pbb5 = psum[5], psb[5]

            def fdt(e, et=et):
                return e.matmul(pb5[:, :],
                                w_dt.rows(0, RDT, et * 128, (et + 1) * 128),
                                dtin_bf.rows(0, RDT),
                                start=True, stop=True)
            OP("pe", fdt, reads=[w_dt.buf, dtin_bf.buf], writes=[pbb5])
            dtb_sc = smc(SM_OFF["dtb"] + i * ET8 + et)

            def f_abs(e, sc=dtb_sc, sp=sp):
                return e.activation(sp.ap(0, TC), pb5[:, :], AF.Abs, bias=sc)
            OP("act", f_abs, reads=[pbb5, smalls.buf], writes=[sp.buf])

            def f_exp(e, sp=sp):
                return e.activation(sp.ap(TC, 2 * TC), sp.ap(0, TC),
                                    AF.Exp, scale=-1.0)
            OP("act", f_exp, reads=[sp.buf], writes=[sp.buf])

            def f_ln(e, sp=sp):
                return e.activation(sp.ap(0, TC), sp.ap(TC, 2 * TC),
                                    AF.Ln, bias=1.0)
            OP("act", f_ln, reads=[sp.buf], writes=[sp.buf])

            def f_rel(e, sc=dtb_sc, sp=sp):
                return e.activation(sp.ap(TC, 2 * TC), pb5[:, :],
                                    AF.Relu, bias=sc)
            OP("act", f_rel, reads=[pbb5, smalls.buf], writes=[sp.buf])

            def f_dt(e, sp=sp, dt_et=dt_et):
                return e.tensor_add(dt_et.ap(), sp.ap(0, TC),
                                    sp.ap(TC, 2 * TC))
            OP("dve", f_dt, reads=[sp.buf], writes=[dt_et.buf])

        def emit_scan(i, c, mid_cb=None):
            u_bf = u_bf2[c % 2]
            zs_bf = zs_bf2[c % 2]
            dma("gp", dtin_bf.rows(0, RDT), xdst_d[c * 96:c * 96 + RDT, :],
                reads=[db["xdst"][c]], writes=[dtin_bf.buf])
            dma("gp", bc.rows(0, 32), xdst_d[c * 96 + RDT:(c + 1) * 96, :],
                reads=[db["xdst"][c]], writes=[bc.buf])
            OP("act", lambda e: e.copy(bcb.rows(0, 32), bc.rows(0, 32)),
               reads=[bc.buf], writes=[bcb.buf])

            for n in range(NST):
                jB = 4 + (n % 2) * 2
                jC = 5 + (n % 2) * 2

                def fb(e, n=n, pb=psum[jB]):
                    return e.matmul(pb[:, :],
                                    sel.rows(0, 32, n * 128, (n + 1) * 128),
                                    bcb.rows(0, 32), start=True, stop=True)
                OP("pe", fb, reads=[sel.buf, bcb.buf], writes=[psb[jB]])

                def fbe(e, d=Bst.ap(n * TC, (n + 1) * TC), pb=psum[jB]):
                    return e.copy(d, pb[:, :])
                OP("act", fbe, reads=[psb[jB]], writes=[Bst.buf])

                def fc2(e, n=n, pb=psum[jC]):
                    return e.matmul(
                        pb[:, :],
                        sel.rows(0, 32, (NST + n) * 128, (NST + n + 1) * 128),
                        bcb.rows(0, 32), start=True, stop=True)
                OP("pe", fc2, reads=[sel.buf, bcb.buf], writes=[psb[jC]])

                def fce(e, d=Cst.ap(n * TC, (n + 1) * TC), pb=psum[jC]):
                    return e.copy(d, pb[:, :])
                OP("act", fce, reads=[psb[jC]], writes=[Cst.buf])

            for et in range(ET8):
                dt_et = dt_et2[et % 2]
                if et == 0:
                    emit_dt_chain(i, 0)
                if et + 1 < ET8:
                    emit_dt_chain(i, et + 1)

                def fw(e, et=et, dt_et=dt_et):
                    return e.tensor_mul(w_buf.ap(), dt_et.ap(),
                                        u_bf.ap(et * TC, (et + 1) * TC))
                OP("dve", fw, reads=[dt_et.buf, u_bf.buf], writes=[w_buf.buf])

                def fb2(e):
                    return e.tensor_tensor(
                        b_strip.v3(TC),
                        w_buf.ap().unsqueeze(1).broadcast_to([128, NST, TC]),
                        Bst.v3(TC), ALU.mult)
                OP("dve", fb2, reads=[w_buf.buf, Bst.buf],
                   writes=[b_strip.buf])
                for n in range(NST):
                    acol = A_sb.ap(i * 128 + et * NST + n,
                                   i * 128 + et * NST + n + 1)
                    ab_s = a_bufs[n % 4]
                    ab = ab_s.ap()

                    def fe(e, ab=ab, acol=acol, dt_et=dt_et):
                        return e.activation(ab, dt_et.ap(), AF.Exp, scale=acol)
                    OP("act", fe, reads=[dt_et.buf, A_sb.buf],
                       writes=[ab_s.buf])
                    stcol = state.ap(i * 128 + et * NST + n,
                                     i * 128 + et * NST + n + 1)

                    def fs(e, n=n, ab=ab, stcol=stcol, ab_s=ab_s):
                        return e.tensor_tensor_scan(
                            h_strip.ap(n * TC, (n + 1) * TC), ab,
                            b_strip.ap(n * TC, (n + 1) * TC), stcol,
                            ALU.mult, ALU.add)
                    OP("dve", fs, reads=[ab_s.buf, b_strip.buf, state.buf],
                       writes=[h_strip.buf])

                def fst(e, et=et):
                    v = h_strip.v3(TC)[:, :, TC - 1:TC]
                    return e.tensor_copy(
                        state.ap(i * 128 + et * NST, i * 128 + (et + 1) * NST),
                        v)
                OP("dve", fst, reads=[h_strip.buf], writes=[state.buf])

                def fq(e):
                    return e.tensor_mul(b_strip.ap(), h_strip.ap(), Cst.ap())
                OP("dve", fq, reads=[h_strip.buf, Cst.buf],
                   writes=[b_strip.buf])
                qv = b_strip.ap().rearrange("p (n two t) -> p n two t",
                                            two=2, t=TC)

                def ftr1(e, qv=qv):
                    return e.tensor_tensor(h_strip.v3(TC)[:, 0:8, :],
                                           qv[:, :, 0, :], qv[:, :, 1, :],
                                           ALU.add)
                OP("dve", ftr1, reads=[b_strip.buf], writes=[h_strip.buf])
                hv = h_strip.ap().rearrange("p (g two t) -> p g two t",
                                            two=2, t=TC)

                def ftr2(e, hv=hv):
                    return e.tensor_tensor(h_strip.v3(TC)[:, 8:12, :],
                                           hv[:, 0:4, 0, :], hv[:, 0:4, 1, :],
                                           ALU.add)
                OP("dve", ftr2, reads=[h_strip.buf], writes=[h_strip.buf])

                def ftr3(e, hv=hv):
                    return e.tensor_tensor(h_strip.v3(TC)[:, 12:14, :],
                                           hv[:, 4:6, 0, :], hv[:, 4:6, 1, :],
                                           ALU.add)
                OP("dve", ftr3, reads=[h_strip.buf], writes=[h_strip.buf])

                def ftr4(e):
                    return e.tensor_tensor(y_f.ap(),
                                           h_strip.ap(12 * TC, 13 * TC),
                                           h_strip.ap(13 * TC, 14 * TC),
                                           ALU.add)
                OP("dve", ftr4, reads=[h_strip.buf], writes=[y_f.buf])

                def fy2(e, et=et, sc=smc(SM_OFF["dp"] + i * ET8 + et)):
                    return e.scalar_tensor_tensor(
                        y_f2.ap(), u_bf.ap(et * TC, (et + 1) * TC), sc,
                        y_f.ap(), ALU.mult, ALU.add)
                OP("dve", fy2, reads=[u_bf.buf, smalls.buf, y_f.buf],
                   writes=[y_f2.buf])

                def fyg(e, et=et):
                    return e.tensor_mul(y_g.ap(et * TC, (et + 1) * TC),
                                        y_f2.ap(),
                                        zs_bf.ap(et * TC, (et + 1) * TC))
                OP("dve", fyg, reads=[y_f2.buf, zs_bf.buf], writes=[y_g.buf])
                if et == 1 and mid_cb is not None:
                    mid_cb()

        def emit_outproj(i, c):
            for m in range(DT8):
                pb, pbb = psum[m % 4], psb[m % 4]
                wt_ = wopT[m % 2]
                dma("sync", wt_.ap(), wopb_in[i, m, :, :],
                    deps=[wb_bufs["wop"][i].w], writes=[wt_.buf])

                def fop(e, pb=pb, wt_=wt_):
                    last = None
                    for kk in range(ET8):
                        last = e.matmul(
                            pb[:, :],
                            wt_.ap(kk * 128, (kk + 1) * 128),
                            y_g.ap(kk * TC, (kk + 1) * TC),
                            start=(kk == 0), stop=(kk == ET8 - 1))
                    return last
                OP("pe", fop, reads=[wt_.buf, y_g.buf], writes=[pbb])

                def foev(e, d=op_ev.ap(m * TC, (m + 1) * TC), pb=pb):
                    return e.copy(d, pb[:, :])
                OP("act", foev, reads=[pbb], writes=[op_ev.buf])
            dma("gp", osrc_d[c * D:(c + 1) * D, :].rearrange(
                "(k p) t -> p k t", p=128), op_ev.v3(TC),
                reads=[op_ev.buf], writes=[db["osrc"][c]])

            def focc(e, c=c):
                return e.collective_compute(
                    "AllReduce", ALU.add, replica_groups=RG_PAIRS,
                    ins=[osrc_d[c * D:(c + 1) * D, :].opt()],
                    outs=[odst_d[c * D:(c + 1) * D, :].opt()])
            OP("gp", focc, reads=[db["osrc"][c]], writes=[db["odst"][c]],
               klass="cc")

        def emit_tail(c):
            t0 = c * TC
            rsl = resid_d[:, t0:t0 + TC].rearrange("(k p) t -> p k t", p=128)
            dma("sync", Rc.v3(TC), rsl, reads=[db["resid"][c]],
                writes=[Rc.buf])
            hsl = odst_d[c * D:(c + 1) * D, :].rearrange("(k p) t -> p k t",
                                                         p=128)
            dma("sync", h_in.v3(TC), hsl, reads=[db["odst"][c]],
                writes=[h_in.buf])
            OP("dve", lambda e: e.tensor_add(Rc.ap(), Rc.ap(), h_in.ap()),
               reads=[Rc.buf, h_in.buf], writes=[Rc.buf])
            emit_stats_ms(
                [(Rc.ap(kk * TC, (kk + 1) * TC), Rc.buf) for kk in range(DT8)],
                stat_bf, r1, rstd)
            for kk in range(DT8):
                def fat(e, d=h_in.ap(kk * TC, (kk + 1) * TC),
                        s=Rc.ap(kk * TC, (kk + 1) * TC),
                        sc=smc(SM_OFF["nf"] + kk)):
                    return e.scalar_tensor_tensor(d, s, sc, rstd.ap(),
                                                  ALU.mult, ALU.mult)
                OP("dve", fat, reads=[Rc.buf, smalls.buf, rstd.buf],
                   writes=[h_in.buf])
            emit_stats_mean_var(
                [(h_in.ap(kk * TC, (kk + 1) * TC), h_in.buf)
                 for kk in range(DT8)], stat_bf, m_rep, v_tmp, r1, rstd)
            dma("sync", Rc.v3(TC),
                x0_in[:, t0:t0 + TC].rearrange("(k p) t -> p k t", p=128),
                writes=[Rc.buf])
            for kk in range(DT8):
                sl = h_in.ap(kk * TC, (kk + 1) * TC)

                def f1(e, s=sl):
                    return e.tensor_sub(s, s, m_rep.ap())
                OP("dve", f1, reads=[h_in.buf, m_rep.buf], writes=[h_in.buf])

                def f2(e, s=sl):
                    return e.tensor_mul(s, s, rstd.ap())
                OP("dve", f2, reads=[h_in.buf, rstd.buf], writes=[h_in.buf])

                def f3(e, s=sl, x=Rc.ap(kk * TC, (kk + 1) * TC)):
                    return e.tensor_add(s, s, x)
                OP("dve", f3, reads=[h_in.buf, Rc.buf], writes=[h_in.buf])
            dma("sync", x1_d[:, t0:t0 + TC].rearrange("(k p) t -> p k t",
                                                      p=128),
                h_in.v3(TC), reads=[h_in.buf], writes=[db["x1"][c]])
            emit_stats_mean_var(
                [(h_in.ap(kk * TC, (kk + 1) * TC), h_in.buf)
                 for kk in range(DT8)], stat_bf, m_rep, v_tmp, r1, rstd)
            for kk in range(DT8):
                sl = h_in.ap(kk * TC, (kk + 1) * TC)

                def f4(e, s=sl):
                    return e.tensor_sub(s, s, m_rep.ap())
                OP("dve", f4, reads=[h_in.buf, m_rep.buf], writes=[h_in.buf])

                def f5(e, d=hn_bf.ap(kk * TC, (kk + 1) * TC), s=sl):
                    return e.tensor_mul(d, s, rstd.ap())
                OP("dve", f5, reads=[h_in.buf, rstd.buf], writes=[hn_bf.buf])
            dma("sync", xn2_d[:, t0:t0 + TC].rearrange("(k p) t -> p k t",
                                                       p=128),
                hn_bf.v3(TC), reads=[hn_bf.buf], writes=[db["xn2"][c]])

        def emit_ffn_phase():
            snap = {s: S.count[s] for s in SEMKEYS}
            A.reset_to_mark()

            def nb(n, dtype=BF16):
                return A.alloc(n, dtype, init_tok=snap)
            hg = nb(MF * TF)
            h1s = nb(MF * TF)
            xn2t = A.alias(h1s, DT8 * TF, BF16)
            wtA = nb(MF * 128)
            wtB = nb(MF * 128)
            g_tmp = nb(TF)
            b2_tmp = nb(TF)
            stat2 = nb(2 * TC)
            f_out = nb(DT8 * TF, F32)
            x1kk = nb(TF, F32)
            r12 = A.alias(x1kk, TC, F32)
            wsC = nb(1024, F32)
            wsD = nb(1024, F32)
            wl_state["stages"] = (wsC, wsD)
            m2 = nb(TC, F32)
            v2 = A.alias(wsC, TC, F32)
            rs2 = A.alias(stat2, TC, F32)
            xout = A.alias(hg, TF, F32)

            OP("sync", lambda e: e.reg_load(r_tok, tok_t[0:1, 0:1]),
               deps=[tok_b.w], silent=True)

            for tt in range(2048 // TF):
                def f_ro(e, k=tt * TF):
                    return e.reg_add(r_off, r_tok, k)
                OP("sync", f_ro, silent=True)

                def fxl(e):
                    src = bass.AP(xn2_dt, r_off,
                                  [[L, 128], [128 * L, DT8], [1, TF]])
                    return e.dma_start(out=xn2t.v3(TF), in_=src)
                OP("sync", fxl, writes=[xn2t.buf],
                   deps=[cb.w for cb in db["xn2"]], klass="d")

                for m in range(MF):
                    wtile_load(w1b_in[m, :, :], DT8, wtA, "w1")
                    for ntile in range(TF // 512):
                        pb, pbb = psum[ntile], psb[ntile]

                        def fm1(e, ntile=ntile, pb=pb):
                            last = None
                            for kk in range(DT8):
                                last = e.matmul(
                                    pb[:, :], wtA.ap(kk * 128, (kk + 1) * 128),
                                    xn2t.ap(kk * TF + ntile * 512,
                                            kk * TF + ntile * 512 + 512),
                                    start=(kk == 0), stop=(kk == DT8 - 1))
                            return last
                        OP("pe", fm1, reads=[wtA.buf, xn2t.buf], writes=[pbb])

                        def fe1(e, d=g_tmp.ap(ntile * 512, ntile * 512 + 512),
                                pb=pb, sc=smc(SM_OFF["s1"] + m)):
                            return e.activation(d, pb[:, :], AF.Silu, scale=sc)
                        OP("act", fe1, reads=[pbb, smalls.buf],
                           writes=[g_tmp.buf])
                    wtile_load(w2b_in[m, :, :], DT8, wtB, "w2")
                    for ntile in range(TF // 512):
                        pb, pbb = psum[2 + ntile], psb[2 + ntile]

                        def fm2(e, ntile=ntile, pb=pb):
                            last = None
                            for kk in range(DT8):
                                last = e.matmul(
                                    pb[:, :], wtB.ap(kk * 128, (kk + 1) * 128),
                                    xn2t.ap(kk * TF + ntile * 512,
                                            kk * TF + ntile * 512 + 512),
                                    start=(kk == 0), stop=(kk == DT8 - 1))
                            return last
                        OP("pe", fm2, reads=[wtB.buf, xn2t.buf], writes=[pbb])

                        def fe2(e, d=b2_tmp.ap(ntile * 512, ntile * 512 + 512),
                                pb=pb, sc=smc(SM_OFF["s2"] + m)):
                            return e.activation(d, pb[:, :], AF.Identity,
                                                scale=sc)
                        OP("act", fe2, reads=[pbb, smalls.buf],
                           writes=[b2_tmp.buf])

                    def fhg(e, d=hg.ap(m * TF, (m + 1) * TF)):
                        return e.tensor_mul(d, g_tmp.ap(), b2_tmp.ap())
                    OP("dve", fhg, reads=[g_tmp.buf, b2_tmp.buf],
                       writes=[hg.buf])

                for src_w, wkey, dst_s, scoff, act in (
                        (h1b_in, "h1", h1s, "sh1", AF.Gelu),
                        (h2b_in, "h2", hg, "sh2", AF.Gelu),
                        (w3b_in, "w3", f_out, "s3", AF.Identity)):
                    rhs_s = hg if dst_s is h1s else (h1s if dst_s is hg
                                                    else hg)
                    n_m = DT8 if dst_s is f_out else MF
                    for m in range(n_m):
                        wtm = (wtA, wtB)[m % 2]
                        wtile_load(src_w[m, :, :], MF, wtm, wkey)
                        for ntile in range(TF // 512):
                            pb, pbb = psum[ntile], psb[ntile]

                            def fm3(e, ntile=ntile, pb=pb, rhs_s=rhs_s,
                                    wtm=wtm):
                                last = None
                                for kk in range(MF):
                                    last = e.matmul(
                                        pb[:, :],
                                        wtm.ap(kk * 128, (kk + 1) * 128),
                                        rhs_s.ap(kk * TF + ntile * 512,
                                                 kk * TF + ntile * 512 + 512),
                                        start=(kk == 0), stop=(kk == MF - 1))
                                return last
                            OP("pe", fm3, reads=[wtm.buf, rhs_s.buf],
                               writes=[pbb])

                            def fe3(e, d=dst_s.ap(m * TF + ntile * 512,
                                                  m * TF + ntile * 512 + 512),
                                    pb=pb, sc=smc(SM_OFF[scoff] + m), act=act):
                                return e.activation(d, pb[:, :], act, scale=sc)
                            OP("act", fe3, reads=[pbb, smalls.buf],
                               writes=[dst_s.buf])

                for half in range(TF // 512):
                    emit_stats_mean_var(
                        [(f_out.ap(kk * TF + half * 512,
                                   kk * TF + half * 512 + 512), f_out.buf)
                         for kk in range(DT8)], stat2, m2, v2, r12, rs2)
                    for kk in range(DT8):
                        sl = f_out.ap(kk * TF + half * 512,
                                      kk * TF + half * 512 + 512)

                        def g1(e, s=sl):
                            return e.tensor_sub(s, s, m2.ap())
                        OP("dve", g1, reads=[f_out.buf, m2.buf],
                           writes=[f_out.buf])

                        def g2(e, s=sl):
                            return e.tensor_mul(s, s, rs2.ap())
                        OP("dve", g2, reads=[f_out.buf, rs2.buf],
                           writes=[f_out.buf])
                for kk in range(DT8):
                    def f_ro2(e, k=kk * 128 * L):
                        return e.reg_add(r_off2, r_off, k)
                    OP("sync", f_ro2, silent=True)

                    def fx1(e):
                        src = bass.AP(x1_dt, r_off2, [[L, 128], [1, TF]])
                        return e.dma_start(out=x1kk.ap(), in_=src)
                    OP("sync", fx1, writes=[x1kk.buf],
                       deps=[cb.w for cb in db["x1"]], klass="d")

                    def g3(e, d=xout.ap(), s=f_out.ap(kk * TF, (kk + 1) * TF)):
                        return e.tensor_add(d, s, x1kk.ap())
                    OP("dve", g3, reads=[f_out.buf, x1kk.buf],
                       writes=[xout.buf])

                    def fo(e, kk=kk, tt=tt):
                        return e.dma_start(
                            out=out_ext[kk * 128:(kk + 1) * 128,
                                        tt * TF:(tt + 1) * TF],
                            in_=xout.ap())
                    OP("sync", fo, reads=[xout.buf], writes=[db["out"]],
                       klass="d")

        dma("sync", smalls.ap(), smalls_t.ap(), writes=[smalls.buf])
        for j0 in range(0, 32 * 128, 512):
            stg_s = (wstgA, wstgB)[(j0 // 512) % 2]
            stg = stg_s.rows(0, 32, 0, 512)
            dma("sync", stg, sel_in_t.ap()[:, j0:j0 + 512],
                writes=[stg_s.buf])

            def fselc(e, d=sel.rows(0, 32, j0, j0 + 512), s=stg):
                return e.copy(d, s)
            OP("act", fselc, reads=[stg_s.buf], writes=[sel.buf])
        dma("sync", tok_t[:, :], tok_in_t.ap(), writes=[tok_b])
        OP("gp", lambda e: e.memset(ones_st.ap(), 1.0 / 1024.0),
           writes=[ones_st.buf])
        OP("gp", lambda e: e.memset(state.ap(), 0.0), writes=[state.buf])
        OP("gp", lambda e: e.memset(tails.ap(), 0.0), writes=[tails.buf])
        OP("act", lambda e: e.activation(
            A_sb.ap(), smalls.ap(SM_OFF["alog"], SM_OFF["alog"] + NL * 128),
            AF.Exp), reads=[smalls.buf], writes=[A_sb.buf])
        OP("dve", lambda e: e.tensor_scalar(A_sb.ap(), A_sb.ap(), -1.0, None,
                                            ALU.mult),
           reads=[A_sb.buf], writes=[A_sb.buf])

        prep_work = []
        for m in range(MF):
            prep_work.append((w1_in[m, :, :], w1b_in[m, :, :], "w1",
                              DT8 * 128))
            prep_work.append((w2_in[m, :, :], w2b_in[m, :, :], "w2",
                              DT8 * 128))
        for m in range(MF):
            prep_work.append((h1_in[m, :, :], h1b_in[m, :, :], "h1",
                              MF * 128))
            prep_work.append((h2_in[m, :, :], h2b_in[m, :, :], "h2",
                              MF * 128))
        for m in range(DT8):
            prep_work.append((w3_in[m, :, :], w3b_in[m, :, :], "w3",
                              MF * 128))

        def drain_prep(n_pieces):
            budget = n_pieces
            while prep_work and budget > 0:
                sf, dbf, key, ncols = prep_work.pop(0)
                emit_prep(sf, dbf, wbuf_of(key), ncols)
                budget -= (ncols + 511) // 512

        tapmap = {"hn": lambda: hn_bf, "u": lambda: u_bf,
                  "zs": lambda: zs_bf, "yg": lambda: y_g,
                  "Rc": lambda: Rc, "op": lambda: op_ev,
                  "dt": lambda: dt_et, "st": lambda: state}
        for i in range(n_layers):
            emit_layer_weights(i)
            for c in range(n_chunks):
                if c == 0:
                    emit_front_p1(i, 0)
                    emit_front_p2(i, 0)
                if c + 1 < n_chunks:
                    emit_front_p1(i, c + 1)
                if c > 0:
                    emit_outproj(i, c - 1)
                mid = ((lambda i=i, c=c: emit_front_p2(i, c + 1))
                       if c + 1 < n_chunks else None)
                emit_scan(i, c, mid)
                drain_prep(26)
                for nm, ti, tc_ in tap_list:
                    if ti == i and tc_ == c:
                        tap_dump(f"{nm}_{ti}_{tc_}", tapmap[nm]())
            emit_outproj(i, n_chunks - 1)
        if do_ffn:
            drain_prep(10**9)
            for c in range(n_chunks):
                emit_tail(c)
            emit_ffn_phase()

        final_counts = dict(S.count)
        with nc.Block() as block:
            def make_body(qname):
                ops = S.ops[qname]

                def body(eng):
                    for waits, fn, semkey, inc in ops:
                        for s2, v in sorted(waits.items()):
                            eng.wait_ge(sems[s2], v)
                        inst = fn(eng)
                        if semkey is not None:
                            inst.then_inc(sems[semkey], inc)
                    if qname == "sync":
                        for s2, v in sorted(final_counts.items()):
                            if v > 0:
                                eng.wait_ge(sems[s2], v)
                return body
            block.sync(make_body("sync"))
            block.vector(make_body("dve"))
            block.scalar(make_body("act"))
            block.tensor(make_body("pe"))
            block.gpsimd(make_body("gp"))

    return nc


def make_inmaps(inputs):
    f32 = np.float32
    x = np.asarray(inputs["L0_attn_hidden_states"], f32)
    ipw = np.asarray(inputs["L0_attn_in_proj_w"], f32)
    cw = np.asarray(inputs["L0_attn_conv_w"], f32)
    cb = np.asarray(inputs["L0_attn_conv_b"], f32)
    xpw = np.asarray(inputs["L0_attn_x_proj_w"], f32)
    dtw = np.asarray(inputs["L0_attn_dt_proj_w"], f32)
    dtb = np.asarray(inputs["L0_attn_dt_proj_b"], f32)
    alog = np.asarray(inputs["L0_attn_A_log"], f32)
    dpar = np.asarray(inputs["L0_attn_D_param"], f32)
    opw = np.asarray(inputs["L0_attn_out_proj_w"], f32)
    nw = np.asarray(inputs["L0_attn_norm_w"], f32)
    nf = np.asarray(inputs["L0_attn_norm_f_w"], f32)
    w1 = np.asarray(inputs["L0_ffn_w1_w"], f32)
    s1 = np.asarray(inputs["L0_ffn_w1_s"], f32)
    w2 = np.asarray(inputs["L0_ffn_w2_w"], f32)
    s2 = np.asarray(inputs["L0_ffn_w2_s"], f32)
    h1 = np.asarray(inputs["L0_ffn_h1_w"], f32)
    sh1 = np.asarray(inputs["L0_ffn_h1_s"], f32)
    h2 = np.asarray(inputs["L0_ffn_h2_w"], f32)
    sh2 = np.asarray(inputs["L0_ffn_h2_s"], f32)
    w3 = np.asarray(inputs["L0_ffn_w3_w"], f32)
    s3 = np.asarray(inputs["L0_ffn_w3_s"], f32)

    def tile_lhs(wT):
        K, M = wT.shape
        n_k = K // 128
        out = np.empty((M // 128, 128, K), f32)
        for m in range(M // 128):
            out[m] = (wT[:, m * 128:(m + 1) * 128]
                      .reshape(n_k, 128, 128).transpose(1, 0, 2)
                      .reshape(128, K))
        return out

    w1t = tile_lhs(np.ascontiguousarray(w1.T))
    w2t = tile_lhs(np.ascontiguousarray(w2.T))
    h1t = tile_lhs(np.ascontiguousarray(h1.T))
    h2t = tile_lhs(np.ascontiguousarray(h2.T))
    w3t = tile_lhs(np.ascontiguousarray(w3.T))
    sel = np.zeros((32, 32 * 128), f32)
    for n in range(32):
        sel[n, n * 128:(n + 1) * 128] = 1.0

    in_maps = []
    p = np.arange(128)
    for core in range(8):
        b, k = core // 2, core % 2
        sl = slice(k * EH, (k + 1) * EH)
        sl2 = slice(E + k * EH, E + (k + 1) * EH)
        wip_flat = np.concatenate([ipw[:, sl, :], ipw[:, sl2, :]],
                                  axis=1).transpose(0, 2, 1)
        wop_flat = opw[:, :, sl].transpose(0, 2, 1)
        wip = np.stack([tile_lhs(np.ascontiguousarray(wip_flat[i]))
                        for i in range(NL)])
        wop = np.stack([tile_lhs(np.ascontiguousarray(wop_flat[i]))
                        for i in range(NL)])
        wxp = np.ascontiguousarray(xpw[:, :, sl].transpose(0, 2, 1))
        wdt = np.ascontiguousarray(dtw[:, sl, :].transpose(0, 2, 1))

        sm = np.zeros((128, SM_COLS), f32)
        for i in range(NL):
            for kk in range(DT8):
                sm[:, SM_OFF["nw"] + i * DT8 + kk] = nw[i, kk * 128 + p]
        for kk in range(DT8):
            sm[:, SM_OFF["nf"] + kk] = nf[kk * 128 + p]
        for i in range(NL):
            for et in range(ET8):
                eidx = k * EH + et * 128 + p
                for n in range(NST):
                    sm[:, SM_OFF["alog"] + i * 128 + et * NST + n] = \
                        alog[i, eidx, n]
                sm[:, SM_OFF["dp"] + i * ET8 + et] = dpar[i, eidx]
                sm[:, SM_OFF["cb"] + i * ET8 + et] = cb[i, eidx]
                sm[:, SM_OFF["dtb"] + i * ET8 + et] = dtb[i, eidx]
                for kc in range(KC):
                    sm[:, SM_OFF["cw"] + i * 32 + et * 4 + kc] = \
                        cw[i, eidx, kc]
        for m in range(MF):
            sm[:, SM_OFF["s1"] + m] = s1[m * 128 + p]
            sm[:, SM_OFF["s2"] + m] = s2[m * 128 + p]
            sm[:, SM_OFF["sh1"] + m] = sh1[m * 128 + p]
            sm[:, SM_OFF["sh2"] + m] = sh2[m * 128 + p]
        for m in range(DT8):
            sm[:, SM_OFF["s3"] + m] = s3[m * 128 + p]
        sm[:, SM_OFF["c_eps"]] = EPS
        sm[:, SM_OFF["c_mhalf"]] = -0.5

        in_maps.append(dict(
            x0=np.ascontiguousarray(x[b].T),
            wip=wip, wxp=wxp, wdt=wdt, wop=wop,
            w1t=w1t, w2t=w2t, h1t=h1t, h2t=h2t, w3t=w3t,
            smalls=sm, sel32=sel,
            tok0=np.array([[k * 2048]], np.int32),
        ))
    return in_maps


def kernel(**inputs):
    in_maps = make_inmaps(inputs)
    nc = build_program()
    res = run_bass_kernel_spmd(nc, in_maps, core_ids=list(range(8)))
    out = np.empty((B, L, D), np.float32)
    for core in range(8):
        b, k = core // 2, core % 2
        r = np.asarray(res.results[core]["out_c"], np.float32)
        out[b, k * 2048:(k + 1) * 2048, :] = r.T
    return out


# version 12 (final)
# speedup vs baseline: 1.2566x; 1.0496x over previous
import numpy as np
import concourse.bass as bass
import concourse.mybir as mybir
from concourse.bass_utils import run_bass_kernel_spmd
from contextlib import ExitStack

F32 = mybir.dt.float32
BF16 = mybir.dt.bfloat16
I32 = mybir.dt.int32
AF = mybir.ActivationFunctionType
ALU = mybir.AluOpType

B, L, D, E, EH, NST, RDT, KC = 4, 4096, 1024, 2048, 1024, 16, 64, 4
NL = 4
TC = 512
NCH = L // TC
DT8 = D // 128
ET8 = EH // 128
TF = 1024
EPS = 1e-5
HFF = 4096
MF = HFF // 128
RG_PAIRS = [[0, 1], [2, 3], [4, 5], [6, 7]]

QUEUES = ("pe", "act", "dve", "sync", "gp")
NDS = {"sync": 8, "gp": 8, "act": 6}
NCCS = 4
SEMKEYS = (("pe.x", "act.x", "dve.x", "gp.x")
           + tuple(f"sync.d{i}" for i in range(NDS["sync"]))
           + tuple(f"gp.d{i}" for i in range(NDS["gp"]))
           + tuple(f"act.d{i}" for i in range(NDS["act"]))
           + tuple(f"gp.cc{i}" for i in range(NCCS)))
KLASS_INC = {"x": 1, "d": 16, "cc": 1}


def _sm_layout():
    off, col = {}, 0

    def take(name, n):
        nonlocal col
        off[name] = col
        col += n
    take("nw", NL * DT8)
    take("nf", DT8)
    take("alog", NL * 128)
    take("dp", NL * ET8)
    take("cb", NL * ET8)
    take("dtb", NL * ET8)
    take("cw", NL * ET8 * KC)
    take("s1", MF)
    take("s2", MF)
    take("sh1", MF)
    take("sh2", MF)
    take("s3", DT8)
    take("c_eps", 1)
    take("c_mhalf", 1)
    take("c_half", 1)
    return off, col


SM_OFF, SM_COLS = _sm_layout()


class Buf:
    __slots__ = ("w", "r")

    def __init__(self, w=None):
        self.w = dict(w) if w else {}
        self.r = {}


class Sched:
    def __init__(self):
        self.ops = {q: [] for q in QUEUES}
        self.count = {s: 0 for s in SEMKEYS}
        self.wait_cache = {q: {} for q in QUEUES}
        self.rr = {}

    def op(self, queue, fn, reads=(), writes=(), deps=(), klass="x",
           silent=False):
        if klass == "x" or silent:
            semkey = f"{queue}.x"
        else:
            n_pool = NDS[queue] if klass == "d" else NCCS
            idx = self.rr.get((queue, klass), 0)
            self.rr[(queue, klass)] = idx + 1
            semkey = f"{queue}.{klass}{idx % n_pool}"
        assert silent or semkey in SEMKEYS, semkey
        d = {}
        if klass in ("d", "cc") and not silent and self.count[semkey] > 0:
            d[semkey] = self.count[semkey]

        def add(tok):
            for s2, c2 in tok.items():
                if (queue == "pe" and s2 == "pe.x"):
                    continue
                if c2 > d.get(s2, 0):
                    d[s2] = c2
        for b in reads:
            add(b.w)
        for b in writes:
            add(b.w)
            add(b.r)
        for tok in deps:
            add(tok)
        wc = self.wait_cache[queue]
        waits = {}
        for s2, c2 in d.items():
            if c2 > wc.get(s2, 0):
                waits[s2] = c2
                wc[s2] = c2
        if silent:
            self.ops[queue].append((waits, fn, None, 0))
            return {}
        inc = KLASS_INC[klass]
        self.count[semkey] += inc
        tokv = self.count[semkey]
        self.ops[queue].append((waits, fn, semkey, inc))
        tok = {semkey: tokv}
        for b in reads:
            b.r[semkey] = max(b.r.get(semkey, 0), tokv)
        for b in writes:
            b.w[semkey] = tokv
        return tok


class SBuf:

    def __init__(self, tensor, lo, n, dtype, init_tok=None):
        self.t = tensor
        self.lo = lo
        self.n = n
        self.dt = dtype
        self.buf = Buf(init_tok)

    def ap(self, a=0, b=None):
        b = self.n if b is None else b
        if self.dt == BF16:
            return self.t[:, self.lo + a:self.lo + b]
        return self.t[:, self.lo + 2 * a:self.lo + 2 * b].bitcast(F32)

    def rows(self, r0, r1, a=0, b=None):
        b = self.n if b is None else b
        if self.dt == BF16:
            return self.t[r0:r1, self.lo + a:self.lo + b]
        return self.t[r0:r1, self.lo + 2 * a:self.lo + 2 * b].bitcast(F32)

    def v3(self, inner):
        return self.ap().rearrange("p (o i) -> p o i", i=inner)


class Arena:
    def __init__(self, tensor, n_units):
        self.t = tensor
        self.n = n_units
        self.pos = 0
        self.marks = []

    def alloc(self, n, dtype=BF16, init_tok=None):
        units = n * (2 if dtype == F32 else 1)
        self.pos = (self.pos + 1) & ~1
        assert self.pos + units <= self.n, ("arena", self.pos, units, self.n)
        lo = self.pos
        self.pos += units
        return SBuf(self.t, lo, n, dtype, init_tok)

    def alias(self, base, n, dtype, off_units=0):
        s = SBuf(self.t, base.lo + off_units, n, dtype)
        s.buf = base.buf
        return s

    def mark(self):
        self.marks.append(self.pos)

    def reset_to_mark(self):
        self.pos = self.marks[-1]


def build_program(n_layers=NL, n_chunks=NCH, do_ffn=True, tap_list=(), defer=True, opev_sep=False, revert=()):
    nc = bass.Bass("TRN2", target_bir_lowering=False)
    S = Sched()

    def din(name, shape, dt=F32):
        return nc.dram_tensor(name, shape, dt, kind="ExternalInput")

    x0_t = din("x0", [D, L])
    wip_t = din("wip", [NL, 16, 128, DT8 * 128])
    wxp_t = din("wxp", [NL, EH, 96])
    wdt_t = din("wdt", [NL, RDT, EH])
    wop_t = din("wop", [NL, DT8, 128, ET8 * 128])
    w1_t = din("w1t", [MF, 128, DT8 * 128])
    w2_t = din("w2t", [MF, 128, DT8 * 128])
    h1_t = din("h1t", [MF, 128, MF * 128])
    h2_t = din("h2t", [MF, 128, MF * 128])
    w3_t = din("w3t", [DT8, 128, MF * 128])
    smalls_t = din("smalls", [128, SM_COLS])
    sel_in_t = din("sel32", [32, 32 * 128])
    tok_in_t = din("tok0", [1, 1], I32)
    out_t = nc.dram_tensor("out_c", [D, L // 2], F32, kind="ExternalOutput")

    x0_in = x0_t.ap()
    wip_in, wxp_in = wip_t.ap(), wxp_t.ap()
    wdt_in, wop_in = wdt_t.ap(), wop_t.ap()
    w1_in, w2_in = w1_t.ap(), w2_t.ap()
    h1_in, h2_in, w3_in = h1_t.ap(), h2_t.ap(), w3_t.ap()
    out_ext = out_t.ap()

    wipb_t = nc.dram_tensor("wipb_d", [NL, 16, 128, DT8 * 128], BF16,
                            kind="Internal")
    wopb_t = nc.dram_tensor("wopb_d", [NL, DT8, 128, ET8 * 128], BF16,
                            kind="Internal")
    w1b_t = nc.dram_tensor("w1b_d", [MF, 128, DT8 * 128], BF16, kind="Internal")
    w2b_t = nc.dram_tensor("w2b_d", [MF, 128, DT8 * 128], BF16, kind="Internal")
    h1b_t = nc.dram_tensor("h1b_d", [MF, 128, MF * 128], BF16, kind="Internal")
    h2b_t = nc.dram_tensor("h2b_d", [MF, 128, MF * 128], BF16, kind="Internal")
    w3b_t = nc.dram_tensor("w3b_d", [DT8, 128, MF * 128], BF16, kind="Internal")
    wipb_in, wopb_in = wipb_t.ap(), wopb_t.ap()
    w1b_in, w2b_in = w1b_t.ap(), w2b_t.ap()
    h1b_in, h2b_in, w3b_in = h1b_t.ap(), h2b_t.ap(), w3b_t.ap()
    wb_bufs = {"wip": [Buf() for _ in range(NL)],
               "wop": [Buf() for _ in range(NL)],
               "w1": Buf(), "w2": Buf(), "h1": Buf(), "h2": Buf(),
               "w3": Buf()}
    resid_t = nc.dram_tensor("resid_d", [D, L], F32, kind="Internal")
    x1_dt = nc.dram_tensor("x1_d", [D, L], F32, kind="Internal")
    xn2_dt = nc.dram_tensor("xn2_d", [D, L], BF16, kind="Internal")
    xsrc_t = nc.dram_tensor("xsrc_d", [NCH * 96, TC], F32, kind="Internal")
    xdst_t = nc.dram_tensor("xdst_d", [NCH * 96, TC], F32, kind="Internal")
    osrc_t = nc.dram_tensor("osrc_d", [NCH * D, TC], F32, kind="Internal")
    odst_t = nc.dram_tensor("odst_d", [NCH * D, TC], F32, kind="Internal")
    resid_d, x1_d, xn2_d = resid_t.ap(), x1_dt.ap(), xn2_dt.ap()
    xsrc_d, xdst_d = xsrc_t.ap(), xdst_t.ap()
    osrc_d, odst_d = osrc_t.ap(), odst_t.ap()

    db = {}
    for t in ("resid", "x1", "xn2", "xsrc", "xdst", "osrc", "odst"):
        db[t] = [Buf() for _ in range(NCH)]
    db["out"] = Buf()

    with ExitStack() as st:
        NA = 106400
        ar_t = st.enter_context(nc.sbuf_tensor("ar", [128, NA], BF16))
        tok_t = st.enter_context(nc.sbuf_tensor("tok_t", [1, 1], I32))
        psum = [st.enter_context(nc.psum_tensor(f"ps{j}", [128, 512], F32))
                for j in range(8)]
        sems = {s: st.enter_context(nc.semaphore("sem_" + s.replace(".", "_")))
                for s in SEMKEYS}
        r_tok = st.enter_context(nc.sync.register("r_tok"))
        r_off = st.enter_context(nc.sync.register("r_off"))
        r_off2 = st.enter_context(nc.sync.register("r_off2"))

        A = Arena(ar_t, NA)
        psb = [Buf() for _ in range(8)]
        tok_b = Buf()
        OP = S.op

        def dma(queue, out_ap, in_ap, reads=(), writes=(), deps=()):
            def fn(eng, o=out_ap, i=in_ap):
                return eng.dma_start(out=o, in_=i)
            return OP(queue, fn, reads=reads, writes=writes, deps=deps,
                      klass="d")

        ones_st = A.alloc(128)
        state = A.alloc(NL * 128)
        tails = A.alloc(NL * ET8 * 3)
        smalls = A.alloc(SM_COLS, F32)
        A_sb = A.alloc(NL * 128, F32)
        wstgA = A.alloc(512, F32)
        wstgB = A.alloc(512, F32)

        def smc(col):
            return smalls.ap(col, col + 1)

        nc.const_aps.register(F32, EPS, smc(SM_OFF["c_eps"]))
        nc.const_aps.register(F32, -0.5, smc(SM_OFF["c_mhalf"]))

        A.mark()

        wipT = [A.alloc(1024), A.alloc(1024)]
        wopT = [A.alloc(1024), A.alloc(1024)]
        w_xp = A.alloc(ET8 * 96)
        w_dt = A.alloc(1024)
        hn_bf = A.alloc(DT8 * TC)
        y_g = A.alloc(ET8 * TC)
        u_bf2 = [A.alloc(ET8 * TC), A.alloc(ET8 * TC)]
        zs_bf2 = [A.alloc(ET8 * TC), A.alloc(ET8 * TC)]
        a_bufs = [A.alloc(TC) for _ in range(4)]
        w_buf = A.alloc(TC)
        b_strip = A.alloc(NST * TC)
        xi_ext = A.alloc(ET8 * (TC + 3))
        xi_bufs = [Buf() for _ in range(ET8)]
        cv = A.alloc(4 * TC)
        stat_bf = A.alias(cv, 2 * TC, BF16)
        h_strip = A.alloc(NST * TC)
        op_ev = (None if opev_sep
                 else A.alias(h_strip, DT8 * TC, F32))
        Bst = A.alloc(NST * TC)
        h_in = A.alias(Bst, DT8 * TC, F32)
        Cst = A.alloc(NST * TC)
        dtin_bf = A.alloc(TC)
        sel = A.alloc(32 * 128)
        bc = (A.alloc(TC, F32) if "bc" in revert else A.alias(wstgA, TC, F32))
        bcb = A.alloc(TC)
        Rc = A.alloc(DT8 * TC, F32)
        if opev_sep:
            op_ev = A.alias(Rc, DT8 * TC, F32)
        dt_et2 = [A.alloc(TC, F32), A.alloc(TC, F32)]
        sp2 = [A.alloc(2 * TC, F32), A.alloc(2 * TC, F32)]
        y_f2 = A.alloc(TC, F32)
        y_f = A.alloc(TC, F32)
        xp_ev = A.alias(y_f, TC, F32)
        m_rep = A.alloc(TC, F32)
        v_tmp = A.alloc(TC, F32)
        rstd = A.alloc(TC, F32)
        r1 = A.alloc(TC, F32)


        def emit_stats_ms(slices, stat_rg, r1_rg, rstd_rg):
            pb, pbb = psum[4], psb[4]
            nt = len(slices)
            for kk, (ap, rb) in enumerate(slices):
                half = kk % 2
                sl = stat_rg.ap(half * TC, (half + 1) * TC)

                def fsq(e, d=sl, s=ap):
                    return e.activation(d, s, AF.Square)
                OP("act", fsq, reads=[rb], writes=[stat_rg.buf])

                def fmm(e, s=sl, first=(kk == 0), last=(kk == nt - 1)):
                    return e.matmul(pb[:, :], ones_st.ap(), s,
                                    start=first, stop=last)
                OP("pe", fmm, reads=[stat_rg.buf, ones_st.buf], writes=[pbb])

            def fln(e):
                return e.activation(r1_rg.ap(), pb[:, :], AF.Ln, bias=EPS)
            OP("act", fln, reads=[pbb, smalls.buf], writes=[r1_rg.buf])

            def fex(e):
                return e.activation(rstd_rg.ap(), r1_rg.ap(), AF.Exp,
                                    scale=-0.5)
            OP("act", fex, reads=[r1_rg.buf, smalls.buf], writes=[rstd_rg.buf])

        def emit_stats_mean_var(slices, stat_rg, m_rg, v_rg, r1_rg, rstd_rg):
            pbm, pbs = psum[4], psum[5]
            pbmb, pbsb = psb[4], psb[5]
            nt = len(slices)
            for kk, (ap, rb) in enumerate(slices):
                half = kk % 2
                sl = stat_rg.ap(half * TC, (half + 1) * TC)

                def fcp(e, d=sl, s=ap):
                    return e.copy(d, s)
                OP("act", fcp, reads=[rb], writes=[stat_rg.buf])

                def fmm(e, s=sl, first=(kk == 0), last=(kk == nt - 1)):
                    return e.matmul(pbm[:, :], ones_st.ap(), s,
                                    start=first, stop=last)
                OP("pe", fmm, reads=[stat_rg.buf, ones_st.buf], writes=[pbmb])
            for kk, (ap, rb) in enumerate(slices):
                half = kk % 2
                sl = stat_rg.ap(half * TC, (half + 1) * TC)

                def fsq(e, d=sl, s=ap):
                    return e.activation(d, s, AF.Square)
                OP("act", fsq, reads=[rb], writes=[stat_rg.buf])

                def fmm(e, s=sl, first=(kk == 0), last=(kk == nt - 1)):
                    return e.matmul(pbs[:, :], ones_st.ap(), s,
                                    start=first, stop=last)
                OP("pe", fmm, reads=[stat_rg.buf, ones_st.buf], writes=[pbsb])
            OP("act", lambda e: e.copy(m_rg.ap(), pbm[:, :]),
               reads=[pbmb], writes=[m_rg.buf])
            OP("dve", lambda e: e.tensor_mul(v_rg.ap(), m_rg.ap(), m_rg.ap()),
               reads=[m_rg.buf], writes=[v_rg.buf])
            OP("dve", lambda e: e.tensor_sub(v_rg.ap(), pbs[:, :], v_rg.ap()),
               reads=[pbsb, v_rg.buf], writes=[v_rg.buf])
            OP("act", lambda e: e.activation(r1_rg.ap(), v_rg.ap(), AF.Ln,
                                             bias=EPS),
               reads=[v_rg.buf, smalls.buf], writes=[r1_rg.buf])
            OP("act", lambda e: e.activation(rstd_rg.ap(), r1_rg.ap(), AF.Exp,
                                             scale=-0.5),
               reads=[r1_rg.buf, smalls.buf], writes=[rstd_rg.buf])

        def cast_load(queue, stage, dst, dst_lo, src_ap, n_cols, rows=128):
            for j0 in range(0, n_cols, 512):
                w = min(512, n_cols - j0)
                stg_s = wstgA if "stage" in revert else                     (wstgA, wstgB)[(j0 // 512) % 2]
                stg = stg_s.rows(0, rows, 0, w)
                dma(queue, stg, src_ap[:, j0:j0 + w], writes=[stg_s.buf])

                def fcv(e, d=dst.rows(0, rows, dst_lo + j0, dst_lo + j0 + w),
                        s=stg):
                    return e.copy(d, s)
                OP("act", fcv, reads=[stg_s.buf], writes=[dst.buf])

        wl_state = {"i": 0, "stages": None}

        def wtile_load(src_tile_bf, n_k, dst, key):
            i = wl_state["i"]
            wl_state["i"] = i + 1
            q = ("sync", "gp")[i % 2]
            dma(q, dst.ap(0, n_k * 128), src_tile_bf,
                deps=[wb_bufs[key].w], writes=[dst.buf])

        prepS = [A.alloc(512, F32), A.alloc(512, F32)]
        prepB = [A.alloc(512), A.alloc(512)]
        prep_i = [0]

        def emit_prep(src_f32, dst_bf, dst_buf, n_cols, q_in="act",
                      q_out="gp"):
            for j0 in range(0, n_cols, 512):
                w = min(512, n_cols - j0)
                k = prep_i[0]
                prep_i[0] += 1
                ps_, pb_ = prepS[k % 2], prepB[k % 2]
                dma(q_in, ps_.ap(0, w), src_f32[:, j0:j0 + w],
                    writes=[ps_.buf])

                def fc(e, d=pb_.ap(0, w), s=ps_.ap(0, w)):
                    return e.copy(d, s)
                OP("act", fc, reads=[ps_.buf], writes=[pb_.buf])
                dma(q_out, dst_bf[:, j0:j0 + w], pb_.ap(0, w),
                    reads=[pb_.buf], writes=[dst_buf if isinstance(dst_buf, Buf)
                                             else dst_buf])

        def layer_w_items(i):
            items = []
            for m in range(16):
                items.append((wip_in[i, m, :, :], wipb_in[i, m, :, :],
                              ("wip", i), DT8 * 128))
            for m in range(DT8):
                items.append((wop_in[i, m, :, :], wopb_in[i, m, :, :],
                              ("wop", i), DT8 * 128))
            return items

        def wbuf_of(key):
            if isinstance(key, tuple):
                return wb_bufs[key[0]][key[1]]
            return wb_bufs[key]

        def emit_layer_weights(i):
            for kk in range(ET8):
                cast_load("sync", None, w_xp, kk * 96,
                          wxp_in[i, kk * 128:(kk + 1) * 128, :], 96)
            cast_load("sync", None, w_dt, 0, wdt_in[i, :, :], 1024, rows=RDT)
            if i == 0:
                for it in layer_w_items(0):
                    emit_prep(it[0], it[1], wbuf_of(it[2]), it[3])
            if i + 1 < n_layers:
                prep_work[0:0] = layer_w_items(i + 1)

        taps = {}

        def tap_dump(name, region):
            ap_out = nc.dram_tensor("tap_" + name, [128, region.n], region.dt,
                                    kind="ExternalOutput").ap()
            tb = Buf()
            dma("sync", ap_out, region.ap(), reads=[region.buf], writes=[tb])
            taps[name] = tb

        def emit_front_p1(i, c):
            t0 = c * TC
            r3 = Rc.v3(TC)
            rsl = resid_d[:, t0:t0 + TC].rearrange("(k p) t -> p k t", p=128)
            if i == 0:
                dma("sync", r3,
                    x0_in[:, t0:t0 + TC].rearrange("(k p) t -> p k t", p=128),
                    writes=[Rc.buf])
                emit_stats_mean_var(
                    [(Rc.ap(kk * TC, (kk + 1) * TC), Rc.buf)
                     for kk in range(DT8)], stat_bf, m_rep, v_tmp, r1, rstd)
                for kk in range(DT8):
                    sl = Rc.ap(kk * TC, (kk + 1) * TC)

                    def f1(e, s=sl):
                        return e.tensor_sub(s, s, m_rep.ap())
                    OP("dve", f1, reads=[Rc.buf, m_rep.buf], writes=[Rc.buf])

                    def f2(e, s=sl):
                        return e.tensor_mul(s, s, rstd.ap())
                    OP("dve", f2, reads=[Rc.buf, rstd.buf], writes=[Rc.buf])
                dma("sync", rsl, r3, reads=[Rc.buf], writes=[db["resid"][c]])
            else:
                hsl = odst_d[c * D:(c + 1) * D, :].rearrange(
                    "(k p) t -> p k t", p=128)
                dma("sync", r3, hsl, reads=[db["odst"][c]], writes=[Rc.buf])

        def emit_front_p1b(i, c):
            emit_stats_ms(
                [(Rc.ap(kk * TC, (kk + 1) * TC), Rc.buf) for kk in range(DT8)],
                stat_bf, r1, rstd)
            for kk in range(DT8):
                def fhn(e, d=hn_bf.ap(kk * TC, (kk + 1) * TC),
                        s=Rc.ap(kk * TC, (kk + 1) * TC),
                        sc=smc(SM_OFF["nw"] + i * DT8 + kk)):
                    return e.scalar_tensor_tensor(d, s, sc, rstd.ap(),
                                                  ALU.mult, ALU.mult)
                OP("dve", fhn, reads=[Rc.buf, smalls.buf, rstd.buf],
                   writes=[hn_bf.buf])

        def emit_front_p2(i, c):
            u_bf = u_bf2[c % 2]
            zs_bf = zs_bf2[c % 2]
            for et in range(ET8):
                base = et * (TC + 3)
                tl = (i * ET8 + et) * 3

                def ftl(e, d=xi_ext.ap(base, base + 3),
                        s=tails.ap(tl, tl + 3)):
                    return e.copy(d, s)
                OP("act", ftl, reads=[tails.buf], writes=[xi_bufs[et]])

            for m in range(16):
                pb, pbb = psum[m % 4], psb[m % 4]
                wt_ = wipT[m % 2]
                dma("sync", wt_.ap(), wipb_in[i, m, :, :],
                    deps=[wb_bufs["wip"][i].w], writes=[wt_.buf])

                def fmm(e, pb=pb, wt_=wt_):
                    last = None
                    for kk in range(DT8):
                        last = e.matmul(
                            pb[:, :],
                            wt_.ap(kk * 128, (kk + 1) * 128),
                            hn_bf.ap(kk * TC, (kk + 1) * TC),
                            start=(kk == 0), stop=(kk == DT8 - 1))
                    return last
                OP("pe", fmm, reads=[wt_.buf, hn_bf.buf], writes=[pbb])
                if m < ET8:
                    def fev(e, d=xi_ext.ap(m * (TC + 3) + 3,
                                           (m + 1) * (TC + 3)), pb=pb):
                        return e.copy(d, pb[:, :])
                    OP("act", fev, reads=[pbb], writes=[xi_bufs[m]])
                else:
                    def fev(e, d=zs_bf.ap((m - ET8) * TC, (m - ET8 + 1) * TC),
                            pb=pb):
                        return e.activation(d, pb[:, :], AF.Silu)
                    OP("act", fev, reads=[pbb], writes=[zs_bf.buf])

            for et in range(ET8):
                base = et * (TC + 3)
                tl = (i * ET8 + et) * 3
                for k in range(KC):
                    sc = smc(SM_OFF["cw"] + i * 32 + et * 4 + k)
                    src = xi_ext.ap(base + k, base + k + TC)
                    if k == 0:
                        def ftap(e, d=cv.ap(0, TC), s=src, sc=sc):
                            return e.tensor_scalar(d, s, sc, None, ALU.mult)
                        OP("dve", ftap, reads=[xi_bufs[et], smalls.buf],
                           writes=[cv.buf])
                    else:
                        def ftap(e, d=cv.ap((k % 2) * TC, (k % 2 + 1) * TC),
                                 s=src, sc=sc,
                                 prev=cv.ap(((k - 1) % 2) * TC,
                                            ((k - 1) % 2 + 1) * TC)):
                            return e.scalar_tensor_tensor(d, s, sc, prev,
                                                          ALU.mult, ALU.add)
                        OP("dve", ftap, reads=[xi_bufs[et], smalls.buf,
                                               cv.buf], writes=[cv.buf])

                def fu(e, d=u_bf.ap(et * TC, (et + 1) * TC),
                       s=cv.ap(TC, 2 * TC),
                       bias=smc(SM_OFF["cb"] + i * ET8 + et)):
                    return e.activation(d, s, AF.Silu, bias=bias)
                OP("act", fu, reads=[cv.buf, smalls.buf], writes=[u_bf.buf])

                def fsv(e, d=tails.ap(tl, tl + 3),
                        s=xi_ext.ap(base + TC, base + TC + 3)):
                    return e.copy(d, s)
                OP("act", fsv, reads=[xi_bufs[et]], writes=[tails.buf])

            pb4, pbb4 = psum[4], psb[4]

            def fxp(e):
                last = None
                for kk in range(ET8):
                    last = e.matmul(pb4[0:96, :],
                                    w_xp.ap(kk * 96, (kk + 1) * 96),
                                    u_bf.ap(kk * TC, (kk + 1) * TC),
                                    start=(kk == 0), stop=(kk == ET8 - 1))
                return last
            OP("pe", fxp, reads=[w_xp.buf, u_bf.buf], writes=[pbb4])
            OP("act", lambda e: e.copy(xp_ev.rows(0, 96), pb4[0:96, :]),
               reads=[pbb4], writes=[xp_ev.buf])
            dma("gp", xsrc_d[c * 96:(c + 1) * 96, :], xp_ev.rows(0, 96),
                reads=[xp_ev.buf], writes=[db["xsrc"][c]])

            def fcc(e, c=c):
                return e.collective_compute(
                    "AllReduce", ALU.add, replica_groups=RG_PAIRS,
                    ins=[xsrc_d[c * 96:(c + 1) * 96, :].opt()],
                    outs=[xdst_d[c * 96:(c + 1) * 96, :].opt()])
            OP("gp", fcc, reads=[db["xsrc"][c]], writes=[db["xdst"][c]],
               klass="cc")

        def emit_dt_chain(i, et):
            dt_et = dt_et2[et % 2]
            sp = sp2[et % 2]
            pb5, pbb5 = psum[5], psb[5]

            def fdt(e, et=et):
                return e.matmul(pb5[:, :],
                                w_dt.rows(0, RDT, et * 128, (et + 1) * 128),
                                dtin_bf.rows(0, RDT),
                                start=True, stop=True)
            OP("pe", fdt, reads=[w_dt.buf, dtin_bf.buf], writes=[pbb5])
            dtb_sc = smc(SM_OFF["dtb"] + i * ET8 + et)

            def f_abs(e, sc=dtb_sc, sp=sp):
                return e.activation(sp.ap(0, TC), pb5[:, :], AF.Abs, bias=sc)
            OP("act", f_abs, reads=[pbb5, smalls.buf], writes=[sp.buf])

            def f_exp(e, sp=sp):
                return e.activation(sp.ap(TC, 2 * TC), sp.ap(0, TC),
                                    AF.Exp, scale=-1.0)
            OP("act", f_exp, reads=[sp.buf], writes=[sp.buf])

            def f_ln(e, sp=sp):
                return e.activation(sp.ap(0, TC), sp.ap(TC, 2 * TC),
                                    AF.Ln, bias=1.0)
            OP("act", f_ln, reads=[sp.buf], writes=[sp.buf])

            def f_rel(e, sc=dtb_sc, sp=sp):
                return e.activation(sp.ap(TC, 2 * TC), pb5[:, :],
                                    AF.Relu, bias=sc)
            OP("act", f_rel, reads=[pbb5, smalls.buf], writes=[sp.buf])

            def f_dt(e, sp=sp, dt_et=dt_et):
                return e.tensor_add(dt_et.ap(), sp.ap(0, TC),
                                    sp.ap(TC, 2 * TC))
            OP("dve", f_dt, reads=[sp.buf], writes=[dt_et.buf])

        def emit_scan(i, c, mid0=None, mid_cb=None):
            u_bf = u_bf2[c % 2]
            zs_bf = zs_bf2[c % 2]
            dma("gp", dtin_bf.rows(0, RDT), xdst_d[c * 96:c * 96 + RDT, :],
                reads=[db["xdst"][c]], writes=[dtin_bf.buf])
            dma("gp", bc.rows(0, 32), xdst_d[c * 96 + RDT:(c + 1) * 96, :],
                reads=[db["xdst"][c]], writes=[bc.buf])
            OP("act", lambda e: e.copy(bcb.rows(0, 32), bc.rows(0, 32)),
               reads=[bc.buf], writes=[bcb.buf])

            for n in range(NST):
                jB = 4 + (n % 2) * 2
                jC = 5 + (n % 2) * 2

                def fb(e, n=n, pb=psum[jB]):
                    return e.matmul(pb[:, :],
                                    sel.rows(0, 32, n * 128, (n + 1) * 128),
                                    bcb.rows(0, 32), start=True, stop=True)
                OP("pe", fb, reads=[sel.buf, bcb.buf], writes=[psb[jB]])

                def fbe(e, d=Bst.ap(n * TC, (n + 1) * TC), pb=psum[jB]):
                    return e.copy(d, pb[:, :])
                OP("act", fbe, reads=[psb[jB]], writes=[Bst.buf])

                def fc2(e, n=n, pb=psum[jC]):
                    return e.matmul(
                        pb[:, :],
                        sel.rows(0, 32, (NST + n) * 128, (NST + n + 1) * 128),
                        bcb.rows(0, 32), start=True, stop=True)
                OP("pe", fc2, reads=[sel.buf, bcb.buf], writes=[psb[jC]])

                def fce(e, d=Cst.ap(n * TC, (n + 1) * TC), pb=psum[jC]):
                    return e.copy(d, pb[:, :])
                OP("act", fce, reads=[psb[jC]], writes=[Cst.buf])

            for et in range(ET8):
                dt_et = dt_et2[et % 2]
                if et == 0:
                    emit_dt_chain(i, 0)
                if et + 1 < ET8:
                    emit_dt_chain(i, et + 1)

                def fw(e, et=et, dt_et=dt_et):
                    return e.tensor_mul(w_buf.ap(), dt_et.ap(),
                                        u_bf.ap(et * TC, (et + 1) * TC))
                OP("dve", fw, reads=[dt_et.buf, u_bf.buf], writes=[w_buf.buf])

                def fb2(e):
                    return e.tensor_tensor(
                        b_strip.v3(TC),
                        w_buf.ap().unsqueeze(1).broadcast_to([128, NST, TC]),
                        Bst.v3(TC), ALU.mult)
                OP("dve", fb2, reads=[w_buf.buf, Bst.buf],
                   writes=[b_strip.buf])
                for n in range(NST):
                    acol = A_sb.ap(i * 128 + et * NST + n,
                                   i * 128 + et * NST + n + 1)
                    ab_s = a_bufs[n % 4]
                    ab = ab_s.ap()

                    def fe(e, ab=ab, acol=acol, dt_et=dt_et):
                        return e.activation(ab, dt_et.ap(), AF.Exp, scale=acol)
                    OP("act", fe, reads=[dt_et.buf, A_sb.buf],
                       writes=[ab_s.buf])
                    stcol = state.ap(i * 128 + et * NST + n,
                                     i * 128 + et * NST + n + 1)

                    def fs(e, n=n, ab=ab, stcol=stcol, ab_s=ab_s):
                        return e.tensor_tensor_scan(
                            h_strip.ap(n * TC, (n + 1) * TC), ab,
                            b_strip.ap(n * TC, (n + 1) * TC), stcol,
                            ALU.mult, ALU.add)
                    OP("dve", fs, reads=[ab_s.buf, b_strip.buf, state.buf],
                       writes=[h_strip.buf])

                def fst(e, et=et):
                    v = h_strip.v3(TC)[:, :, TC - 1:TC]
                    return e.tensor_copy(
                        state.ap(i * 128 + et * NST, i * 128 + (et + 1) * NST),
                        v)
                OP("dve", fst, reads=[h_strip.buf], writes=[state.buf])

                def fq(e):
                    return e.tensor_mul(b_strip.ap(), h_strip.ap(), Cst.ap())
                OP("dve", fq, reads=[h_strip.buf, Cst.buf],
                   writes=[b_strip.buf])
                qv = b_strip.ap().rearrange("p (n two t) -> p n two t",
                                            two=2, t=TC)

                def ftr1(e, qv=qv):
                    return e.tensor_tensor(h_strip.v3(TC)[:, 0:8, :],
                                           qv[:, :, 0, :], qv[:, :, 1, :],
                                           ALU.add)
                OP("dve", ftr1, reads=[b_strip.buf], writes=[h_strip.buf])
                hv = h_strip.ap().rearrange("p (g two t) -> p g two t",
                                            two=2, t=TC)

                def ftr2(e, hv=hv):
                    return e.tensor_tensor(h_strip.v3(TC)[:, 8:12, :],
                                           hv[:, 0:4, 0, :], hv[:, 0:4, 1, :],
                                           ALU.add)
                OP("dve", ftr2, reads=[h_strip.buf], writes=[h_strip.buf])

                def ftr3(e, hv=hv):
                    return e.tensor_tensor(h_strip.v3(TC)[:, 12:14, :],
                                           hv[:, 4:6, 0, :], hv[:, 4:6, 1, :],
                                           ALU.add)
                OP("dve", ftr3, reads=[h_strip.buf], writes=[h_strip.buf])

                def ftr4(e):
                    return e.tensor_tensor(y_f.ap(),
                                           h_strip.ap(12 * TC, 13 * TC),
                                           h_strip.ap(13 * TC, 14 * TC),
                                           ALU.add)
                OP("dve", ftr4, reads=[h_strip.buf], writes=[y_f.buf])

                def fy2(e, et=et, sc=smc(SM_OFF["dp"] + i * ET8 + et)):
                    return e.scalar_tensor_tensor(
                        y_f2.ap(), u_bf.ap(et * TC, (et + 1) * TC), sc,
                        y_f.ap(), ALU.mult, ALU.add)
                OP("dve", fy2, reads=[u_bf.buf, smalls.buf, y_f.buf],
                   writes=[y_f2.buf])

                def fyg(e, et=et):
                    return e.tensor_mul(y_g.ap(et * TC, (et + 1) * TC),
                                        y_f2.ap(),
                                        zs_bf.ap(et * TC, (et + 1) * TC))
                OP("dve", fyg, reads=[y_f2.buf, zs_bf.buf], writes=[y_g.buf])
                if et == 0 and mid0 is not None:
                    mid0()
                if et == 1 and mid_cb is not None:
                    mid_cb()

        def emit_outproj(i, c):
            src_R = (resid_d if i == 0 else odst_d)
            src_tok = (db["resid"][c] if i == 0 else db["odst"][c])
            if i == 0:
                hsl = resid_d[:, c * TC:(c + 1) * TC].rearrange(
                    "(k p) t -> p k t", p=128)
            else:
                hsl = odst_d[c * D:(c + 1) * D, :].rearrange(
                    "(k p) t -> p k t", p=128)
            dma("sync", h_in.v3(TC), hsl, reads=[src_tok], writes=[h_in.buf])
            for m in range(DT8):
                pb, pbb = psum[m % 4], psb[m % 4]
                wt_ = wopT[m % 2]
                dma("sync", wt_.ap(), wopb_in[i, m, :, :],
                    deps=[wb_bufs["wop"][i].w], writes=[wt_.buf])

                def fop(e, pb=pb, wt_=wt_):
                    last = None
                    for kk in range(ET8):
                        last = e.matmul(
                            pb[:, :],
                            wt_.ap(kk * 128, (kk + 1) * 128),
                            y_g.ap(kk * TC, (kk + 1) * TC),
                            start=(kk == 0), stop=(kk == ET8 - 1))
                    return last
                OP("pe", fop, reads=[wt_.buf, y_g.buf], writes=[pbb])

                def foev(e, d=op_ev.ap(m * TC, (m + 1) * TC), pb=pb,
                         s=h_in.ap(m * TC, (m + 1) * TC),
                         sc=smc(SM_OFF["c_half"])):
                    return e.scalar_tensor_tensor(d, s, sc, pb[:, :],
                                                  ALU.mult, ALU.add)
                OP("dve", foev, reads=[pbb, h_in.buf, smalls.buf],
                   writes=[op_ev.buf])
            dma("gp", osrc_d[c * D:(c + 1) * D, :].rearrange(
                "(k p) t -> p k t", p=128), op_ev.v3(TC),
                reads=[op_ev.buf], writes=[db["osrc"][c]])

            def focc(e, c=c):
                return e.collective_compute(
                    "AllReduce", ALU.add, replica_groups=RG_PAIRS,
                    ins=[osrc_d[c * D:(c + 1) * D, :].opt()],
                    outs=[odst_d[c * D:(c + 1) * D, :].opt()])
            OP("gp", focc, reads=[db["osrc"][c]], writes=[db["odst"][c]],
               klass="cc")

        def emit_tail(c):
            t0 = c * TC
            hsl = odst_d[c * D:(c + 1) * D, :].rearrange("(k p) t -> p k t",
                                                         p=128)
            dma("sync", Rc.v3(TC), hsl, reads=[db["odst"][c]],
                writes=[Rc.buf])
            emit_stats_ms(
                [(Rc.ap(kk * TC, (kk + 1) * TC), Rc.buf) for kk in range(DT8)],
                stat_bf, r1, rstd)
            for kk in range(DT8):
                def fat(e, d=h_in.ap(kk * TC, (kk + 1) * TC),
                        s=Rc.ap(kk * TC, (kk + 1) * TC),
                        sc=smc(SM_OFF["nf"] + kk)):
                    return e.scalar_tensor_tensor(d, s, sc, rstd.ap(),
                                                  ALU.mult, ALU.mult)
                OP("dve", fat, reads=[Rc.buf, smalls.buf, rstd.buf],
                   writes=[h_in.buf])
            emit_stats_mean_var(
                [(h_in.ap(kk * TC, (kk + 1) * TC), h_in.buf)
                 for kk in range(DT8)], stat_bf, m_rep, v_tmp, r1, rstd)
            dma("sync", Rc.v3(TC),
                x0_in[:, t0:t0 + TC].rearrange("(k p) t -> p k t", p=128),
                writes=[Rc.buf])
            for kk in range(DT8):
                sl = h_in.ap(kk * TC, (kk + 1) * TC)

                def f1(e, s=sl):
                    return e.tensor_sub(s, s, m_rep.ap())
                OP("dve", f1, reads=[h_in.buf, m_rep.buf], writes=[h_in.buf])

                def f2(e, s=sl):
                    return e.tensor_mul(s, s, rstd.ap())
                OP("dve", f2, reads=[h_in.buf, rstd.buf], writes=[h_in.buf])

                def f3(e, s=sl, x=Rc.ap(kk * TC, (kk + 1) * TC)):
                    return e.tensor_add(s, s, x)
                OP("dve", f3, reads=[h_in.buf, Rc.buf], writes=[h_in.buf])
            dma("sync", x1_d[:, t0:t0 + TC].rearrange("(k p) t -> p k t",
                                                      p=128),
                h_in.v3(TC), reads=[h_in.buf], writes=[db["x1"][c]])
            emit_stats_mean_var(
                [(h_in.ap(kk * TC, (kk + 1) * TC), h_in.buf)
                 for kk in range(DT8)], stat_bf, m_rep, v_tmp, r1, rstd)
            for kk in range(DT8):
                sl = h_in.ap(kk * TC, (kk + 1) * TC)

                def f4(e, s=sl):
                    return e.tensor_sub(s, s, m_rep.ap())
                OP("dve", f4, reads=[h_in.buf, m_rep.buf], writes=[h_in.buf])

                def f5(e, d=hn_bf.ap(kk * TC, (kk + 1) * TC), s=sl):
                    return e.tensor_mul(d, s, rstd.ap())
                OP("dve", f5, reads=[h_in.buf, rstd.buf], writes=[hn_bf.buf])
            dma("sync", xn2_d[:, t0:t0 + TC].rearrange("(k p) t -> p k t",
                                                       p=128),
                hn_bf.v3(TC), reads=[hn_bf.buf], writes=[db["xn2"][c]])

        def emit_ffn_phase():
            snap = {s: S.count[s] for s in SEMKEYS}
            A.reset_to_mark()

            def nb(n, dtype=BF16):
                return A.alloc(n, dtype, init_tok=snap)
            hg = nb(MF * TF)
            h1s = nb(MF * TF)
            xn2t = A.alias(h1s, DT8 * TF, BF16)
            wtA = nb(MF * 128)
            wtB = nb(MF * 128)
            g_tmp = nb(TF)
            b2_tmp = nb(TF)
            stat2 = nb(2 * TC)
            f_out = nb(DT8 * TF, F32)
            x1kk = nb(TF, F32)
            r12 = A.alias(x1kk, TC, F32)
            wsC = nb(1024, F32)
            wsD = nb(1024, F32)
            wl_state["stages"] = (wsC, wsD)
            m2 = nb(TC, F32)
            v2 = A.alias(wsC, TC, F32)
            rs2 = A.alias(stat2, TC, F32)
            xout = A.alias(hg, TF, F32)

            OP("sync", lambda e: e.reg_load(r_tok, tok_t[0:1, 0:1]),
               deps=[tok_b.w], silent=True)

            for tt in range(2048 // TF):
                def f_ro(e, k=tt * TF):
                    return e.reg_add(r_off, r_tok, k)
                OP("sync", f_ro, silent=True)

                def fxl(e):
                    src = bass.AP(xn2_dt, r_off,
                                  [[L, 128], [128 * L, DT8], [1, TF]])
                    return e.dma_start(out=xn2t.v3(TF), in_=src)
                OP("sync", fxl, writes=[xn2t.buf],
                   deps=[cb.w for cb in db["xn2"]], klass="d")

                for m in range(MF):
                    wtile_load(w1b_in[m, :, :], DT8, wtA, "w1")
                    for ntile in range(TF // 512):
                        pb, pbb = psum[ntile], psb[ntile]

                        def fm1(e, ntile=ntile, pb=pb):
                            last = None
                            for kk in range(DT8):
                                last = e.matmul(
                                    pb[:, :], wtA.ap(kk * 128, (kk + 1) * 128),
                                    xn2t.ap(kk * TF + ntile * 512,
                                            kk * TF + ntile * 512 + 512),
                                    start=(kk == 0), stop=(kk == DT8 - 1))
                            return last
                        OP("pe", fm1, reads=[wtA.buf, xn2t.buf], writes=[pbb])

                        def fe1(e, d=g_tmp.ap(ntile * 512, ntile * 512 + 512),
                                pb=pb, sc=smc(SM_OFF["s1"] + m)):
                            return e.activation(d, pb[:, :], AF.Silu, scale=sc)
                        OP("act", fe1, reads=[pbb, smalls.buf],
                           writes=[g_tmp.buf])
                    wtile_load(w2b_in[m, :, :], DT8, wtB, "w2")
                    for ntile in range(TF // 512):
                        pb, pbb = psum[2 + ntile], psb[2 + ntile]

                        def fm2(e, ntile=ntile, pb=pb):
                            last = None
                            for kk in range(DT8):
                                last = e.matmul(
                                    pb[:, :], wtB.ap(kk * 128, (kk + 1) * 128),
                                    xn2t.ap(kk * TF + ntile * 512,
                                            kk * TF + ntile * 512 + 512),
                                    start=(kk == 0), stop=(kk == DT8 - 1))
                            return last
                        OP("pe", fm2, reads=[wtB.buf, xn2t.buf], writes=[pbb])

                        def fe2(e, d=b2_tmp.ap(ntile * 512, ntile * 512 + 512),
                                pb=pb, sc=smc(SM_OFF["s2"] + m)):
                            return e.activation(d, pb[:, :], AF.Identity,
                                                scale=sc)
                        OP("act", fe2, reads=[pbb, smalls.buf],
                           writes=[b2_tmp.buf])

                    def fhg(e, d=hg.ap(m * TF, (m + 1) * TF)):
                        return e.tensor_mul(d, g_tmp.ap(), b2_tmp.ap())
                    OP("dve", fhg, reads=[g_tmp.buf, b2_tmp.buf],
                       writes=[hg.buf])

                for src_w, wkey, dst_s, scoff, act in (
                        (h1b_in, "h1", h1s, "sh1", AF.Gelu),
                        (h2b_in, "h2", hg, "sh2", AF.Gelu),
                        (w3b_in, "w3", f_out, "s3", AF.Identity)):
                    rhs_s = hg if dst_s is h1s else (h1s if dst_s is hg
                                                    else hg)
                    n_m = DT8 if dst_s is f_out else MF
                    for m in range(n_m):
                        wtm = (wtA, wtB)[m % 2]
                        wtile_load(src_w[m, :, :], MF, wtm, wkey)
                        for ntile in range(TF // 512):
                            pb, pbb = psum[ntile], psb[ntile]

                            def fm3(e, ntile=ntile, pb=pb, rhs_s=rhs_s,
                                    wtm=wtm):
                                last = None
                                for kk in range(MF):
                                    last = e.matmul(
                                        pb[:, :],
                                        wtm.ap(kk * 128, (kk + 1) * 128),
                                        rhs_s.ap(kk * TF + ntile * 512,
                                                 kk * TF + ntile * 512 + 512),
                                        start=(kk == 0), stop=(kk == MF - 1))
                                return last
                            OP("pe", fm3, reads=[wtm.buf, rhs_s.buf],
                               writes=[pbb])

                            def fe3(e, d=dst_s.ap(m * TF + ntile * 512,
                                                  m * TF + ntile * 512 + 512),
                                    pb=pb, sc=smc(SM_OFF[scoff] + m), act=act):
                                return e.activation(d, pb[:, :], act, scale=sc)
                            OP("act", fe3, reads=[pbb, smalls.buf],
                               writes=[dst_s.buf])

                for half in range(TF // 512):
                    emit_stats_mean_var(
                        [(f_out.ap(kk * TF + half * 512,
                                   kk * TF + half * 512 + 512), f_out.buf)
                         for kk in range(DT8)], stat2, m2, v2, r12, rs2)
                    for kk in range(DT8):
                        sl = f_out.ap(kk * TF + half * 512,
                                      kk * TF + half * 512 + 512)

                        def g1(e, s=sl):
                            return e.tensor_sub(s, s, m2.ap())
                        OP("dve", g1, reads=[f_out.buf, m2.buf],
                           writes=[f_out.buf])

                        def g2(e, s=sl):
                            return e.tensor_mul(s, s, rs2.ap())
                        OP("dve", g2, reads=[f_out.buf, rs2.buf],
                           writes=[f_out.buf])
                for kk in range(DT8):
                    def f_ro2(e, k=kk * 128 * L):
                        return e.reg_add(r_off2, r_off, k)
                    OP("sync", f_ro2, silent=True)

                    def fx1(e):
                        src = bass.AP(x1_dt, r_off2, [[L, 128], [1, TF]])
                        return e.dma_start(out=x1kk.ap(), in_=src)
                    OP("sync", fx1, writes=[x1kk.buf],
                       deps=[cb.w for cb in db["x1"]], klass="d")

                    def g3(e, d=xout.ap(), s=f_out.ap(kk * TF, (kk + 1) * TF)):
                        return e.tensor_add(d, s, x1kk.ap())
                    OP("dve", g3, reads=[f_out.buf, x1kk.buf],
                       writes=[xout.buf])

                    def fo(e, kk=kk, tt=tt):
                        return e.dma_start(
                            out=out_ext[kk * 128:(kk + 1) * 128,
                                        tt * TF:(tt + 1) * TF],
                            in_=xout.ap())
                    OP("sync", fo, reads=[xout.buf], writes=[db["out"]],
                       klass="d")

        dma("sync", smalls.ap(), smalls_t.ap(), writes=[smalls.buf])
        for j0 in range(0, 32 * 128, 512):
            stg_s = (wstgA, wstgB)[(j0 // 512) % 2]
            stg = stg_s.rows(0, 32, 0, 512)
            dma("sync", stg, sel_in_t.ap()[:, j0:j0 + 512],
                writes=[stg_s.buf])

            def fselc(e, d=sel.rows(0, 32, j0, j0 + 512), s=stg):
                return e.copy(d, s)
            OP("act", fselc, reads=[stg_s.buf], writes=[sel.buf])
        dma("sync", tok_t[:, :], tok_in_t.ap(), writes=[tok_b])
        OP("gp", lambda e: e.memset(ones_st.ap(), 1.0 / 1024.0),
           writes=[ones_st.buf])
        OP("gp", lambda e: e.memset(state.ap(), 0.0), writes=[state.buf])
        OP("gp", lambda e: e.memset(tails.ap(), 0.0), writes=[tails.buf])
        OP("act", lambda e: e.activation(
            A_sb.ap(), smalls.ap(SM_OFF["alog"], SM_OFF["alog"] + NL * 128),
            AF.Exp), reads=[smalls.buf], writes=[A_sb.buf])
        OP("dve", lambda e: e.tensor_scalar(A_sb.ap(), A_sb.ap(), -1.0, None,
                                            ALU.mult),
           reads=[A_sb.buf], writes=[A_sb.buf])

        prep_work = []
        for m in range(MF):
            prep_work.append((w1_in[m, :, :], w1b_in[m, :, :], "w1",
                              DT8 * 128))
            prep_work.append((w2_in[m, :, :], w2b_in[m, :, :], "w2",
                              DT8 * 128))
        for m in range(MF):
            prep_work.append((h1_in[m, :, :], h1b_in[m, :, :], "h1",
                              MF * 128))
            prep_work.append((h2_in[m, :, :], h2b_in[m, :, :], "h2",
                              MF * 128))
        for m in range(DT8):
            prep_work.append((w3_in[m, :, :], w3b_in[m, :, :], "w3",
                              MF * 128))

        def drain_prep(n_pieces):
            budget = n_pieces
            while prep_work and budget > 0:
                sf, dbf, key, ncols = prep_work.pop(0)
                emit_prep(sf, dbf, wbuf_of(key), ncols)
                budget -= (ncols + 511) // 512

        tapmap = {"hn": lambda: hn_bf, "u": lambda: u_bf,
                  "zs": lambda: zs_bf, "yg": lambda: y_g,
                  "Rc": lambda: Rc, "op": lambda: op_ev,
                  "dt": lambda: dt_et, "st": lambda: state}
        for i in range(n_layers):
            emit_layer_weights(i)
            for c in range(n_chunks):
                if c == 0:
                    emit_front_p1(i, 0)
                    emit_front_p1b(i, 0)
                    emit_front_p2(i, 0)
                if c + 1 < n_chunks:
                    emit_front_p1(i, c + 1)
                if c > 0:
                    emit_outproj(i, c - 1)
                mid0 = ((lambda i=i, c=c: emit_front_p1b(i, c + 1))
                        if c + 1 < n_chunks else None)
                mid = ((lambda i=i, c=c: emit_front_p2(i, c + 1))
                       if c + 1 < n_chunks else None)
                emit_scan(i, c, mid0, mid)
                drain_prep(26)
                for nm, ti, tc_ in tap_list:
                    if ti == i and tc_ == c:
                        tap_dump(f"{nm}_{ti}_{tc_}", tapmap[nm]())
            emit_outproj(i, n_chunks - 1)
        if do_ffn:
            drain_prep(10**9)
            for c in range(n_chunks):
                emit_tail(c)
            emit_ffn_phase()

        final_counts = dict(S.count)
        with nc.Block() as block:
            def make_body(qname):
                ops = S.ops[qname]

                def body(eng):
                    for waits, fn, semkey, inc in ops:
                        for s2, v in sorted(waits.items()):
                            eng.wait_ge(sems[s2], v)
                        inst = fn(eng)
                        if semkey is not None:
                            inst.then_inc(sems[semkey], inc)
                    if qname == "sync":
                        for s2, v in sorted(final_counts.items()):
                            if v > 0:
                                eng.wait_ge(sems[s2], v)
                return body
            block.sync(make_body("sync"))
            block.vector(make_body("dve"))
            block.scalar(make_body("act"))
            block.tensor(make_body("pe"))
            block.gpsimd(make_body("gp"))

    return nc


def make_inmaps(inputs):
    f32 = np.float32
    x = np.asarray(inputs["L0_attn_hidden_states"], f32)
    ipw = np.asarray(inputs["L0_attn_in_proj_w"], f32)
    cw = np.asarray(inputs["L0_attn_conv_w"], f32)
    cb = np.asarray(inputs["L0_attn_conv_b"], f32)
    xpw = np.asarray(inputs["L0_attn_x_proj_w"], f32)
    dtw = np.asarray(inputs["L0_attn_dt_proj_w"], f32)
    dtb = np.asarray(inputs["L0_attn_dt_proj_b"], f32)
    alog = np.asarray(inputs["L0_attn_A_log"], f32)
    dpar = np.asarray(inputs["L0_attn_D_param"], f32)
    opw = np.asarray(inputs["L0_attn_out_proj_w"], f32)
    nw = np.asarray(inputs["L0_attn_norm_w"], f32)
    nf = np.asarray(inputs["L0_attn_norm_f_w"], f32)
    w1 = np.asarray(inputs["L0_ffn_w1_w"], f32)
    s1 = np.asarray(inputs["L0_ffn_w1_s"], f32)
    w2 = np.asarray(inputs["L0_ffn_w2_w"], f32)
    s2 = np.asarray(inputs["L0_ffn_w2_s"], f32)
    h1 = np.asarray(inputs["L0_ffn_h1_w"], f32)
    sh1 = np.asarray(inputs["L0_ffn_h1_s"], f32)
    h2 = np.asarray(inputs["L0_ffn_h2_w"], f32)
    sh2 = np.asarray(inputs["L0_ffn_h2_s"], f32)
    w3 = np.asarray(inputs["L0_ffn_w3_w"], f32)
    s3 = np.asarray(inputs["L0_ffn_w3_s"], f32)

    def tile_lhs(wT):
        K, M = wT.shape
        n_k = K // 128
        out = np.empty((M // 128, 128, K), f32)
        for m in range(M // 128):
            out[m] = (wT[:, m * 128:(m + 1) * 128]
                      .reshape(n_k, 128, 128).transpose(1, 0, 2)
                      .reshape(128, K))
        return out

    w1t = tile_lhs(np.ascontiguousarray(w1.T))
    w2t = tile_lhs(np.ascontiguousarray(w2.T))
    h1t = tile_lhs(np.ascontiguousarray(h1.T))
    h2t = tile_lhs(np.ascontiguousarray(h2.T))
    w3t = tile_lhs(np.ascontiguousarray(w3.T))
    sel = np.zeros((32, 32 * 128), f32)
    for n in range(32):
        sel[n, n * 128:(n + 1) * 128] = 1.0

    in_maps = []
    p = np.arange(128)
    for core in range(8):
        b, k = core // 2, core % 2
        sl = slice(k * EH, (k + 1) * EH)
        sl2 = slice(E + k * EH, E + (k + 1) * EH)
        wip_flat = np.concatenate([ipw[:, sl, :], ipw[:, sl2, :]],
                                  axis=1).transpose(0, 2, 1)
        wop_flat = opw[:, :, sl].transpose(0, 2, 1)
        wip = np.stack([tile_lhs(np.ascontiguousarray(wip_flat[i]))
                        for i in range(NL)])
        wop = np.stack([tile_lhs(np.ascontiguousarray(wop_flat[i]))
                        for i in range(NL)])
        wxp = np.ascontiguousarray(xpw[:, :, sl].transpose(0, 2, 1))
        wdt = np.ascontiguousarray(dtw[:, sl, :].transpose(0, 2, 1))

        sm = np.zeros((128, SM_COLS), f32)
        for i in range(NL):
            for kk in range(DT8):
                sm[:, SM_OFF["nw"] + i * DT8 + kk] = nw[i, kk * 128 + p]
        for kk in range(DT8):
            sm[:, SM_OFF["nf"] + kk] = nf[kk * 128 + p]
        for i in range(NL):
            for et in range(ET8):
                eidx = k * EH + et * 128 + p
                for n in range(NST):
                    sm[:, SM_OFF["alog"] + i * 128 + et * NST + n] = \
                        alog[i, eidx, n]
                sm[:, SM_OFF["dp"] + i * ET8 + et] = dpar[i, eidx]
                sm[:, SM_OFF["cb"] + i * ET8 + et] = cb[i, eidx]
                sm[:, SM_OFF["dtb"] + i * ET8 + et] = dtb[i, eidx]
                for kc in range(KC):
                    sm[:, SM_OFF["cw"] + i * 32 + et * 4 + kc] = \
                        cw[i, eidx, kc]
        for m in range(MF):
            sm[:, SM_OFF["s1"] + m] = s1[m * 128 + p]
            sm[:, SM_OFF["s2"] + m] = s2[m * 128 + p]
            sm[:, SM_OFF["sh1"] + m] = sh1[m * 128 + p]
            sm[:, SM_OFF["sh2"] + m] = sh2[m * 128 + p]
        for m in range(DT8):
            sm[:, SM_OFF["s3"] + m] = s3[m * 128 + p]
        sm[:, SM_OFF["c_eps"]] = EPS
        sm[:, SM_OFF["c_mhalf"]] = -0.5
        sm[:, SM_OFF["c_half"]] = 0.5

        in_maps.append(dict(
            x0=np.ascontiguousarray(x[b].T),
            wip=wip, wxp=wxp, wdt=wdt, wop=wop,
            w1t=w1t, w2t=w2t, h1t=h1t, h2t=h2t, w3t=w3t,
            smalls=sm, sel32=sel,
            tok0=np.array([[k * 2048]], np.int32),
        ))
    return in_maps


def kernel(**inputs):
    in_maps = make_inmaps(inputs)
    nc = build_program()
    res = run_bass_kernel_spmd(nc, in_maps, core_ids=list(range(8)))
    out = np.empty((B, L, D), np.float32)
    for core in range(8):
        b, k = core // 2, core % 2
        r = np.asarray(res.results[core]["out_c"], np.float32)
        out[b, k * 2048:(k + 1) * 2048, :] = r.T
    return out


# version 13
# speedup vs baseline: 1.2619x; 1.0042x over previous
import numpy as np
import concourse.bass as bass
import concourse.mybir as mybir
from concourse.bass_utils import run_bass_kernel_spmd
from contextlib import ExitStack

F32 = mybir.dt.float32
BF16 = mybir.dt.bfloat16
I32 = mybir.dt.int32
AF = mybir.ActivationFunctionType
ALU = mybir.AluOpType

B, L, D, E, EH, NST, RDT, KC = 4, 4096, 1024, 2048, 1024, 16, 64, 4
NL = 4
TC = 512
NCH = L // TC
DT8 = D // 128
ET8 = EH // 128
TF = 1024
EPS = 1e-5
HFF = 4096
MF = HFF // 128
RG_PAIRS = [[0, 1], [2, 3], [4, 5], [6, 7]]

QUEUES = ("pe", "act", "dve", "sync", "gp")
NDS = {"sync": 8, "gp": 8, "act": 6}
NCCS = 4
SEMKEYS = (("pe.x", "act.x", "dve.x", "gp.x")
           + tuple(f"sync.d{i}" for i in range(NDS["sync"]))
           + tuple(f"gp.d{i}" for i in range(NDS["gp"]))
           + tuple(f"act.d{i}" for i in range(NDS["act"]))
           + tuple(f"gp.cc{i}" for i in range(NCCS)))
KLASS_INC = {"x": 1, "d": 16, "cc": 1}


def _sm_layout():
    off, col = {}, 0

    def take(name, n):
        nonlocal col
        off[name] = col
        col += n
    take("nw", NL * DT8)
    take("nf", DT8)
    take("alog", NL * 128)
    take("dp", NL * ET8)
    take("cb", NL * ET8)
    take("dtb", NL * ET8)
    take("cw", NL * ET8 * KC)
    take("s1", MF)
    take("s2", MF)
    take("sh1", MF)
    take("sh2", MF)
    take("s3", DT8)
    take("c_eps", 1)
    take("c_mhalf", 1)
    take("c_half", 1)
    return off, col


SM_OFF, SM_COLS = _sm_layout()


class Buf:
    __slots__ = ("w", "r")

    def __init__(self, w=None):
        self.w = dict(w) if w else {}
        self.r = {}


class Sched:
    def __init__(self):
        self.ops = {q: [] for q in QUEUES}
        self.count = {s: 0 for s in SEMKEYS}
        self.wait_cache = {q: {} for q in QUEUES}
        self.rr = {}

    def op(self, queue, fn, reads=(), writes=(), deps=(), klass="x",
           silent=False):
        if klass == "x" or silent:
            semkey = f"{queue}.x"
        else:
            n_pool = NDS[queue] if klass == "d" else NCCS
            idx = self.rr.get((queue, klass), 0)
            self.rr[(queue, klass)] = idx + 1
            semkey = f"{queue}.{klass}{idx % n_pool}"
        assert silent or semkey in SEMKEYS, semkey
        d = {}
        if klass in ("d", "cc") and not silent and self.count[semkey] > 0:
            d[semkey] = self.count[semkey]

        def add(tok):
            for s2, c2 in tok.items():
                if (queue == "pe" and s2 == "pe.x"):
                    continue
                if c2 > d.get(s2, 0):
                    d[s2] = c2
        for b in reads:
            add(b.w)
        for b in writes:
            add(b.w)
            add(b.r)
        for tok in deps:
            add(tok)
        wc = self.wait_cache[queue]
        waits = {}
        for s2, c2 in d.items():
            if c2 > wc.get(s2, 0):
                waits[s2] = c2
                wc[s2] = c2
        if silent:
            self.ops[queue].append((waits, fn, None, 0))
            return {}
        inc = KLASS_INC[klass]
        self.count[semkey] += inc
        tokv = self.count[semkey]
        self.ops[queue].append((waits, fn, semkey, inc))
        tok = {semkey: tokv}
        for b in reads:
            b.r[semkey] = max(b.r.get(semkey, 0), tokv)
        for b in writes:
            b.w[semkey] = tokv
        return tok


class SBuf:

    def __init__(self, tensor, lo, n, dtype, init_tok=None):
        self.t = tensor
        self.lo = lo
        self.n = n
        self.dt = dtype
        self.buf = Buf(init_tok)

    def ap(self, a=0, b=None):
        b = self.n if b is None else b
        if self.dt == BF16:
            return self.t[:, self.lo + a:self.lo + b]
        return self.t[:, self.lo + 2 * a:self.lo + 2 * b].bitcast(F32)

    def rows(self, r0, r1, a=0, b=None):
        b = self.n if b is None else b
        if self.dt == BF16:
            return self.t[r0:r1, self.lo + a:self.lo + b]
        return self.t[r0:r1, self.lo + 2 * a:self.lo + 2 * b].bitcast(F32)

    def v3(self, inner):
        return self.ap().rearrange("p (o i) -> p o i", i=inner)


class Arena:
    def __init__(self, tensor, n_units):
        self.t = tensor
        self.n = n_units
        self.pos = 0
        self.marks = []

    def alloc(self, n, dtype=BF16, init_tok=None):
        units = n * (2 if dtype == F32 else 1)
        self.pos = (self.pos + 1) & ~1
        assert self.pos + units <= self.n, ("arena", self.pos, units, self.n)
        lo = self.pos
        self.pos += units
        return SBuf(self.t, lo, n, dtype, init_tok)

    def alias(self, base, n, dtype, off_units=0):
        s = SBuf(self.t, base.lo + off_units, n, dtype)
        s.buf = base.buf
        return s

    def mark(self):
        self.marks.append(self.pos)

    def reset_to_mark(self):
        self.pos = self.marks[-1]


def build_program(n_layers=NL, n_chunks=NCH, do_ffn=True, tap_list=(), defer=True, opev_sep=False, revert=()):
    nc = bass.Bass("TRN2", target_bir_lowering=False)
    S = Sched()

    def din(name, shape, dt=F32):
        return nc.dram_tensor(name, shape, dt, kind="ExternalInput")

    x0_t = din("x0", [D, L])
    wip_t = din("wip", [NL, 16, 128, DT8 * 128])
    wxp_t = din("wxp", [NL, EH, 96])
    wdt_t = din("wdt", [NL, RDT, EH])
    wop_t = din("wop", [NL, DT8, 128, ET8 * 128])
    w1_t = din("w1t", [MF, 128, DT8 * 128])
    w2_t = din("w2t", [MF, 128, DT8 * 128])
    h1_t = din("h1t", [MF, 128, MF * 128])
    h2_t = din("h2t", [MF, 128, MF * 128])
    w3_t = din("w3t", [DT8, 128, MF * 128])
    smalls_t = din("smalls", [128, SM_COLS])
    sel_in_t = din("sel32", [32, 32 * 128])
    tok_in_t = din("tok0", [1, 1], I32)
    out_t = nc.dram_tensor("out_c", [D, L // 2], F32, kind="ExternalOutput")

    x0_in = x0_t.ap()
    wip_in, wxp_in = wip_t.ap(), wxp_t.ap()
    wdt_in, wop_in = wdt_t.ap(), wop_t.ap()
    w1_in, w2_in = w1_t.ap(), w2_t.ap()
    h1_in, h2_in, w3_in = h1_t.ap(), h2_t.ap(), w3_t.ap()
    out_ext = out_t.ap()

    wipb_t = nc.dram_tensor("wipb_d", [NL, 16, 128, DT8 * 128], BF16,
                            kind="Internal")
    wopb_t = nc.dram_tensor("wopb_d", [NL, DT8, 128, ET8 * 128], BF16,
                            kind="Internal")
    w1b_t = nc.dram_tensor("w1b_d", [MF, 128, DT8 * 128], BF16, kind="Internal")
    w2b_t = nc.dram_tensor("w2b_d", [MF, 128, DT8 * 128], BF16, kind="Internal")
    h1b_t = nc.dram_tensor("h1b_d", [MF, 128, MF * 128], BF16, kind="Internal")
    h2b_t = nc.dram_tensor("h2b_d", [MF, 128, MF * 128], BF16, kind="Internal")
    w3b_t = nc.dram_tensor("w3b_d", [DT8, 128, MF * 128], BF16, kind="Internal")
    wipb_in, wopb_in = wipb_t.ap(), wopb_t.ap()
    w1b_in, w2b_in = w1b_t.ap(), w2b_t.ap()
    h1b_in, h2b_in, w3b_in = h1b_t.ap(), h2b_t.ap(), w3b_t.ap()
    wb_bufs = {"wip": [Buf() for _ in range(NL)],
               "wop": [Buf() for _ in range(NL)],
               "w1": Buf(), "w2": Buf(), "h1": Buf(), "h2": Buf(),
               "w3": Buf()}
    resid_t = nc.dram_tensor("resid_d", [D, L], F32, kind="Internal")
    x1_dt = nc.dram_tensor("x1_d", [D, L], F32, kind="Internal")
    xn2_dt = nc.dram_tensor("xn2_d", [D, L], BF16, kind="Internal")
    xsrc_t = nc.dram_tensor("xsrc_d", [NCH * 96, TC], F32, kind="Internal")
    xdst_t = nc.dram_tensor("xdst_d", [NCH * 96, TC], F32, kind="Internal")
    osrc_t = nc.dram_tensor("osrc_d", [NCH * D, TC], F32, kind="Internal")
    odst_t = nc.dram_tensor("odst_d", [NCH * D, TC], F32, kind="Internal")
    resid_d, x1_d, xn2_d = resid_t.ap(), x1_dt.ap(), xn2_dt.ap()
    xsrc_d, xdst_d = xsrc_t.ap(), xdst_t.ap()
    osrc_d, odst_d = osrc_t.ap(), odst_t.ap()

    db = {}
    for t in ("resid", "x1", "xn2", "xsrc", "xdst", "osrc", "odst"):
        db[t] = [Buf() for _ in range(NCH)]
    db["out"] = Buf()

    with ExitStack() as st:
        NA = 106400
        ar_t = st.enter_context(nc.sbuf_tensor("ar", [128, NA], BF16))
        tok_t = st.enter_context(nc.sbuf_tensor("tok_t", [1, 1], I32))
        psum = [st.enter_context(nc.psum_tensor(f"ps{j}", [128, 512], F32))
                for j in range(8)]
        sems = {s: st.enter_context(nc.semaphore("sem_" + s.replace(".", "_")))
                for s in SEMKEYS}
        r_tok = st.enter_context(nc.sync.register("r_tok"))
        r_off = st.enter_context(nc.sync.register("r_off"))
        r_off2 = st.enter_context(nc.sync.register("r_off2"))

        A = Arena(ar_t, NA)
        psb = [Buf() for _ in range(8)]
        tok_b = Buf()
        OP = S.op

        def dma(queue, out_ap, in_ap, reads=(), writes=(), deps=()):
            def fn(eng, o=out_ap, i=in_ap):
                return eng.dma_start(out=o, in_=i)
            return OP(queue, fn, reads=reads, writes=writes, deps=deps,
                      klass="d")

        ones_st = A.alloc(128)
        state = A.alloc(NL * 128)
        tails = A.alloc(NL * ET8 * 3)
        smalls = A.alloc(SM_COLS, F32)
        A_sb = A.alloc(NL * 128, F32)
        wstgA = A.alloc(512, F32)
        wstgB = A.alloc(512, F32)

        def smc(col):
            return smalls.ap(col, col + 1)

        nc.const_aps.register(F32, EPS, smc(SM_OFF["c_eps"]))
        nc.const_aps.register(F32, -0.5, smc(SM_OFF["c_mhalf"]))

        A.mark()

        wipT = [A.alloc(1024), A.alloc(1024)]
        wopT = [A.alloc(1024), A.alloc(1024)]
        w_xp = A.alloc(ET8 * 96)
        w_dt = A.alloc(1024)
        hn_bf = A.alloc(DT8 * TC)
        y_g = A.alloc(ET8 * TC)
        u_bf2 = [A.alloc(ET8 * TC), A.alloc(ET8 * TC)]
        zs_bf2 = [A.alloc(ET8 * TC), A.alloc(ET8 * TC)]
        a_bufs = [A.alloc(TC) for _ in range(4)]
        w_buf = A.alloc(TC)
        b_strip = A.alloc(NST * TC)
        xi_ext = A.alloc(ET8 * (TC + 3))
        xi_bufs = [Buf() for _ in range(ET8)]
        cv = A.alloc(4 * TC)
        stat_bf = A.alias(cv, 2 * TC, BF16)
        h_strip = A.alloc(NST * TC)
        op_ev = (None if opev_sep
                 else A.alias(h_strip, DT8 * TC, F32))
        Bst = A.alloc(NST * TC)
        h_in = A.alias(Bst, DT8 * TC, F32)
        Cst = A.alloc(NST * TC)
        dtin_bf = A.alloc(TC)
        sel = A.alloc(32 * 128)
        bc = (A.alloc(TC, F32) if "bc" in revert else A.alias(wstgA, TC, F32))
        bcb = A.alloc(TC)
        Rc = A.alloc(DT8 * TC, F32)
        if opev_sep:
            op_ev = A.alias(Rc, DT8 * TC, F32)
        dt_et2 = [A.alloc(TC, F32), A.alloc(TC, F32)]
        sp2 = [A.alloc(2 * TC, F32), A.alloc(2 * TC, F32)]
        y_f2 = A.alloc(TC, F32)
        y_f = A.alloc(TC, F32)
        xp_ev = A.alias(y_f, TC, F32)
        m_rep = A.alloc(TC, F32)
        v_tmp = A.alloc(TC, F32)
        rstd = A.alloc(TC, F32)
        r1 = A.alloc(TC, F32)


        def emit_stats_ms(slices, stat_rg, r1_rg, rstd_rg):
            pb, pbb = psum[4], psb[4]
            nt = len(slices)
            for kk, (ap, rb) in enumerate(slices):
                half = kk % 2
                sl = stat_rg.ap(half * TC, (half + 1) * TC)

                def fsq(e, d=sl, s=ap):
                    return e.activation(d, s, AF.Square)
                OP("act", fsq, reads=[rb], writes=[stat_rg.buf])

                def fmm(e, s=sl, first=(kk == 0), last=(kk == nt - 1)):
                    return e.matmul(pb[:, :], ones_st.ap(), s,
                                    start=first, stop=last)
                OP("pe", fmm, reads=[stat_rg.buf, ones_st.buf], writes=[pbb])

            def fln(e):
                return e.activation(r1_rg.ap(), pb[:, :], AF.Ln, bias=EPS)
            OP("act", fln, reads=[pbb, smalls.buf], writes=[r1_rg.buf])

            def fex(e):
                return e.activation(rstd_rg.ap(), r1_rg.ap(), AF.Exp,
                                    scale=-0.5)
            OP("act", fex, reads=[r1_rg.buf, smalls.buf], writes=[rstd_rg.buf])

        def emit_stats_mean_var(slices, stat_rg, m_rg, v_rg, r1_rg, rstd_rg):
            pbm, pbs = psum[4], psum[5]
            pbmb, pbsb = psb[4], psb[5]
            nt = len(slices)
            for kk, (ap, rb) in enumerate(slices):
                half = kk % 2
                sl = stat_rg.ap(half * TC, (half + 1) * TC)

                def fcp(e, d=sl, s=ap):
                    return e.copy(d, s)
                OP("act", fcp, reads=[rb], writes=[stat_rg.buf])

                def fmm(e, s=sl, first=(kk == 0), last=(kk == nt - 1)):
                    return e.matmul(pbm[:, :], ones_st.ap(), s,
                                    start=first, stop=last)
                OP("pe", fmm, reads=[stat_rg.buf, ones_st.buf], writes=[pbmb])
            for kk, (ap, rb) in enumerate(slices):
                half = kk % 2
                sl = stat_rg.ap(half * TC, (half + 1) * TC)

                def fsq(e, d=sl, s=ap):
                    return e.activation(d, s, AF.Square)
                OP("act", fsq, reads=[rb], writes=[stat_rg.buf])

                def fmm(e, s=sl, first=(kk == 0), last=(kk == nt - 1)):
                    return e.matmul(pbs[:, :], ones_st.ap(), s,
                                    start=first, stop=last)
                OP("pe", fmm, reads=[stat_rg.buf, ones_st.buf], writes=[pbsb])
            OP("act", lambda e: e.copy(m_rg.ap(), pbm[:, :]),
               reads=[pbmb], writes=[m_rg.buf])
            OP("dve", lambda e: e.tensor_mul(v_rg.ap(), m_rg.ap(), m_rg.ap()),
               reads=[m_rg.buf], writes=[v_rg.buf])
            OP("dve", lambda e: e.tensor_sub(v_rg.ap(), pbs[:, :], v_rg.ap()),
               reads=[pbsb, v_rg.buf], writes=[v_rg.buf])
            OP("act", lambda e: e.activation(r1_rg.ap(), v_rg.ap(), AF.Ln,
                                             bias=EPS),
               reads=[v_rg.buf, smalls.buf], writes=[r1_rg.buf])
            OP("act", lambda e: e.activation(rstd_rg.ap(), r1_rg.ap(), AF.Exp,
                                             scale=-0.5),
               reads=[r1_rg.buf, smalls.buf], writes=[rstd_rg.buf])

        def cast_load(queue, stage, dst, dst_lo, src_ap, n_cols, rows=128):
            for j0 in range(0, n_cols, 512):
                w = min(512, n_cols - j0)
                stg_s = wstgA if "stage" in revert else                     (wstgA, wstgB)[(j0 // 512) % 2]
                stg = stg_s.rows(0, rows, 0, w)
                dma(queue, stg, src_ap[:, j0:j0 + w], writes=[stg_s.buf])

                def fcv(e, d=dst.rows(0, rows, dst_lo + j0, dst_lo + j0 + w),
                        s=stg):
                    return e.copy(d, s)
                OP("act", fcv, reads=[stg_s.buf], writes=[dst.buf])

        wl_state = {"i": 0, "stages": None}

        def wtile_load(src_tile_bf, n_k, dst, key):
            i = wl_state["i"]
            wl_state["i"] = i + 1
            q = ("sync", "gp")[i % 2]
            dma(q, dst.ap(0, n_k * 128), src_tile_bf,
                deps=[wb_bufs[key].w], writes=[dst.buf])

        prepS = [A.alloc(512, F32), A.alloc(512, F32)]
        prepB = [A.alloc(512), A.alloc(512)]
        prep_i = [0]

        def emit_prep(src_f32, dst_bf, dst_buf, n_cols, q_in="act",
                      q_out="gp"):
            for j0 in range(0, n_cols, 512):
                w = min(512, n_cols - j0)
                k = prep_i[0]
                prep_i[0] += 1
                ps_, pb_ = prepS[k % 2], prepB[k % 2]
                dma(q_in, ps_.ap(0, w), src_f32[:, j0:j0 + w],
                    writes=[ps_.buf])

                def fc(e, d=pb_.ap(0, w), s=ps_.ap(0, w)):
                    return e.copy(d, s)
                OP("act", fc, reads=[ps_.buf], writes=[pb_.buf])
                dma(q_out, dst_bf[:, j0:j0 + w], pb_.ap(0, w),
                    reads=[pb_.buf], writes=[dst_buf if isinstance(dst_buf, Buf)
                                             else dst_buf])

        def layer_w_items(i):
            items = []
            for m in range(16):
                items.append((wip_in[i, m, :, :], wipb_in[i, m, :, :],
                              ("wip", i), DT8 * 128))
            for m in range(DT8):
                items.append((wop_in[i, m, :, :], wopb_in[i, m, :, :],
                              ("wop", i), DT8 * 128))
            return items

        def wbuf_of(key):
            if isinstance(key, tuple):
                return wb_bufs[key[0]][key[1]]
            return wb_bufs[key]

        def emit_layer_weights(i):
            for kk in range(ET8):
                cast_load("sync", None, w_xp, kk * 96,
                          wxp_in[i, kk * 128:(kk + 1) * 128, :], 96)
            cast_load("sync", None, w_dt, 0, wdt_in[i, :, :], 1024, rows=RDT)
            if i == 0:
                for it in layer_w_items(0):
                    emit_prep(it[0], it[1], wbuf_of(it[2]), it[3])
            if i + 1 < n_layers:
                prep_work[0:0] = layer_w_items(i + 1)

        taps = {}

        def tap_dump(name, region):
            ap_out = nc.dram_tensor("tap_" + name, [128, region.n], region.dt,
                                    kind="ExternalOutput").ap()
            tb = Buf()
            dma("sync", ap_out, region.ap(), reads=[region.buf], writes=[tb])
            taps[name] = tb

        def emit_front_p1(i, c):
            t0 = c * TC
            r3 = Rc.v3(TC)
            rsl = resid_d[:, t0:t0 + TC].rearrange("(k p) t -> p k t", p=128)
            if i == 0:
                dma("sync", r3,
                    x0_in[:, t0:t0 + TC].rearrange("(k p) t -> p k t", p=128),
                    writes=[Rc.buf])
                emit_stats_mean_var(
                    [(Rc.ap(kk * TC, (kk + 1) * TC), Rc.buf)
                     for kk in range(DT8)], stat_bf, m_rep, v_tmp, r1, rstd)
                for kk in range(DT8):
                    sl = Rc.ap(kk * TC, (kk + 1) * TC)

                    def f1(e, s=sl):
                        return e.tensor_sub(s, s, m_rep.ap())
                    OP("dve", f1, reads=[Rc.buf, m_rep.buf], writes=[Rc.buf])

                    def f2(e, s=sl):
                        return e.tensor_mul(s, s, rstd.ap())
                    OP("dve", f2, reads=[Rc.buf, rstd.buf], writes=[Rc.buf])
                dma("sync", rsl, r3, reads=[Rc.buf], writes=[db["resid"][c]])
            else:
                hsl = odst_d[c * D:(c + 1) * D, :].rearrange(
                    "(k p) t -> p k t", p=128)
                dma("sync", r3, hsl, reads=[db["odst"][c]], writes=[Rc.buf])

        def emit_front_p1b(i, c):
            emit_stats_ms(
                [(Rc.ap(kk * TC, (kk + 1) * TC), Rc.buf) for kk in range(DT8)],
                stat_bf, r1, rstd)
            for kk in range(DT8):
                def fhn(e, d=hn_bf.ap(kk * TC, (kk + 1) * TC),
                        s=Rc.ap(kk * TC, (kk + 1) * TC),
                        sc=smc(SM_OFF["nw"] + i * DT8 + kk)):
                    return e.scalar_tensor_tensor(d, s, sc, rstd.ap(),
                                                  ALU.mult, ALU.mult)
                OP("dve", fhn, reads=[Rc.buf, smalls.buf, rstd.buf],
                   writes=[hn_bf.buf])

        def emit_front_p2(i, c):
            u_bf = u_bf2[c % 2]
            zs_bf = zs_bf2[c % 2]
            for et in range(ET8):
                base = et * (TC + 3)
                tl = (i * ET8 + et) * 3

                def ftl(e, d=xi_ext.ap(base, base + 3),
                        s=tails.ap(tl, tl + 3)):
                    return e.copy(d, s)
                OP("act", ftl, reads=[tails.buf], writes=[xi_bufs[et]])

            for m in range(16):
                pb, pbb = psum[m % 4], psb[m % 4]
                wt_ = wipT[m % 2]
                dma("sync", wt_.ap(), wipb_in[i, m, :, :],
                    deps=[wb_bufs["wip"][i].w], writes=[wt_.buf])

                def fmm(e, pb=pb, wt_=wt_):
                    last = None
                    for kk in range(DT8):
                        last = e.matmul(
                            pb[:, :],
                            wt_.ap(kk * 128, (kk + 1) * 128),
                            hn_bf.ap(kk * TC, (kk + 1) * TC),
                            start=(kk == 0), stop=(kk == DT8 - 1))
                    return last
                OP("pe", fmm, reads=[wt_.buf, hn_bf.buf], writes=[pbb])
                if m < ET8:
                    def fev(e, d=xi_ext.ap(m * (TC + 3) + 3,
                                           (m + 1) * (TC + 3)), pb=pb):
                        return e.copy(d, pb[:, :])
                    OP("act", fev, reads=[pbb], writes=[xi_bufs[m]])
                else:
                    def fev(e, d=zs_bf.ap((m - ET8) * TC, (m - ET8 + 1) * TC),
                            pb=pb):
                        return e.activation(d, pb[:, :], AF.Silu)
                    OP("act", fev, reads=[pbb], writes=[zs_bf.buf])

            for et in range(ET8):
                base = et * (TC + 3)
                tl = (i * ET8 + et) * 3
                for k in range(KC):
                    sc = smc(SM_OFF["cw"] + i * 32 + et * 4 + k)
                    src = xi_ext.ap(base + k, base + k + TC)
                    if k == 0:
                        def ftap(e, d=cv.ap(0, TC), s=src, sc=sc):
                            return e.tensor_scalar(d, s, sc, None, ALU.mult)
                        OP("dve", ftap, reads=[xi_bufs[et], smalls.buf],
                           writes=[cv.buf])
                    else:
                        def ftap(e, d=cv.ap((k % 2) * TC, (k % 2 + 1) * TC),
                                 s=src, sc=sc,
                                 prev=cv.ap(((k - 1) % 2) * TC,
                                            ((k - 1) % 2 + 1) * TC)):
                            return e.scalar_tensor_tensor(d, s, sc, prev,
                                                          ALU.mult, ALU.add)
                        OP("dve", ftap, reads=[xi_bufs[et], smalls.buf,
                                               cv.buf], writes=[cv.buf])

                def fu(e, d=u_bf.ap(et * TC, (et + 1) * TC),
                       s=cv.ap(TC, 2 * TC),
                       bias=smc(SM_OFF["cb"] + i * ET8 + et)):
                    return e.activation(d, s, AF.Silu, bias=bias)
                OP("act", fu, reads=[cv.buf, smalls.buf], writes=[u_bf.buf])

                def fsv(e, d=tails.ap(tl, tl + 3),
                        s=xi_ext.ap(base + TC, base + TC + 3)):
                    return e.copy(d, s)
                OP("act", fsv, reads=[xi_bufs[et]], writes=[tails.buf])

            pb4, pbb4 = psum[4], psb[4]

            def fxp(e):
                last = None
                for kk in range(ET8):
                    last = e.matmul(pb4[0:96, :],
                                    w_xp.ap(kk * 96, (kk + 1) * 96),
                                    u_bf.ap(kk * TC, (kk + 1) * TC),
                                    start=(kk == 0), stop=(kk == ET8 - 1))
                return last
            OP("pe", fxp, reads=[w_xp.buf, u_bf.buf], writes=[pbb4])
            OP("act", lambda e: e.copy(xp_ev.rows(0, 96), pb4[0:96, :]),
               reads=[pbb4], writes=[xp_ev.buf])
            dma("gp", xsrc_d[c * 96:(c + 1) * 96, :], xp_ev.rows(0, 96),
                reads=[xp_ev.buf], writes=[db["xsrc"][c]])

            def fcc(e, c=c):
                return e.collective_compute(
                    "AllReduce", ALU.add, replica_groups=RG_PAIRS,
                    ins=[xsrc_d[c * 96:(c + 1) * 96, :].opt()],
                    outs=[xdst_d[c * 96:(c + 1) * 96, :].opt()])
            OP("gp", fcc, reads=[db["xsrc"][c]], writes=[db["xdst"][c]],
               klass="cc")

        def emit_dt_chain(i, et):
            dt_et = dt_et2[et % 2]
            sp = sp2[et % 2]
            pb5, pbb5 = psum[5], psb[5]

            def fdt(e, et=et):
                return e.matmul(pb5[:, :],
                                w_dt.rows(0, RDT, et * 128, (et + 1) * 128),
                                dtin_bf.rows(0, RDT),
                                start=True, stop=True)
            OP("pe", fdt, reads=[w_dt.buf, dtin_bf.buf], writes=[pbb5])
            dtb_sc = smc(SM_OFF["dtb"] + i * ET8 + et)

            def f_abs(e, sc=dtb_sc, sp=sp):
                return e.activation(sp.ap(0, TC), pb5[:, :], AF.Abs, bias=sc)
            OP("act", f_abs, reads=[pbb5, smalls.buf], writes=[sp.buf])

            def f_exp(e, sp=sp):
                return e.activation(sp.ap(TC, 2 * TC), sp.ap(0, TC),
                                    AF.Exp, scale=-1.0)
            OP("act", f_exp, reads=[sp.buf], writes=[sp.buf])

            def f_ln(e, sp=sp):
                return e.activation(sp.ap(0, TC), sp.ap(TC, 2 * TC),
                                    AF.Ln, bias=1.0)
            OP("act", f_ln, reads=[sp.buf], writes=[sp.buf])

            def f_rel(e, sc=dtb_sc, sp=sp):
                return e.activation(sp.ap(TC, 2 * TC), pb5[:, :],
                                    AF.Relu, bias=sc)
            OP("act", f_rel, reads=[pbb5, smalls.buf], writes=[sp.buf])

            def f_dt(e, sp=sp, dt_et=dt_et):
                return e.tensor_add(dt_et.ap(), sp.ap(0, TC),
                                    sp.ap(TC, 2 * TC))
            OP("dve", f_dt, reads=[sp.buf], writes=[dt_et.buf])

        def emit_scan(i, c, mid0=None, mid_cb=None, mid6=None):
            u_bf = u_bf2[c % 2]
            zs_bf = zs_bf2[c % 2]
            dma("gp", dtin_bf.rows(0, RDT), xdst_d[c * 96:c * 96 + RDT, :],
                reads=[db["xdst"][c]], writes=[dtin_bf.buf])
            dma("gp", bc.rows(0, 32), xdst_d[c * 96 + RDT:(c + 1) * 96, :],
                reads=[db["xdst"][c]], writes=[bc.buf])
            OP("act", lambda e: e.copy(bcb.rows(0, 32), bc.rows(0, 32)),
               reads=[bc.buf], writes=[bcb.buf])

            for n in range(NST):
                jB = 4 + (n % 2) * 2
                jC = 5 + (n % 2) * 2

                def fb(e, n=n, pb=psum[jB]):
                    return e.matmul(pb[:, :],
                                    sel.rows(0, 32, n * 128, (n + 1) * 128),
                                    bcb.rows(0, 32), start=True, stop=True)
                OP("pe", fb, reads=[sel.buf, bcb.buf], writes=[psb[jB]])

                def fbe(e, d=Bst.ap(n * TC, (n + 1) * TC), pb=psum[jB]):
                    return e.copy(d, pb[:, :])
                OP("act", fbe, reads=[psb[jB]], writes=[Bst.buf])

                def fc2(e, n=n, pb=psum[jC]):
                    return e.matmul(
                        pb[:, :],
                        sel.rows(0, 32, (NST + n) * 128, (NST + n + 1) * 128),
                        bcb.rows(0, 32), start=True, stop=True)
                OP("pe", fc2, reads=[sel.buf, bcb.buf], writes=[psb[jC]])

                def fce(e, d=Cst.ap(n * TC, (n + 1) * TC), pb=psum[jC]):
                    return e.copy(d, pb[:, :])
                OP("act", fce, reads=[psb[jC]], writes=[Cst.buf])

            for et in range(ET8):
                dt_et = dt_et2[et % 2]
                if et == 0:
                    emit_dt_chain(i, 0)
                if et + 1 < ET8:
                    emit_dt_chain(i, et + 1)

                def fw(e, et=et, dt_et=dt_et):
                    return e.tensor_mul(w_buf.ap(), dt_et.ap(),
                                        u_bf.ap(et * TC, (et + 1) * TC))
                OP("dve", fw, reads=[dt_et.buf, u_bf.buf], writes=[w_buf.buf])

                def fb2(e):
                    return e.tensor_tensor(
                        b_strip.v3(TC),
                        w_buf.ap().unsqueeze(1).broadcast_to([128, NST, TC]),
                        Bst.v3(TC), ALU.mult)
                OP("dve", fb2, reads=[w_buf.buf, Bst.buf],
                   writes=[b_strip.buf])
                for n in range(NST):
                    acol = A_sb.ap(i * 128 + et * NST + n,
                                   i * 128 + et * NST + n + 1)
                    ab_s = a_bufs[n % 4]
                    ab = ab_s.ap()

                    def fe(e, ab=ab, acol=acol, dt_et=dt_et):
                        return e.activation(ab, dt_et.ap(), AF.Exp, scale=acol)
                    OP("act", fe, reads=[dt_et.buf, A_sb.buf],
                       writes=[ab_s.buf])
                    stcol = state.ap(i * 128 + et * NST + n,
                                     i * 128 + et * NST + n + 1)

                    def fs(e, n=n, ab=ab, stcol=stcol, ab_s=ab_s):
                        return e.tensor_tensor_scan(
                            h_strip.ap(n * TC, (n + 1) * TC), ab,
                            b_strip.ap(n * TC, (n + 1) * TC), stcol,
                            ALU.mult, ALU.add)
                    OP("dve", fs, reads=[ab_s.buf, b_strip.buf, state.buf],
                       writes=[h_strip.buf])

                def fst(e, et=et):
                    v = h_strip.v3(TC)[:, :, TC - 1:TC]
                    return e.tensor_copy(
                        state.ap(i * 128 + et * NST, i * 128 + (et + 1) * NST),
                        v)
                OP("dve", fst, reads=[h_strip.buf], writes=[state.buf])

                def fq(e):
                    return e.tensor_mul(b_strip.ap(), h_strip.ap(), Cst.ap())
                OP("dve", fq, reads=[h_strip.buf, Cst.buf],
                   writes=[b_strip.buf])
                qv = b_strip.ap().rearrange("p (n two t) -> p n two t",
                                            two=2, t=TC)

                def ftr1(e, qv=qv):
                    return e.tensor_tensor(h_strip.v3(TC)[:, 0:8, :],
                                           qv[:, :, 0, :], qv[:, :, 1, :],
                                           ALU.add)
                OP("dve", ftr1, reads=[b_strip.buf], writes=[h_strip.buf])
                hv = h_strip.ap().rearrange("p (g two t) -> p g two t",
                                            two=2, t=TC)

                def ftr2(e, hv=hv):
                    return e.tensor_tensor(h_strip.v3(TC)[:, 8:12, :],
                                           hv[:, 0:4, 0, :], hv[:, 0:4, 1, :],
                                           ALU.add)
                OP("dve", ftr2, reads=[h_strip.buf], writes=[h_strip.buf])

                def ftr3(e, hv=hv):
                    return e.tensor_tensor(h_strip.v3(TC)[:, 12:14, :],
                                           hv[:, 4:6, 0, :], hv[:, 4:6, 1, :],
                                           ALU.add)
                OP("dve", ftr3, reads=[h_strip.buf], writes=[h_strip.buf])

                def ftr4(e):
                    return e.tensor_tensor(y_f.ap(),
                                           h_strip.ap(12 * TC, 13 * TC),
                                           h_strip.ap(13 * TC, 14 * TC),
                                           ALU.add)
                OP("dve", ftr4, reads=[h_strip.buf], writes=[y_f.buf])

                def fy2(e, et=et, sc=smc(SM_OFF["dp"] + i * ET8 + et)):
                    return e.scalar_tensor_tensor(
                        y_f2.ap(), u_bf.ap(et * TC, (et + 1) * TC), sc,
                        y_f.ap(), ALU.mult, ALU.add)
                OP("dve", fy2, reads=[u_bf.buf, smalls.buf, y_f.buf],
                   writes=[y_f2.buf])

                def fyg(e, et=et):
                    return e.tensor_mul(y_g.ap(et * TC, (et + 1) * TC),
                                        y_f2.ap(),
                                        zs_bf.ap(et * TC, (et + 1) * TC))
                OP("dve", fyg, reads=[y_f2.buf, zs_bf.buf], writes=[y_g.buf])
                if et == 0 and mid0 is not None:
                    mid0()
                if et == 1 and mid_cb is not None:
                    mid_cb()
                if et == 6 and mid6 is not None:
                    mid6()

        def emit_outproj(i, c):
            src_R = (resid_d if i == 0 else odst_d)
            src_tok = (db["resid"][c] if i == 0 else db["odst"][c])
            if i == 0:
                hsl = resid_d[:, c * TC:(c + 1) * TC].rearrange(
                    "(k p) t -> p k t", p=128)
            else:
                hsl = odst_d[c * D:(c + 1) * D, :].rearrange(
                    "(k p) t -> p k t", p=128)
            dma("sync", h_in.v3(TC), hsl, reads=[src_tok], writes=[h_in.buf])
            for m in range(DT8):
                pb, pbb = psum[m % 4], psb[m % 4]
                wt_ = wopT[m % 2]
                dma("sync", wt_.ap(), wopb_in[i, m, :, :],
                    deps=[wb_bufs["wop"][i].w], writes=[wt_.buf])

                def fop(e, pb=pb, wt_=wt_):
                    last = None
                    for kk in range(ET8):
                        last = e.matmul(
                            pb[:, :],
                            wt_.ap(kk * 128, (kk + 1) * 128),
                            y_g.ap(kk * TC, (kk + 1) * TC),
                            start=(kk == 0), stop=(kk == ET8 - 1))
                    return last
                OP("pe", fop, reads=[wt_.buf, y_g.buf], writes=[pbb])

                def foev(e, d=op_ev.ap(m * TC, (m + 1) * TC), pb=pb,
                         s=h_in.ap(m * TC, (m + 1) * TC),
                         sc=smc(SM_OFF["c_half"])):
                    return e.scalar_tensor_tensor(d, s, sc, pb[:, :],
                                                  ALU.mult, ALU.add)
                OP("dve", foev, reads=[pbb, h_in.buf, smalls.buf],
                   writes=[op_ev.buf])
            dma("gp", osrc_d[c * D:(c + 1) * D, :].rearrange(
                "(k p) t -> p k t", p=128), op_ev.v3(TC),
                reads=[op_ev.buf], writes=[db["osrc"][c]])

            def focc(e, c=c):
                return e.collective_compute(
                    "AllReduce", ALU.add, replica_groups=RG_PAIRS,
                    ins=[osrc_d[c * D:(c + 1) * D, :].opt()],
                    outs=[odst_d[c * D:(c + 1) * D, :].opt()])
            OP("gp", focc, reads=[db["osrc"][c]], writes=[db["odst"][c]],
               klass="cc")

        def emit_tail(c):
            t0 = c * TC
            hsl = odst_d[c * D:(c + 1) * D, :].rearrange("(k p) t -> p k t",
                                                         p=128)
            dma("sync", Rc.v3(TC), hsl, reads=[db["odst"][c]],
                writes=[Rc.buf])
            emit_stats_ms(
                [(Rc.ap(kk * TC, (kk + 1) * TC), Rc.buf) for kk in range(DT8)],
                stat_bf, r1, rstd)
            for kk in range(DT8):
                def fat(e, d=h_in.ap(kk * TC, (kk + 1) * TC),
                        s=Rc.ap(kk * TC, (kk + 1) * TC),
                        sc=smc(SM_OFF["nf"] + kk)):
                    return e.scalar_tensor_tensor(d, s, sc, rstd.ap(),
                                                  ALU.mult, ALU.mult)
                OP("dve", fat, reads=[Rc.buf, smalls.buf, rstd.buf],
                   writes=[h_in.buf])
            emit_stats_mean_var(
                [(h_in.ap(kk * TC, (kk + 1) * TC), h_in.buf)
                 for kk in range(DT8)], stat_bf, m_rep, v_tmp, r1, rstd)
            dma("sync", Rc.v3(TC),
                x0_in[:, t0:t0 + TC].rearrange("(k p) t -> p k t", p=128),
                writes=[Rc.buf])
            for kk in range(DT8):
                sl = h_in.ap(kk * TC, (kk + 1) * TC)

                def f1(e, s=sl):
                    return e.tensor_sub(s, s, m_rep.ap())
                OP("dve", f1, reads=[h_in.buf, m_rep.buf], writes=[h_in.buf])

                def f2(e, s=sl):
                    return e.tensor_mul(s, s, rstd.ap())
                OP("dve", f2, reads=[h_in.buf, rstd.buf], writes=[h_in.buf])

                def f3(e, s=sl, x=Rc.ap(kk * TC, (kk + 1) * TC)):
                    return e.tensor_add(s, s, x)
                OP("dve", f3, reads=[h_in.buf, Rc.buf], writes=[h_in.buf])
            dma("sync", x1_d[:, t0:t0 + TC].rearrange("(k p) t -> p k t",
                                                      p=128),
                h_in.v3(TC), reads=[h_in.buf], writes=[db["x1"][c]])
            emit_stats_mean_var(
                [(h_in.ap(kk * TC, (kk + 1) * TC), h_in.buf)
                 for kk in range(DT8)], stat_bf, m_rep, v_tmp, r1, rstd)
            for kk in range(DT8):
                sl = h_in.ap(kk * TC, (kk + 1) * TC)

                def f4(e, s=sl):
                    return e.tensor_sub(s, s, m_rep.ap())
                OP("dve", f4, reads=[h_in.buf, m_rep.buf], writes=[h_in.buf])

                def f5(e, d=hn_bf.ap(kk * TC, (kk + 1) * TC), s=sl):
                    return e.tensor_mul(d, s, rstd.ap())
                OP("dve", f5, reads=[h_in.buf, rstd.buf], writes=[hn_bf.buf])
            dma("sync", xn2_d[:, t0:t0 + TC].rearrange("(k p) t -> p k t",
                                                       p=128),
                hn_bf.v3(TC), reads=[hn_bf.buf], writes=[db["xn2"][c]])

        def emit_ffn_phase():
            snap = {s: S.count[s] for s in SEMKEYS}
            A.reset_to_mark()

            def nb(n, dtype=BF16):
                return A.alloc(n, dtype, init_tok=snap)
            hg = nb(MF * TF)
            h1s = nb(MF * TF)
            xn2t = A.alias(h1s, DT8 * TF, BF16)
            wtA = nb(MF * 128)
            wtB = nb(MF * 128)
            g_tmp = nb(TF)
            b2_tmp = nb(TF)
            stat2 = nb(2 * TC)
            f_out = nb(DT8 * TF, F32)
            x1kk = nb(TF, F32)
            r12 = A.alias(x1kk, TC, F32)
            wsC = nb(1024, F32)
            wsD = nb(1024, F32)
            wl_state["stages"] = (wsC, wsD)
            m2 = nb(TC, F32)
            v2 = A.alias(wsC, TC, F32)
            rs2 = A.alias(stat2, TC, F32)
            xout = A.alias(hg, TF, F32)

            OP("sync", lambda e: e.reg_load(r_tok, tok_t[0:1, 0:1]),
               deps=[tok_b.w], silent=True)

            for tt in range(2048 // TF):
                def f_ro(e, k=tt * TF):
                    return e.reg_add(r_off, r_tok, k)
                OP("sync", f_ro, silent=True)

                def fxl(e):
                    src = bass.AP(xn2_dt, r_off,
                                  [[L, 128], [128 * L, DT8], [1, TF]])
                    return e.dma_start(out=xn2t.v3(TF), in_=src)
                OP("sync", fxl, writes=[xn2t.buf],
                   deps=[cb.w for cb in db["xn2"]], klass="d")

                for m in range(MF):
                    wtile_load(w1b_in[m, :, :], DT8, wtA, "w1")
                    for ntile in range(TF // 512):
                        pb, pbb = psum[ntile], psb[ntile]

                        def fm1(e, ntile=ntile, pb=pb):
                            last = None
                            for kk in range(DT8):
                                last = e.matmul(
                                    pb[:, :], wtA.ap(kk * 128, (kk + 1) * 128),
                                    xn2t.ap(kk * TF + ntile * 512,
                                            kk * TF + ntile * 512 + 512),
                                    start=(kk == 0), stop=(kk == DT8 - 1))
                            return last
                        OP("pe", fm1, reads=[wtA.buf, xn2t.buf], writes=[pbb])

                        def fe1(e, d=g_tmp.ap(ntile * 512, ntile * 512 + 512),
                                pb=pb, sc=smc(SM_OFF["s1"] + m)):
                            return e.activation(d, pb[:, :], AF.Silu, scale=sc)
                        OP("act", fe1, reads=[pbb, smalls.buf],
                           writes=[g_tmp.buf])
                    wtile_load(w2b_in[m, :, :], DT8, wtB, "w2")
                    for ntile in range(TF // 512):
                        pb, pbb = psum[2 + ntile], psb[2 + ntile]

                        def fm2(e, ntile=ntile, pb=pb):
                            last = None
                            for kk in range(DT8):
                                last = e.matmul(
                                    pb[:, :], wtB.ap(kk * 128, (kk + 1) * 128),
                                    xn2t.ap(kk * TF + ntile * 512,
                                            kk * TF + ntile * 512 + 512),
                                    start=(kk == 0), stop=(kk == DT8 - 1))
                            return last
                        OP("pe", fm2, reads=[wtB.buf, xn2t.buf], writes=[pbb])

                        def fe2(e, d=b2_tmp.ap(ntile * 512, ntile * 512 + 512),
                                pb=pb, sc=smc(SM_OFF["s2"] + m)):
                            return e.activation(d, pb[:, :], AF.Identity,
                                                scale=sc)
                        OP("act", fe2, reads=[pbb, smalls.buf],
                           writes=[b2_tmp.buf])

                    def fhg(e, d=hg.ap(m * TF, (m + 1) * TF)):
                        return e.tensor_mul(d, g_tmp.ap(), b2_tmp.ap())
                    OP("dve", fhg, reads=[g_tmp.buf, b2_tmp.buf],
                       writes=[hg.buf])

                for src_w, wkey, dst_s, scoff, act in (
                        (h1b_in, "h1", h1s, "sh1", AF.Gelu),
                        (h2b_in, "h2", hg, "sh2", AF.Gelu),
                        (w3b_in, "w3", f_out, "s3", AF.Identity)):
                    rhs_s = hg if dst_s is h1s else (h1s if dst_s is hg
                                                    else hg)
                    n_m = DT8 if dst_s is f_out else MF
                    for m in range(n_m):
                        wtm = (wtA, wtB)[m % 2]
                        wtile_load(src_w[m, :, :], MF, wtm, wkey)
                        for ntile in range(TF // 512):
                            pb, pbb = psum[ntile], psb[ntile]

                            def fm3(e, ntile=ntile, pb=pb, rhs_s=rhs_s,
                                    wtm=wtm):
                                last = None
                                for kk in range(MF):
                                    last = e.matmul(
                                        pb[:, :],
                                        wtm.ap(kk * 128, (kk + 1) * 128),
                                        rhs_s.ap(kk * TF + ntile * 512,
                                                 kk * TF + ntile * 512 + 512),
                                        start=(kk == 0), stop=(kk == MF - 1))
                                return last
                            OP("pe", fm3, reads=[wtm.buf, rhs_s.buf],
                               writes=[pbb])

                            def fe3(e, d=dst_s.ap(m * TF + ntile * 512,
                                                  m * TF + ntile * 512 + 512),
                                    pb=pb, sc=smc(SM_OFF[scoff] + m), act=act):
                                return e.activation(d, pb[:, :], act, scale=sc)
                            OP("act", fe3, reads=[pbb, smalls.buf],
                               writes=[dst_s.buf])

                for half in range(TF // 512):
                    emit_stats_mean_var(
                        [(f_out.ap(kk * TF + half * 512,
                                   kk * TF + half * 512 + 512), f_out.buf)
                         for kk in range(DT8)], stat2, m2, v2, r12, rs2)
                    for kk in range(DT8):
                        sl = f_out.ap(kk * TF + half * 512,
                                      kk * TF + half * 512 + 512)

                        def g1(e, s=sl):
                            return e.tensor_sub(s, s, m2.ap())
                        OP("dve", g1, reads=[f_out.buf, m2.buf],
                           writes=[f_out.buf])

                        def g2(e, s=sl):
                            return e.tensor_mul(s, s, rs2.ap())
                        OP("dve", g2, reads=[f_out.buf, rs2.buf],
                           writes=[f_out.buf])
                for kk in range(DT8):
                    def f_ro2(e, k=kk * 128 * L):
                        return e.reg_add(r_off2, r_off, k)
                    OP("sync", f_ro2, silent=True)

                    def fx1(e):
                        src = bass.AP(x1_dt, r_off2, [[L, 128], [1, TF]])
                        return e.dma_start(out=x1kk.ap(), in_=src)
                    OP("sync", fx1, writes=[x1kk.buf],
                       deps=[cb.w for cb in db["x1"]], klass="d")

                    def g3(e, d=xout.ap(), s=f_out.ap(kk * TF, (kk + 1) * TF)):
                        return e.tensor_add(d, s, x1kk.ap())
                    OP("dve", g3, reads=[f_out.buf, x1kk.buf],
                       writes=[xout.buf])

                    def fo(e, kk=kk, tt=tt):
                        return e.dma_start(
                            out=out_ext[kk * 128:(kk + 1) * 128,
                                        tt * TF:(tt + 1) * TF],
                            in_=xout.ap())
                    OP("sync", fo, reads=[xout.buf], writes=[db["out"]],
                       klass="d")

        dma("sync", smalls.ap(), smalls_t.ap(), writes=[smalls.buf])
        for j0 in range(0, 32 * 128, 512):
            stg_s = (wstgA, wstgB)[(j0 // 512) % 2]
            stg = stg_s.rows(0, 32, 0, 512)
            dma("sync", stg, sel_in_t.ap()[:, j0:j0 + 512],
                writes=[stg_s.buf])

            def fselc(e, d=sel.rows(0, 32, j0, j0 + 512), s=stg):
                return e.copy(d, s)
            OP("act", fselc, reads=[stg_s.buf], writes=[sel.buf])
        dma("sync", tok_t[:, :], tok_in_t.ap(), writes=[tok_b])
        OP("gp", lambda e: e.memset(ones_st.ap(), 1.0 / 1024.0),
           writes=[ones_st.buf])
        OP("gp", lambda e: e.memset(state.ap(), 0.0), writes=[state.buf])
        OP("gp", lambda e: e.memset(tails.ap(), 0.0), writes=[tails.buf])
        OP("act", lambda e: e.activation(
            A_sb.ap(), smalls.ap(SM_OFF["alog"], SM_OFF["alog"] + NL * 128),
            AF.Exp), reads=[smalls.buf], writes=[A_sb.buf])
        OP("dve", lambda e: e.tensor_scalar(A_sb.ap(), A_sb.ap(), -1.0, None,
                                            ALU.mult),
           reads=[A_sb.buf], writes=[A_sb.buf])

        prep_work = []
        for m in range(MF):
            prep_work.append((w1_in[m, :, :], w1b_in[m, :, :], "w1",
                              DT8 * 128))
            prep_work.append((w2_in[m, :, :], w2b_in[m, :, :], "w2",
                              DT8 * 128))
        for m in range(MF):
            prep_work.append((h1_in[m, :, :], h1b_in[m, :, :], "h1",
                              MF * 128))
            prep_work.append((h2_in[m, :, :], h2b_in[m, :, :], "h2",
                              MF * 128))
        for m in range(DT8):
            prep_work.append((w3_in[m, :, :], w3b_in[m, :, :], "w3",
                              MF * 128))

        def drain_prep(n_pieces):
            budget = n_pieces
            while prep_work and budget > 0:
                sf, dbf, key, ncols = prep_work.pop(0)
                emit_prep(sf, dbf, wbuf_of(key), ncols)
                budget -= (ncols + 511) // 512

        tapmap = {"hn": lambda: hn_bf, "u": lambda: u_bf,
                  "zs": lambda: zs_bf, "yg": lambda: y_g,
                  "Rc": lambda: Rc, "op": lambda: op_ev,
                  "dt": lambda: dt_et, "st": lambda: state}
        def emit_layer_bridge(i):
            emit_layer_weights(i + 1)
            emit_front_p1(i + 1, 0)
            emit_front_p1b(i + 1, 0)
            emit_front_p2(i + 1, 0)

        for i in range(n_layers):
            if i == 0:
                emit_layer_weights(0)
            for c in range(n_chunks):
                if c == 0 and i == 0:
                    emit_front_p1(0, 0)
                    emit_front_p1b(0, 0)
                    emit_front_p2(0, 0)
                if c + 1 < n_chunks:
                    emit_front_p1(i, c + 1)
                if c > 0:
                    emit_outproj(i, c - 1)
                mid0 = ((lambda i=i, c=c: emit_front_p1b(i, c + 1))
                        if c + 1 < n_chunks else None)
                mid = ((lambda i=i, c=c: emit_front_p2(i, c + 1))
                       if c + 1 < n_chunks else None)
                mid6 = ((lambda i=i: emit_layer_bridge(i))
                        if (c + 1 == n_chunks and i + 1 < n_layers) else None)
                emit_scan(i, c, mid0, mid, mid6)
                drain_prep(26)
                for nm, ti, tc_ in tap_list:
                    if ti == i and tc_ == c:
                        tap_dump(f"{nm}_{ti}_{tc_}", tapmap[nm]())
            emit_outproj(i, n_chunks - 1)
        if do_ffn:
            drain_prep(10**9)
            for c in range(n_chunks):
                emit_tail(c)
            emit_ffn_phase()

        final_counts = dict(S.count)
        with nc.Block() as block:
            def make_body(qname):
                ops = S.ops[qname]

                def body(eng):
                    for waits, fn, semkey, inc in ops:
                        for s2, v in sorted(waits.items()):
                            eng.wait_ge(sems[s2], v)
                        inst = fn(eng)
                        if semkey is not None:
                            inst.then_inc(sems[semkey], inc)
                    if qname == "sync":
                        for s2, v in sorted(final_counts.items()):
                            if v > 0:
                                eng.wait_ge(sems[s2], v)
                return body
            block.sync(make_body("sync"))
            block.vector(make_body("dve"))
            block.scalar(make_body("act"))
            block.tensor(make_body("pe"))
            block.gpsimd(make_body("gp"))

    return nc


def make_inmaps(inputs):
    f32 = np.float32
    x = np.asarray(inputs["L0_attn_hidden_states"], f32)
    ipw = np.asarray(inputs["L0_attn_in_proj_w"], f32)
    cw = np.asarray(inputs["L0_attn_conv_w"], f32)
    cb = np.asarray(inputs["L0_attn_conv_b"], f32)
    xpw = np.asarray(inputs["L0_attn_x_proj_w"], f32)
    dtw = np.asarray(inputs["L0_attn_dt_proj_w"], f32)
    dtb = np.asarray(inputs["L0_attn_dt_proj_b"], f32)
    alog = np.asarray(inputs["L0_attn_A_log"], f32)
    dpar = np.asarray(inputs["L0_attn_D_param"], f32)
    opw = np.asarray(inputs["L0_attn_out_proj_w"], f32)
    nw = np.asarray(inputs["L0_attn_norm_w"], f32)
    nf = np.asarray(inputs["L0_attn_norm_f_w"], f32)
    w1 = np.asarray(inputs["L0_ffn_w1_w"], f32)
    s1 = np.asarray(inputs["L0_ffn_w1_s"], f32)
    w2 = np.asarray(inputs["L0_ffn_w2_w"], f32)
    s2 = np.asarray(inputs["L0_ffn_w2_s"], f32)
    h1 = np.asarray(inputs["L0_ffn_h1_w"], f32)
    sh1 = np.asarray(inputs["L0_ffn_h1_s"], f32)
    h2 = np.asarray(inputs["L0_ffn_h2_w"], f32)
    sh2 = np.asarray(inputs["L0_ffn_h2_s"], f32)
    w3 = np.asarray(inputs["L0_ffn_w3_w"], f32)
    s3 = np.asarray(inputs["L0_ffn_w3_s"], f32)

    def tile_lhs(wT):
        K, M = wT.shape
        n_k = K // 128
        out = np.empty((M // 128, 128, K), f32)
        for m in range(M // 128):
            out[m] = (wT[:, m * 128:(m + 1) * 128]
                      .reshape(n_k, 128, 128).transpose(1, 0, 2)
                      .reshape(128, K))
        return out

    w1t = tile_lhs(np.ascontiguousarray(w1.T))
    w2t = tile_lhs(np.ascontiguousarray(w2.T))
    h1t = tile_lhs(np.ascontiguousarray(h1.T))
    h2t = tile_lhs(np.ascontiguousarray(h2.T))
    w3t = tile_lhs(np.ascontiguousarray(w3.T))
    sel = np.zeros((32, 32 * 128), f32)
    for n in range(32):
        sel[n, n * 128:(n + 1) * 128] = 1.0

    in_maps = []
    p = np.arange(128)
    for core in range(8):
        b, k = core // 2, core % 2
        sl = slice(k * EH, (k + 1) * EH)
        sl2 = slice(E + k * EH, E + (k + 1) * EH)
        wip_flat = np.concatenate([ipw[:, sl, :], ipw[:, sl2, :]],
                                  axis=1).transpose(0, 2, 1)
        wop_flat = opw[:, :, sl].transpose(0, 2, 1)
        wip = np.stack([tile_lhs(np.ascontiguousarray(wip_flat[i]))
                        for i in range(NL)])
        wop = np.stack([tile_lhs(np.ascontiguousarray(wop_flat[i]))
                        for i in range(NL)])
        wxp = np.ascontiguousarray(xpw[:, :, sl].transpose(0, 2, 1))
        wdt = np.ascontiguousarray(dtw[:, sl, :].transpose(0, 2, 1))

        sm = np.zeros((128, SM_COLS), f32)
        for i in range(NL):
            for kk in range(DT8):
                sm[:, SM_OFF["nw"] + i * DT8 + kk] = nw[i, kk * 128 + p]
        for kk in range(DT8):
            sm[:, SM_OFF["nf"] + kk] = nf[kk * 128 + p]
        for i in range(NL):
            for et in range(ET8):
                eidx = k * EH + et * 128 + p
                for n in range(NST):
                    sm[:, SM_OFF["alog"] + i * 128 + et * NST + n] = \
                        alog[i, eidx, n]
                sm[:, SM_OFF["dp"] + i * ET8 + et] = dpar[i, eidx]
                sm[:, SM_OFF["cb"] + i * ET8 + et] = cb[i, eidx]
                sm[:, SM_OFF["dtb"] + i * ET8 + et] = dtb[i, eidx]
                for kc in range(KC):
                    sm[:, SM_OFF["cw"] + i * 32 + et * 4 + kc] = \
                        cw[i, eidx, kc]
        for m in range(MF):
            sm[:, SM_OFF["s1"] + m] = s1[m * 128 + p]
            sm[:, SM_OFF["s2"] + m] = s2[m * 128 + p]
            sm[:, SM_OFF["sh1"] + m] = sh1[m * 128 + p]
            sm[:, SM_OFF["sh2"] + m] = sh2[m * 128 + p]
        for m in range(DT8):
            sm[:, SM_OFF["s3"] + m] = s3[m * 128 + p]
        sm[:, SM_OFF["c_eps"]] = EPS
        sm[:, SM_OFF["c_mhalf"]] = -0.5
        sm[:, SM_OFF["c_half"]] = 0.5

        in_maps.append(dict(
            x0=np.ascontiguousarray(x[b].T),
            wip=wip, wxp=wxp, wdt=wdt, wop=wop,
            w1t=w1t, w2t=w2t, h1t=h1t, h2t=h2t, w3t=w3t,
            smalls=sm, sel32=sel,
            tok0=np.array([[k * 2048]], np.int32),
        ))
    return in_maps


def kernel(**inputs):
    in_maps = make_inmaps(inputs)
    nc = build_program()
    res = run_bass_kernel_spmd(nc, in_maps, core_ids=list(range(8)))
    out = np.empty((B, L, D), np.float32)
    for core in range(8):
        b, k = core // 2, core % 2
        r = np.asarray(res.results[core]["out_c"], np.float32)
        out[b, k * 2048:(k + 1) * 2048, :] = r.T
    return out
